# version 22
# baseline (speedup 1.0000x reference)
import numpy as np
from contextlib import ExitStack

import concourse.bass as bass
import concourse.mybir as mybir
from concourse.bass_utils import run_bass_kernel_spmd

F32 = mybir.dt.float32
BF16 = mybir.dt.bfloat16
AF = mybir.ActivationFunctionType
ALU = mybir.AluOpType

P = 128
T = 4096
DM = 1024
DI = 2048
DH = 4096
NST = 16
RNK = 64
NF = DM // P
NJ = DI // P
NH = DH // P
TC = 512
NCH = T // TC
TS = 256
EPS = 1e-5

ENGINES = ("sync", "vector", "scalar", "tensor", "gpsimd")

FB_Y = 0
FB_A = FB_Y + NJ * 512
FB_XF = FB_A + 4 * 512
FB_X1 = FB_XF + 2 * 512
FB_TS = FB_X1 + NF * 512
FB_TS2 = FB_TS
FB_BC1 = FB_TS + 512
FB_FF = FB_BC1 + 2 * 512
FB_SR = FB_FF + NF * 256
FB_END = FB_SR + 2 * 512
FA_URAW = 0
FA_CVA = FA_URAW + 4100
FA_CVB = FA_CVA + 2048
FA_XDB = FA_CVB + 2048
FA_WIN = FA_XDB + 4096
FA_XS = FA_WIN + 2 * 1024
FA_WC = FA_XS + 1024
FA_EV = FA_WC + 1024
FA_SR = FA_EV + 2048
FA_END = FA_SR + 3 * 512
FW = max(FB_END, FA_END)

BB_U = 0
BB_DT = BB_U + NJ * 512
BB_G = BB_DT + NJ * 512
BB_P = BB_G + NH * 256
BB_YM = BB_P + NH * 256
BB_LN1 = BB_YM + NJ * 512
BB_SZ = BB_LN1 + NF * 512
BB_BB = BB_SZ + 2 * 512
BB_CB = BB_BB + 2 * 1024
BB_B = BB_CB + 2 * 1024
BB_H = BB_B + 1024
BB_HC = BB_H + 1024
BB_WP = BB_HC + 1024
BB_END = BB_WP + 3 * 4096
BA_LNM = 0
BA_WINB = BA_LNM + NF * 2048
BA_UBF = BA_WINB + 2 * 1024
BA_SZS = BA_UBF + 4096
BA_WXB = BA_SZS + 2048
BA_DTR = BA_WXB + NJ * 96
BA_WDT = BA_DTR + 4096
BA_DTS = BA_WDT + NJ * 128
BA_WC = BB_WP
BA_URAW = BA_DTS + 4096
BA_EV = BA_URAW + 4100
BA_END = BA_EV + 2048
BW = max(BB_END, BA_END)

assert FW * 4 + BW * 2 <= 216200, (FW * 4, BW * 2)


ENGINE_SEM = {"vector": "v", "scalar": "s", "tensor": "mm", "gpsimd": "gp"}


class Prog:
    def __init__(self, selfwait=False):
        self.streams = {e: [] for e in ENGINES}
        self.counts = {}
        self.semnames = []
        self.selfwait = selfwait

    def sem(self, name):
        if name not in self.counts:
            self.counts[name] = 0
            self.semnames.append(name)

    def mark(self, name):
        self.sem(name)
        return self.counts[name]

    def emit(self, eng, fn, waits=(), incs=()):
        if self.selfwait and eng in ENGINE_SEM:
            es = ENGINE_SEM[eng]
            self.sem(es)
            waits = tuple(waits) + ((es, self.counts[es]),)
        waits = tuple((s, v) for (s, v) in waits if v > 0)
        for s, v in waits:
            self.sem(s)
        for s, d in incs:
            self.sem(s)
            self.counts[s] += d
        self.streams[eng].append((waits, fn, tuple(incs)))

    def dma(self, eng, dst_fn, src_fn, sem, waits=()):
        self.emit(eng, lambda e, d=dst_fn, s=src_fn: e.dma_start(d(), s()),
                  waits=waits, incs=[(sem, 16)])
        return self.mark(sem)


def _ap(t, off, dims):
    return bass.AP(t, off, [list(d) for d in dims])


def build_program(selfwait=False, detect=False, debug=False):
    nc = bass.Bass(target_bir_lowering=False, dynamic_dma_scratch_size=8192,
                   detect_race_conditions=detect)

    xT = nc.dram_tensor("xT", [DM, T], F32, kind="ExternalInput")
    winp = nc.dram_tensor("winp", [32 * P * 1024], F32,
                          kind="ExternalInput")
    wxp = nc.dram_tensor("wxp", [DI, 96], F32, kind="ExternalInput")
    wdt = nc.dram_tensor("wdt", [RNK, DI], F32, kind="ExternalInput")
    wout = nc.dram_tensor("wout", [4 * P * 16 * 256], F32,
                          kind="ExternalInput")
    w1t = nc.dram_tensor("w1t", [16 * P * 8 * 256], F32,
                         kind="ExternalInput")
    w2t = nc.dram_tensor("w2t", [16 * P * 8 * 256], F32,
                         kind="ExternalInput")
    h1t = nc.dram_tensor("h1t", [16 * P * 32 * 256], F32,
                         kind="ExternalInput")
    h2t = nc.dram_tensor("h2t", [16 * P * 32 * 256], F32,
                         kind="ExternalInput")
    w3t = nc.dram_tensor("w3t", [4 * P * 32 * 256], F32,
                         kind="ExternalInput")
    cvec = nc.dram_tensor("cvec", [P, 652], F32, kind="ExternalInput")

    y_out = nc.dram_tensor("y_out", [DM, T], F32, kind="ExternalOutput")

    uraw_scr = nc.dram_tensor("uraw_scr", [DI, T], BF16, kind="Internal")
    u_scr = nc.dram_tensor("u_scr", [DI, T], BF16, kind="Internal")
    dt_scr = nc.dram_tensor("dt_scr", [DI, T], BF16, kind="Internal")
    sz_scr = nc.dram_tensor("sz_scr", [DI, T], BF16, kind="Internal")
    bc_scr = nc.dram_tensor("bc_scr", [32, T], BF16, kind="Internal")
    st_scr = nc.dram_tensor("st_scr", [8, T], F32, kind="Internal")
    winb = nc.dram_tensor("winb", [32 * P * 1024], BF16, kind="Internal")
    wxpb = nc.dram_tensor("wxpb", [DI, 96], BF16, kind="Internal")
    wdtb = nc.dram_tensor("wdtb", [RNK, DI], BF16, kind="Internal")
    op16 = nc.dram_tensor("op16", [4 * DI * 256], BF16, kind="Internal")
    w1b = nc.dram_tensor("w1b", [16 * DM * 256], BF16, kind="Internal")
    w2b = nc.dram_tensor("w2b", [16 * DM * 256], BF16, kind="Internal")
    h1b = nc.dram_tensor("h1b", [16 * DH * 256], BF16, kind="Internal")
    h2b = nc.dram_tensor("h2b", [16 * DH * 256], BF16, kind="Internal")
    w3b = nc.dram_tensor("w3b", [4 * DH * 256], BF16, kind="Internal")
    if debug:
        dbg_y = nc.dram_tensor("dbg_y", [P, NJ * 512], F32,
                               kind="ExternalOutput")
        dbg_ym = nc.dram_tensor("dbg_ym", [P, NJ * 512], BF16,
                                kind="ExternalOutput")
        dbg_x1 = nc.dram_tensor("dbg_x1", [P, NF * 512], F32,
                                kind="ExternalOutput")
        dbg_ln1 = nc.dram_tensor("dbg_ln1", [P, NF * 512], BF16,
                                 kind="ExternalOutput")
        dbg_u = nc.dram_tensor("dbg_u", [P, NJ * 512], BF16,
                               kind="ExternalOutput")
        dbg_dt = nc.dram_tensor("dbg_dt", [P, NJ * 512], BF16,
                                kind="ExternalOutput")
        dbg_bb = nc.dram_tensor("dbg_bb", [P, 2048], BF16,
                                kind="ExternalOutput")
        dbg_bh = nc.dram_tensor("dbg_bh", [P, 2048], F32,
                                kind="ExternalOutput")
        dbg_an = nc.dram_tensor("dbg_an", [P, 256], F32,
                                kind="ExternalOutput")
        dbg_a = nc.dram_tensor("dbg_a", [P, 1024], F32,
                               kind="ExternalOutput")
        dbg_st = nc.dram_tensor("dbg_st", [P, 256], F32,
                                kind="ExternalOutput")

    pr = Prog(selfwait=selfwait)
    for name in ("v", "s", "mm", "gp"):
        pr.sem(name)

    with ExitStack() as ctx:
        c_sb = ctx.enter_context(nc.sbuf_tensor("c_sb", [P, 652], F32))
        aneg = ctx.enter_context(nc.sbuf_tensor("aneg", [P, NJ * NST], F32))
        hs1 = ctx.enter_context(nc.sbuf_tensor("hs1", [P, NH], F32))
        k12 = ctx.enter_context(nc.sbuf_tensor("k12", [P, NH], F32))
        state = ctx.enter_context(nc.sbuf_tensor("state", [P, NJ * NST], F32))
        fpool = ctx.enter_context(nc.sbuf_tensor("fpool", [P, FW], F32))
        bpool = ctx.enter_context(nc.sbuf_tensor("bpool", [P, BW], BF16))
        ps = [ctx.enter_context(nc.psum_tensor(f"ps{i}", [P, 512], F32))
              for i in range(8)]

        bank_guard = [("v", 0)] * 8

        C_CW, C_CB, C_DTB, C_AL = 0, 64, 80, 96
        C_D, C_NW, C_S1, C_S2 = 352, 368, 376, 408
        C_SH1, C_SH2, C_S3, C_ONE = 440, 472, 504, 512
        C_EPS, C_1PE, C_EPS2, C_NHALF, C_NONE, C_RDM = 513, 514, 515, 516, 517, 518
        C_OROW = 520

        def fp(off, w, p0=0, pn=P):
            return fpool[p0:p0 + pn, off:off + w]

        def bp(off, w, p0=0, pn=P):
            return bpool[p0:p0 + pn, off:off + w]

        def cs0(col):
            return c_sb[0:1, col:col + 1]

        def csP(col):
            return c_sb[:, col:col + 1]

        m_cv = pr.dma("sync", lambda: c_sb[:, :], lambda: cvec[:, :], "cvld")
        pr.emit("vector", lambda e: e.memset(state[:, :], 0.0),
                incs=[("v", 1)])
        pr.emit("vector", lambda e: e.memset(bp(BA_URAW, 4), 0.0),
                incs=[("v", 1)])
        pr.emit("scalar",
                lambda e: e.activation(aneg[:, :], c_sb[:, C_AL:C_AL + 256],
                                       AF.Exp),
                waits=[("cvld", m_cv)], incs=[("s", 1)])
        pr.emit("vector",
                lambda e: e.tensor_scalar(aneg[:, :], aneg[:, :], -1.0, None,
                                          ALU.mult),
                waits=[("s", pr.mark("s"))], incs=[("v", 1)])
        pr.emit("vector",
                lambda e: e.tensor_scalar(hs1[:, :], c_sb[:, C_S1:C_S1 + NH],
                                          0.5, None, ALU.mult),
                waits=[("cvld", m_cv)], incs=[("v", 1)])
        pr.emit("vector",
                lambda e: e.scalar_tensor_tensor(
                    k12[:, :], c_sb[:, C_S1:C_S1 + NH], 0.5,
                    c_sb[:, C_S2:C_S2 + NH], ALU.mult, ALU.mult),
                waits=[("cvld", m_cv)], incs=[("v", 1)])

        sqs = fp(FA_WC, 512)
        xg = [("mm", 0), ("mm", 0)]
        for piece in range(8):
            t0 = piece * 512
            for fi in range(NF):
                sl = fi % 2
                buf = fp(FA_XS + sl * 512, 512)
                mload = pr.dma(
                    "sync", lambda b=buf: b,
                    lambda fi=fi, t0=t0: xT[fi * P:(fi + 1) * P, t0:t0 + 512],
                    "xa%d" % sl, waits=[xg[sl]])
                pr.emit("scalar",
                        lambda e, b=buf: e.activation(sqs, b, AF.Square),
                        waits=[("xa%d" % sl, mload), ("mm", pr.mark("mm"))],
                        incs=[("s", 1)])
                msq = pr.mark("s")
                w0 = [("xa%d" % sl, mload)] + (
                    [bank_guard[0], ("cvld", m_cv)] if fi == 0 else [])
                pr.emit("tensor",
                        lambda e, b=buf, fi=fi: e.matmul(
                            ps[0][0:1, :], c_sb[:, C_ONE:C_ONE + 1], b,
                            start=(fi == 0), stop=(fi == NF - 1)),
                        waits=w0, incs=[("mm", 1)])
                w1 = [("s", msq)] + (
                    [bank_guard[1], ("cvld", m_cv)] if fi == 0 else [])
                pr.emit("tensor",
                        lambda e, fi=fi: e.matmul(
                            ps[1][0:1, :], c_sb[:, C_ONE:C_ONE + 1], sqs,
                            start=(fi == 0), stop=(fi == NF - 1)),
                        waits=w1, incs=[("mm", 1)])
                xg[sl] = ("mm", pr.mark("mm"))
            m_stats = pr.mark("mm")
            mrow = fp(FA_SR, 512, 0, 1)
            vrow = fp(FA_SR + 512, 512, 0, 1)
            rrow = fp(FA_SR + 1024, 512, 0, 1)
            pr.emit("scalar",
                    lambda e: e.activation(mrow, ps[0][0:1, :], AF.Copy,
                                           scale=cs0(C_RDM)),
                    waits=[("mm", m_stats), ("spst", pr.mark("spst"))],
                    incs=[("s", 1)])
            pr.emit("scalar",
                    lambda e: e.activation(vrow, ps[1][0:1, :], AF.Copy,
                                           scale=cs0(C_RDM)),
                    incs=[("s", 1)])
            bank_guard[0] = ("s", pr.mark("s"))
            bank_guard[1] = ("s", pr.mark("s"))
            m_mv = pr.mark("s")
            pr.emit("scalar", lambda e: e.activation(rrow, mrow, AF.Square),
                    waits=[("spst", pr.mark("spst"))], incs=[("s", 1)])
            pr.emit("vector",
                    lambda e: e.tensor_tensor(vrow, vrow, rrow, ALU.subtract),
                    waits=[("s", pr.mark("s"))], incs=[("v", 1)])
            m_var = pr.mark("v")
            pr.dma("sync", lambda t0=t0: st_scr[0:1, t0:t0 + 512],
                   lambda: mrow, "spst", waits=[("s", m_mv)])
            pr.emit("scalar",
                    lambda e: e.activation(rrow, vrow, AF.Ln,
                                           bias=cs0(C_EPS)),
                    waits=[("v", m_var)], incs=[("s", 1)])
            pr.emit("scalar",
                    lambda e: e.activation(rrow, rrow, AF.Exp,
                                           scale=cs0(C_NHALF)),
                    incs=[("s", 1)])
            pr.dma("sync", lambda t0=t0: st_scr[1:2, t0:t0 + 512],
                   lambda: rrow, "spst", waits=[("s", pr.mark("s"))])
            m_r1w = pr.mark("spst")
            pr.emit("scalar",
                    lambda e: e.activation(rrow, vrow, AF.Ln,
                                           scale=cs0(C_1PE),
                                           bias=cs0(C_EPS2)),
                    waits=[("spst", m_r1w)], incs=[("s", 1)])
            pr.emit("scalar",
                    lambda e: e.activation(rrow, rrow, AF.Exp,
                                           scale=cs0(C_NHALF)),
                    incs=[("s", 1)])
            pr.dma("sync", lambda t0=t0: st_scr[2:3, t0:t0 + 512],
                   lambda: rrow, "spst", waits=[("s", pr.mark("s"))])
        m_stats_done = pr.mark("spst")

        def a3_ap(t, blk, nkt):
            dims = [[nkt * 256, P]]
            nch = (nkt * 256) // 2048
            if nch > 1:
                dims.append([2048, nch])
            dims.append([1, min(2048, nkt * 256)])
            return _ap(t, blk * P * nkt * 256, dims)

        a3_items = [(lambda: wdtb[:, :], lambda: wdt[:, :]),
                    (lambda: _ap(wxpb, 0, [[2048, 96], [1, 2048]]),
                     lambda: _ap(wxp, 0, [[2048, 96], [1, 2048]]))]
        for c in range(4):
            a3_items.append((
                lambda c=c: _ap(winb, c * P * 8192,
                                [[8192, P], [2048, 4], [1, 2048]]),
                lambda c=c: _ap(winp, c * P * 8192,
                                [[8192, P], [2048, 4], [1, 2048]])))
        for (srct, blob, nblk, K) in ((wout, op16, 4, DI), (w1t, w1b, 16, DM),
                                      (w2t, w2b, 16, DM), (h1t, h1b, 16, DH),
                                      (h2t, h2b, 16, DH), (w3t, w3b, 4, DH)):
            nkt = K // P
            for blk in range(nblk):
                a3_items.append((
                    lambda blob=blob, blk=blk, nkt=nkt: a3_ap(blob, blk, nkt),
                    lambda srct=srct, blk=blk, nkt=nkt: a3_ap(srct, blk,
                                                              nkt)))
        a3_prev = [0, 0]
        for i, (dfn, sfn) in enumerate(a3_items):
            r = i % 2
            w = [("a3_%d" % r, a3_prev[r])] if a3_prev[r] else []
            a3_prev[r] = pr.dma("gpsimd", dfn, sfn, "a3_%d" % r, waits=w)
        m_a3_join = (("a3_0", a3_prev[0]), ("a3_1", a3_prev[1]))

        pr.emit("sync", lambda e: None, waits=list(m_a3_join))
        ev_guard = ("spev", 0)
        szs_guard = ("spsz", 0)
        for th in range(2):
            t0h = th * 2048
            mb = fp(FA_CVA, 2048)
            rb = fp(FA_CVB, 2048)
            pr.dma("sync", lambda: mb,
                   lambda t0h=t0h: _ap(st_scr, t0h, [[0, P], [1, 2048]]),
                   "bca", waits=[("spst", m_stats_done),
                                 ("v", pr.mark("v")), ("s", pr.mark("s"))])
            m_bc = pr.dma("sync", lambda: rb,
                          lambda t0h=t0h: _ap(st_scr, 2 * T + t0h,
                                              [[0, P], [1, 2048]]),
                          "bca")
            for fi in range(NF):
                for pc in range(4):
                    sl = (fi * 4 + pc) % 2
                    buf = fp(FA_XS + sl * 512, 512)
                    mload = pr.dma(
                        "sync", lambda b=buf: b,
                        lambda fi=fi, t0h=t0h, pc=pc: xT[
                            fi * P:(fi + 1) * P,
                            t0h + pc * 512:t0h + pc * 512 + 512],
                        "xa%d" % sl, waits=[xg[sl]])
                    pr.emit("vector",
                            lambda e, b=buf, pc=pc: e.tensor_tensor(
                                fp(FA_URAW + 4, 512), b,
                                fp(FA_CVA + pc * 512, 512), ALU.subtract),
                            waits=[("xa%d" % sl, mload), ("bca", m_bc)],
                            incs=[("v", 1)])
                    pr.emit("vector",
                            lambda e, fi=fi, pc=pc: e.scalar_tensor_tensor(
                                bp(BA_LNM + fi * 2048 + pc * 512, 512),
                                fp(FA_URAW + 4, 512),
                                c_sb[:, C_NW + fi:C_NW + fi + 1],
                                fp(FA_CVB + pc * 512, 512),
                                ALU.mult, ALU.mult),
                            incs=[("v", 1)])
                    xg[sl] = ("v", pr.mark("v"))
            m_lnm = pr.mark("v")
            wb_guard = [("mm", 0), ("mm", 0)]
            for dtile in range(32):
                ws = dtile % 2
                mw = pr.dma(
                    "sync", lambda ws=ws: bp(BA_WINB + ws * 1024, 1024),
                    lambda d=dtile: _ap(winb, d * P * 1024,
                                        [[1024, P], [1, 1024]]),
                    "wia%d" % ws, waits=[wb_guard[ws]])
                for fi in range(NF):
                    for tsb in range(4):
                        w = []
                        if fi == 0:
                            w = [("wia%d" % ws, mw), ("v", m_lnm),
                                 bank_guard[tsb]]
                        pr.emit(
                            "tensor",
                            lambda e, fi=fi, tsb=tsb, ws=ws: e.matmul(
                                ps[tsb][:, :],
                                bp(BA_WINB + ws * 1024 + fi * P, P),
                                bp(BA_LNM + fi * 2048 + tsb * 512, 512),
                                start=(fi == 0), stop=(fi == NF - 1)),
                            waits=w, incs=[("mm", 1)])
                m_grp = pr.mark("mm")
                wb_guard[ws] = ("mm", m_grp)
                if dtile < NJ:
                    for tsb in range(4):
                        pr.emit("scalar",
                                lambda e, tsb=tsb: e.activation(
                                    bp(BA_EV + tsb * 512, 512),
                                    ps[tsb][:, :], AF.Copy),
                                waits=[("mm", m_grp), ev_guard]
                                if tsb == 0 else [("mm", m_grp)],
                                incs=[("s", 1)])
                        bank_guard[tsb] = ("s", pr.mark("s"))
                    m_ev = pr.mark("s")
                    pr.dma("sync",
                           lambda d=dtile, t0h=t0h: uraw_scr[
                               d * P:(d + 1) * P, t0h:t0h + 2048],
                           lambda: bp(BA_EV, 2048), "spev",
                           waits=[("s", m_ev)])
                    ev_guard = ("spev", pr.mark("spev"))
                else:
                    for tsb in range(4):
                        pr.emit("scalar",
                                lambda e, tsb=tsb: e.activation(
                                    bp(BA_SZS + tsb * 512, 512),
                                    ps[tsb][:, :], AF.Silu),
                                waits=[("mm", m_grp), szs_guard]
                                if tsb == 0 else [("mm", m_grp)],
                                incs=[("s", 1)])
                        bank_guard[tsb] = ("s", pr.mark("s"))
                    m_ev = pr.mark("s")
                    j = dtile - NJ
                    pr.dma("sync",
                           lambda j=j, t0h=t0h: sz_scr[
                               j * P:(j + 1) * P, t0h:t0h + 2048],
                           lambda: bp(BA_SZS, 2048), "spsz",
                           waits=[("s", m_ev)])
                    szs_guard = ("spsz", pr.mark("spsz"))
        m_a1a = pr.mark("spev")

        for j in range(NJ):
            pr.dma("sync", lambda j=j: bp(BA_WXB + j * 96, 96),
                   lambda j=j: wxpb[j * P:(j + 1) * P, :], "wlb")
        pr.dma("sync", lambda: bp(BA_WDT, NJ * P, 0, 64),
               lambda: wdtb[:, :], "wlb")

        uraw_guard = ("v", 0)
        for j in range(NJ):
            mload = pr.dma("sync", lambda: bp(BA_URAW + 4, T),
                           lambda j=j: uraw_scr[j * P:(j + 1) * P, :],
                           "url", waits=[("spev", m_a1a), uraw_guard])
            m_pc = []
            for pc2 in range(2):
                o = pc2 * 2048
                pr.emit("vector",
                        lambda e, j=j, o=o: e.tensor_scalar(
                            fp(FA_CVA, 2048), bp(BA_URAW + 1 + o, 2048),
                            c_sb[:, C_CW + j * 4:C_CW + j * 4 + 1], None,
                            ALU.mult),
                        waits=[("url", mload), ("s", pr.mark("s"))],
                        incs=[("v", 1)])
                pr.emit("vector",
                        lambda e, j=j, o=o: e.scalar_tensor_tensor(
                            fp(FA_CVB, 2048), bp(BA_URAW + 2 + o, 2048),
                            c_sb[:, C_CW + j * 4 + 1:C_CW + j * 4 + 2],
                            fp(FA_CVA, 2048), ALU.mult, ALU.add),
                        incs=[("v", 1)])
                pr.emit("vector",
                        lambda e, j=j, o=o: e.scalar_tensor_tensor(
                            fp(FA_CVA, 2048), bp(BA_URAW + 3 + o, 2048),
                            c_sb[:, C_CW + j * 4 + 2:C_CW + j * 4 + 3],
                            fp(FA_CVB, 2048), ALU.mult, ALU.add),
                        incs=[("v", 1)])
                pr.emit("vector",
                        lambda e, j=j, o=o: e.scalar_tensor_tensor(
                            fp(FA_CVB, 2048), bp(BA_URAW + 4 + o, 2048),
                            c_sb[:, C_CW + j * 4 + 3:C_CW + j * 4 + 4],
                            fp(FA_CVA, 2048), ALU.mult, ALU.add),
                        incs=[("v", 1)])
                m_conv = pr.mark("v")
                pr.emit("scalar",
                        lambda e, j=j, o=o: e.activation(
                            bp(BA_UBF + o, 2048), fp(FA_CVB, 2048), AF.Silu,
                            bias=c_sb[:, C_CB + j:C_CB + j + 1]),
                        waits=[("v", m_conv), ("mm", pr.mark("mm")),
                               ("spu", pr.mark("spu"))],
                        incs=[("s", 1)])
                m_pc.append(pr.mark("s"))
            uraw_guard = ("s", m_pc[-1])
            m_u = m_pc[-1]
            pr.dma("sync", lambda j=j: u_scr[j * P:(j + 1) * P, :],
                   lambda: bp(BA_UBF, T), "spu", waits=[("s", m_u)])
            for tsn in range(8):
                bank = 4 + (tsn % 2)
                w = [("s", m_u), bank_guard[bank]]
                pr.emit("tensor",
                        lambda e, j=j, tsn=tsn, b=bank: e.matmul(
                            ps[b][0:96, :], bp(BA_WXB + j * 96, 96),
                            bp(BA_UBF + tsn * 512, 512),
                            start=True, stop=True),
                        waits=w, incs=[("mm", 1)])
                m_mmx = pr.mark("mm")
                if j == 0:
                    pr.emit("vector",
                            lambda e, tsn=tsn, b=bank: e.tensor_copy(
                                fp(FA_XDB + tsn * 512, 512, 0, 96),
                                ps[b][0:96, :]),
                            waits=[("mm", m_mmx)], incs=[("v", 1)])
                else:
                    pr.emit("vector",
                            lambda e, tsn=tsn, b=bank: e.tensor_tensor(
                                fp(FA_XDB + tsn * 512, 512, 0, 96),
                                fp(FA_XDB + tsn * 512, 512, 0, 96),
                                ps[b][0:96, :], ALU.add),
                            waits=[("mm", m_mmx)], incs=[("v", 1)])
                bank_guard[bank] = ("v", pr.mark("v"))
        m_a1b = pr.mark("v")

        pr.emit("vector",
                lambda e: e.tensor_copy(bp(BA_DTS, T, 0, 32),
                                        fp(FA_XDB, T, 64, 32)),
                waits=[("v", m_a1b)], incs=[("v", 1)])
        m_bccast = pr.mark("v")
        pr.dma("sync", lambda: bc_scr[:, :],
               lambda: bp(BA_DTS, T, 0, 32), "spbc",
               waits=[("v", m_bccast)])
        m_bcw = pr.mark("spbc")
        pr.emit("vector",
                lambda e: e.tensor_copy(bp(BA_DTR, T, 0, 64),
                                        fp(FA_XDB, T, 0, 64)),
                waits=[("v", m_a1b)], incs=[("v", 1)])
        m_dtr = pr.mark("v")
        dts_guard = ("spbc", m_bcw)
        for j in range(NJ):
            for tsn in range(8):
                bank = tsn % 2
                pr.emit("tensor",
                        lambda e, j=j, tsn=tsn, b=bank: e.matmul(
                            ps[b][:, :], bp(BA_WDT + j * P, P, 0, 64),
                            bp(BA_DTR + tsn * 512, 512, 0, 64),
                            start=True, stop=True),
                        waits=[("v", m_dtr), bank_guard[bank]],
                        incs=[("mm", 1)])
                m_mmd = pr.mark("mm")
                col = tsn * 512
                pr.emit("scalar",
                        lambda e, b=bank, j=j: e.activation(
                            fp(FA_CVA, 512), ps[b][:, :], AF.Abs,
                            bias=c_sb[:, C_DTB + j:C_DTB + j + 1]),
                        waits=[("mm", m_mmd), ("v", pr.mark("v"))],
                        incs=[("s", 1)])
                pr.emit("scalar",
                        lambda e: e.activation(fp(FA_CVA, 512),
                                               fp(FA_CVA, 512), AF.Exp,
                                               scale=csP(C_NONE)),
                        incs=[("s", 1)])
                pr.emit("scalar",
                        lambda e: e.activation(fp(FA_CVA, 512),
                                               fp(FA_CVA, 512), AF.Ln,
                                               bias=1.0),
                        incs=[("s", 1)])
                pr.emit("scalar",
                        lambda e, b=bank, j=j: e.activation(
                            fp(FA_CVB, 512), ps[b][:, :], AF.Relu,
                            bias=c_sb[:, C_DTB + j:C_DTB + j + 1]),
                        incs=[("s", 1)])
                bank_guard[bank] = ("s", pr.mark("s"))
                m_sp4 = pr.mark("s")
                pr.emit("vector",
                        lambda e, col=col: e.tensor_tensor(
                            bp(BA_DTS + col, 512), fp(FA_CVA, 512),
                            fp(FA_CVB, 512), ALU.add),
                        waits=[("s", m_sp4), dts_guard], incs=[("v", 1)])
            m_dtj = pr.mark("v")
            pr.dma("sync", lambda j=j: dt_scr[j * P:(j + 1) * P, :],
                   lambda: bp(BA_DTS, T), "spdt", waits=[("v", m_dtj)])
            dts_guard = ("spdt", pr.mark("spdt"))

        m_spu = pr.mark("spu")
        m_spdt = pr.mark("spdt")
        m_spsz = pr.mark("spsz")

        u_marks = {}
        dt_marks = {}
        sz_marks = {}
        u_guard = [("v", 0)] * NJ
        dt_guard = [("s", 0)] * NJ
        sz_guard = [("v", 0)] * 2
        bb_guard = [("spbc", m_bcw)] * 2
        cb_guard = [("spbc", m_bcw)] * 2
        a_guard = [("v", 0)] * 4
        hc_guard = [("gp", 0)]
        ff_guard = [("o%d" % fi, 0) for fi in range(NF)]
        wp_guard = [("mm", 0)] * 3
        m_outproj = [0] * (NCH + 1)
        m_lnx1_dbg = [0] * (NCH + 1)
        m_lnx1_s = [0]
        m_y0 = [0]
        m_w = [0]
        unit_rr = [0]
        dbg_gate = [None, None]
        m_ymix = [0] * (NCH + 1)
        m_scan_end_v = [0] * (NCH + 1)
        m_scan_end_s = [0] * (NCH + 1)

        def u_off(j):
            return BB_U + j * 512

        def dt_off(j):
            return BB_DT + j * 512

        def emit_loads(g):
            gw = [dbg_gate[1]] if dbg_gate[1] else []
            dbg_gate[1] = None
            for j in range(NJ):
                u_marks[(g, j)] = pr.dma(
                    "sync", lambda j=j: bp(u_off(j), 512),
                    lambda g=g, j=j: u_scr[j * P:(j + 1) * P,
                                           g * TC:(g + 1) * TC],
                    "u%d" % j, waits=[u_guard[j]] + gw)
                dt_marks[(g, j)] = pr.dma(
                    "sync", lambda j=j: bp(dt_off(j), 512),
                    lambda g=g, j=j: dt_scr[j * P:(j + 1) * P,
                                            g * TC:(g + 1) * TC],
                    "d%d" % j, waits=[dt_guard[j]])
            for j in range(2):
                sz_marks[(g, j)] = pr.dma(
                    "sync", lambda j=j: bp(BB_SZ + (j % 2) * 512, 512),
                    lambda g=g, j=j: sz_scr[j * P:(j + 1) * P,
                                            g * TC:(g + 1) * TC],
                    "z%d" % (j % 2), waits=[sz_guard[j % 2]])

        def emit_scan_head(g):
            prev_ymix = m_ymix[g - 1] if g > 0 else 0
            for j in range(NJ):
                pr.emit("scalar",
                        lambda e, j=j: e.activation(
                            fp(FB_Y + j * 512, 512), bp(u_off(j), 512),
                            AF.Copy, scale=c_sb[:, C_D + j:C_D + j + 1]),
                        waits=[("u%d" % j, u_marks[(g, j)]),
                               ("v", prev_ymix)],
                        incs=[("s", 1)])
            m_y0[0] = pr.mark("s")
            for j in range(NJ):
                pr.emit("vector",
                        lambda e, j=j: e.tensor_tensor(
                            bp(u_off(j), 512), bp(u_off(j), 512),
                            bp(dt_off(j), 512), ALU.mult),
                        waits=[("u%d" % j, u_marks[(g, j)]),
                               ("d%d" % j, dt_marks[(g, j)]),
                               ("s", m_y0[0])],
                        incs=[("v", 1)])
            m_w[0] = pr.mark("v")
            if debug and g == 0:
                pr.dma("sync", lambda: dbg_st[:, :], lambda: state[:, :],
                       "dbgs", waits=[("v", m_w[0])])
                pr.dma("sync", lambda: dbg_an[:, :], lambda: aneg[:, :],
                       "dbgs")

        def emit_scan_q(g, q):
            qs = q % 2
            bbo = BB_BB + qs * 1024
            cbo = BB_CB + qs * 1024
            mq_b = pr.dma(
                "scalar", lambda o=bbo: bp(o, 1024),
                lambda g=g, q=q: _ap(
                    bc_scr, (q * 2) * T + g * TC,
                    [[0, P], [T, 2], [1, TC]]),
                "bq%d" % qs, waits=[bb_guard[qs]])
            mq_c = pr.dma(
                "scalar", lambda o=cbo: bp(o, 1024),
                lambda g=g, q=q: _ap(
                    bc_scr, (16 + q * 2) * T + g * TC,
                    [[0, P], [T, 2], [1, TC]]),
                "cq%d" % qs, waits=[cb_guard[qs]])
            for j in range(NJ):
                pr.emit("vector",
                        lambda e, j=j, bbo=bbo: e.tensor_tensor(
                            bp(BB_B, 1024),
                            _ap(bpool, u_off(j),
                                [[BW, P], [0, 2], [1, TC]]),
                            bp(bbo, 1024), ALU.mult),
                        waits=[("bq%d" % qs, mq_b), ("v", m_w[0])],
                        incs=[("v", 1)])
                for nn in range(2):
                    n = q * 2 + nn
                    asl = (j * 2 + nn) % 4
                    pr.emit("scalar",
                            lambda e, g=g, j=j, n=n, asl=asl:
                            e.activation(
                                fp(FB_A + asl * 512, 512),
                                bp(dt_off(j), 512),
                                AF.Exp,
                                scale=aneg[:, j * NST + n:
                                           j * NST + n + 1]),
                            waits=[("d%d" % j, dt_marks[(g, j)]),
                                   a_guard[asl]],
                            incs=[("s", 1)])
                    m_a = pr.mark("s")
                    pr.emit("vector",
                            lambda e, j=j, n=n, nn=nn, asl=asl:
                            e.tensor_tensor_scan(
                                bp(BB_H + nn * 512, 512),
                                fp(FB_A + asl * 512, 512),
                                bp(BB_B + nn * 512, 512),
                                state[:, j * NST + n:
                                      j * NST + n + 1],
                                ALU.mult, ALU.add),
                            waits=[("s", m_a)],
                            incs=[("v", 1)])
                    a_guard[asl] = ("v", pr.mark("v"))
                pr.emit("vector",
                        lambda e, cbo=cbo: e.tensor_tensor(
                            bp(BB_HC, 1024), bp(BB_H, 1024),
                            bp(cbo, 1024), ALU.mult),
                        waits=[("cq%d" % qs, mq_c), hc_guard[0]],
                        incs=[("v", 1)])
                m_hc = pr.mark("v")
                for nn in range(2):
                    n = q * 2 + nn
                    pr.emit("vector",
                            lambda e, j=j, n=n, nn=nn: e.tensor_copy(
                                state[:, j * NST + n:j * NST + n + 1],
                                bp(BB_H + nn * 512 + 511, 1)),
                            incs=[("v", 1)])
                if debug and g == 0 and q == 0 and j == 0:
                    pr.dma("sync", lambda: dbg_bb[:, 0:1024],
                           lambda: bp(BB_BB, 1024), "dbgs",
                           waits=[("v", m_hc)])
                    pr.dma("sync", lambda: dbg_bb[:, 1024:2048],
                           lambda: bp(BB_CB, 1024), "dbgs")
                    dbg_gate[0] = ("dbgs", pr.mark("dbgs"))
                for nn in range(2):
                    pr.emit("vector",
                            lambda e, j=j, nn=nn: e.tensor_tensor(
                                fp(FB_Y + j * 512, 512),
                                fp(FB_Y + j * 512, 512),
                                bp(BB_HC + nn * 512, 512),
                                ALU.add),
                            waits=[("s", m_y0[0])],
                            incs=[("v", 1)])
                hc_guard[0] = ("v", pr.mark("v"))
            bb_guard[qs] = ("v", pr.mark("v"))
            cb_guard[qs] = ("v", pr.mark("v"))

        def emit_scan_tail(g):
            m_adds = pr.mark("gp")
            for j in range(NJ):
                w = [("gp", m_adds), ("z%d" % (j % 2), sz_marks[(g, j)]),
                     ("mm", m_outproj[g - 1] if g > 0 else 0)]
                pr.emit("vector",
                        lambda e, g=g, j=j: e.tensor_tensor(
                            bp(BB_YM + j * 512, 512),
                            fp(FB_Y + j * 512, 512),
                            bp(BB_SZ + (j % 2) * 512, 512), ALU.mult),
                        waits=w, incs=[("v", 1)])
                sz_guard[j % 2] = ("v", pr.mark("v"))
                if j + 2 < NJ:
                    sz_marks[(g, j + 2)] = pr.dma(
                        "sync",
                        lambda j=j: bp(BB_SZ + (j % 2) * 512, 512),
                        lambda g=g, j=j: sz_scr[(j + 2) * P:(j + 3) * P,
                                                g * TC:(g + 1) * TC],
                        "z%d" % (j % 2), waits=[sz_guard[j % 2]])
            m_ymix[g] = pr.mark("v")
            for j in range(NJ):
                u_guard[j] = ("v", m_ymix[g])
                dt_guard[j] = ("s", pr.mark("s"))

        def wp_load(unit, blob, off_elems, kcnt, nkt, extra_waits=()):
            m = pr.dma(
                "sync", lambda u=unit, kcnt=kcnt: bp(BB_WP + u * 4096,
                                                     kcnt * 256),
                lambda blob=blob, o=off_elems, kcnt=kcnt, nkt=nkt: _ap(
                    blob, o, [[nkt * 256, P], [1, kcnt * 256]]),
                "w%d" % unit, waits=[wp_guard[unit]] + list(extra_waits))
            return m

        def emit_back_head(g):
            t0 = g * TC
            mb1 = pr.dma("sync", lambda: fp(FB_BC1, 512),
                         lambda g=g: _ap(st_scr, t0, [[0, P], [1, TC]]),
                         "bc1", waits=[("v", pr.mark("v")),
                                       ("spst", m_stats_done)])
            rb1 = pr.dma("sync", lambda: fp(FB_BC1 + 512, 512),
                         lambda g=g: _ap(st_scr, T + t0, [[0, P], [1, TC]]),
                         "bc1")
            for blk in range(4):
                unit = unit_rr[0] % 3
                unit_rr[0] += 1
                mw = wp_load(unit, op16, blk * P * 16 * 256, NJ, 16)
                for mt in range(2):
                    fi = blk * 2 + mt
                    bank = fi % 4
                    for j in range(NJ):
                        w = []
                        if j == 0:
                            w = [("w%d" % unit, mw), ("v", m_ymix[g]),
                                 bank_guard[bank]]
                        pr.emit("tensor",
                                lambda e, mt=mt, j=j, b=bank, u=unit:
                                e.matmul(
                                    ps[b][:, :],
                                    bp(BB_WP + u * 4096 + j * 256 + mt * P,
                                       P),
                                    bp(BB_YM + j * 512, 512),
                                    start=(j == 0), stop=(j == NJ - 1)),
                                waits=w, incs=[("mm", 1)])
                    m_op = pr.mark("mm")
                    xsl = fi % 2
                    mxf = pr.dma(
                        "sync", lambda xsl=xsl: fp(FB_XF + xsl * 512, 512),
                        lambda g=g, fi=fi: xT[fi * P:(fi + 1) * P,
                                              t0:t0 + TC],
                        "x%d" % xsl, waits=[("v", pr.mark("v"))] if fi >= 2
                        else [("v", m_ymix[g])])
                    pr.emit("vector",
                            lambda e, xsl=xsl: e.tensor_tensor(
                                fp(FB_TS, 512), fp(FB_XF + xsl * 512, 512),
                                fp(FB_BC1, 512), ALU.subtract),
                            waits=[("x%d" % xsl, mxf), ("bc1", rb1),
                                   ("mm", m_op), ("s", pr.mark("s"))],
                            incs=[("v", 1)])
                    pr.emit("vector",
                            lambda e: e.tensor_tensor(
                                fp(FB_TS, 512), fp(FB_TS, 512),
                                fp(FB_BC1 + 512, 512), ALU.mult),
                            incs=[("v", 1)])
                    pr.emit("vector",
                            lambda e, xsl=xsl: e.tensor_tensor(
                                fp(FB_TS, 512), fp(FB_TS, 512),
                                fp(FB_XF + xsl * 512, 512), ALU.add),
                            incs=[("v", 1)])
                    pr.emit("vector",
                            lambda e, fi=fi, b=bank: e.tensor_tensor(
                                fp(FB_X1 + fi * 512, 512), fp(FB_TS, 512),
                                ps[b][:, :], ALU.add),
                            incs=[("v", 1)])
                    bank_guard[bank] = ("v", pr.mark("v"))
                wp_guard[unit] = ("mm", m_op)
            m_outproj[g] = pr.mark("mm")
            m_x1 = pr.mark("v")

            for fi in range(NF):
                pr.emit("scalar",
                        lambda e, fi=fi: e.activation(
                            fp(FB_TS, 512), fp(FB_X1 + fi * 512, 512),
                            AF.Square),
                        waits=[("v", m_x1), ("mm", pr.mark("mm"))],
                        incs=[("s", 1)])
                msq = pr.mark("s")
                w6 = [("v", m_x1)] + ([bank_guard[6]] if fi == 0 else [])
                pr.emit("tensor",
                        lambda e, fi=fi: e.matmul(
                            ps[6][0:1, :], c_sb[:, C_ONE:C_ONE + 1],
                            fp(FB_X1 + fi * 512, 512),
                            start=(fi == 0), stop=(fi == NF - 1)),
                        waits=w6, incs=[("mm", 1)])
                w7 = [("s", msq)] + ([bank_guard[7]] if fi == 0 else [])
                pr.emit("tensor",
                        lambda e, fi=fi: e.matmul(
                            ps[7][0:1, :], c_sb[:, C_ONE:C_ONE + 1],
                            fp(FB_TS, 512),
                            start=(fi == 0), stop=(fi == NF - 1)),
                        waits=w7, incs=[("mm", 1)])
            m_st2 = pr.mark("mm")
            m2row = fp(FB_SR, 512, 0, 1)
            r2row = fp(FB_SR + 512, 512, 0, 1)
            pr.emit("scalar",
                    lambda e: e.activation(m2row, ps[6][0:1, :], AF.Copy,
                                           scale=cs0(C_RDM)),
                    waits=[("mm", m_st2), ("st2", pr.mark("st2"))],
                    incs=[("s", 1)])
            pr.emit("scalar",
                    lambda e: e.activation(r2row, ps[7][0:1, :], AF.Copy,
                                           scale=cs0(C_RDM)),
                    incs=[("s", 1)])
            bank_guard[6] = ("s", pr.mark("s"))
            bank_guard[7] = ("s", pr.mark("s"))
            pr.emit("scalar",
                    lambda e: e.activation(fp(FB_TS, 512, 0, 1), m2row,
                                           AF.Square),
                    incs=[("s", 1)])
            m_s2 = pr.mark("s")
            pr.emit("vector",
                    lambda e: e.tensor_tensor(r2row, r2row,
                                              fp(FB_TS, 512, 0, 1),
                                              ALU.subtract),
                    waits=[("s", m_s2)], incs=[("v", 1)])
            m_v2 = pr.mark("v")
            pr.emit("scalar",
                    lambda e: e.activation(r2row, r2row, AF.Ln,
                                           bias=cs0(C_EPS)),
                    waits=[("v", m_v2)], incs=[("s", 1)])
            pr.emit("scalar",
                    lambda e: e.activation(r2row, r2row, AF.Exp,
                                           scale=cs0(C_NHALF)),
                    incs=[("s", 1)])
            m_rr2 = pr.mark("s")
            pr.emit("tensor",
                    lambda e: e.matmul(ps[6][:, :],
                                       c_sb[0:1, C_OROW:C_OROW + P],
                                       m2row, start=True, stop=True),
                    waits=[("s", m_rr2), bank_guard[6]], incs=[("mm", 1)])
            pr.emit("tensor",
                    lambda e: e.matmul(ps[7][:, :],
                                       c_sb[0:1, C_OROW:C_OROW + P],
                                       r2row, start=True, stop=True),
                    waits=[("s", m_rr2), bank_guard[7]], incs=[("mm", 1)])
            m_bc2 = pr.mark("mm")
            for fi in range(NF):
                pr.emit("vector",
                        lambda e, fi=fi: e.tensor_tensor(
                            fp(FB_TS, 512), fp(FB_X1 + fi * 512, 512),
                            ps[6][:, :], ALU.subtract),
                        waits=[("mm", m_bc2)], incs=[("v", 1)])
                pr.emit("vector",
                        lambda e, fi=fi: e.tensor_tensor(
                            bp(BB_LN1 + fi * 512, 512), fp(FB_TS, 512),
                            ps[7][:, :], ALU.mult),
                        incs=[("v", 1)])
            bank_guard[6] = ("v", pr.mark("v"))
            bank_guard[7] = ("v", pr.mark("v"))
            m_lnx1_s[0] = pr.mark("v")
            m_lnx1_dbg[g] = m_lnx1_s[0]


        def gen_back_ffn(g):
            t0 = g * TC
            m_lnx1 = m_lnx1_s[0]
            for hf in range(2):
                hoff = hf * 256
                for blk in range(16):
                    unit = unit_rr[0] % 3
                    unit_rr[0] += 1
                    mw1 = pr.dma(
                        "sync", lambda u=unit: bp(BB_WP + u * 4096,
                                                  NF * 256),
                        lambda blk=blk: _ap(w1b, blk * P * 8 * 256,
                                            [[8 * 256, P], [1, NF * 256]]),
                        "w%d" % unit, waits=[wp_guard[unit]])
                    mw2 = pr.dma(
                        "sync", lambda u=unit: bp(BB_WP + u * 4096 +
                                                  NF * 256, NF * 256),
                        lambda blk=blk: _ap(w2b, blk * P * 8 * 256,
                                            [[8 * 256, P], [1, NF * 256]]),
                        "w%d" % unit)
                    for mt in range(2):
                        ht = blk * 2 + mt
                        b1 = (ht % 3) * 2
                        b2 = b1 + 1
                        for k in range(NF):
                            w = []
                            if k == 0:
                                w = [("w%d" % unit, mw2), ("v", m_lnx1),
                                     bank_guard[b1]]
                            pr.emit("tensor",
                                    lambda e, k=k, mt=mt, b=b1, u=unit,
                                    hoff=hoff: e.matmul(
                                        ps[b][:, 0:256],
                                        bp(BB_WP + u * 4096 + k * 256 +
                                           mt * P, P),
                                        bp(BB_LN1 + k * 512 + hoff, 256),
                                        start=(k == 0), stop=(k == NF - 1)),
                                    waits=w, incs=[("mm", 1)])
                        for k in range(NF):
                            w = []
                            if k == 0:
                                w = [("w%d" % unit, mw2), bank_guard[b2]]
                            pr.emit("tensor",
                                    lambda e, k=k, mt=mt, b=b2, u=unit,
                                    hoff=hoff: e.matmul(
                                        ps[b][:, 0:256],
                                        bp(BB_WP + u * 4096 + NF * 256 +
                                           k * 256 + mt * P, P),
                                        bp(BB_LN1 + k * 512 + hoff, 256),
                                        start=(k == 0), stop=(k == NF - 1)),
                                    waits=w, incs=[("mm", 1)])
                        m_a12 = pr.mark("mm")
                        pr.emit("scalar",
                                lambda e, ht=ht, b=b1: e.activation(
                                    fp(FB_TS2, 256), ps[b][:, 0:256],
                                    AF.Tanh, scale=hs1[:, ht:ht + 1]),
                                waits=[("mm", m_a12), ("v", pr.mark("v"))],
                                incs=[("s", 1)])
                        m_t = pr.mark("s")
                        pr.emit("vector",
                                lambda e, b=b1: e.scalar_tensor_tensor(
                                    fp(FB_TS2, 256), fp(FB_TS2, 256), 1.0,
                                    ps[b][:, 0:256], ALU.add, ALU.mult),
                                waits=[("s", m_t)], incs=[("v", 1)])
                        bank_guard[b1] = ("v", pr.mark("v"))
                        pr.emit("vector",
                                lambda e, ht=ht, b=b2:
                                e.scalar_tensor_tensor(
                                    bp(BB_G + ht * 256, 256),
                                    fp(FB_TS2, 256),
                                    k12[:, ht:ht + 1], ps[b][:, 0:256],
                                    ALU.mult, ALU.mult),
                                incs=[("v", 1)])
                        bank_guard[b2] = ("v", pr.mark("v"))
                    wp_guard[unit] = ("mm", pr.mark("mm"))
                    yield
                m_g = pr.mark("v")

                def h_blocks(blob, src_off, dst_off, sc, m_in, m_in_sem):
                    for blk in range(16):
                        ua = unit_rr[0] % 3
                        unit_rr[0] += 1
                        ub = unit_rr[0] % 3
                        unit_rr[0] += 1
                        mwa = wp_load(ua, blob, blk * P * 32 * 256, 16, 32)
                        mwb = wp_load(ub, blob,
                                      blk * P * 32 * 256 + 16 * 256, 16, 32)
                        for mt in range(2):
                            ht = blk * 2 + mt
                            bank = ht % 6
                            for k in range(NH):
                                unit = ua if k < 16 else ub
                                kk = k % 16
                                w = []
                                if k == 0:
                                    w = [("w%d" % ua, mwa),
                                         ("w%d" % ub, mwb),
                                         (m_in_sem, m_in),
                                         bank_guard[bank]]
                                pr.emit(
                                    "tensor",
                                    lambda e, k=k, kk=kk, mt=mt, b=bank,
                                    u=unit, so=src_off: e.matmul(
                                        ps[b][:, 0:256],
                                        bp(BB_WP + u * 4096 + kk * 256 +
                                           mt * P, P),
                                        bp(so + k * 256, 256),
                                        start=(k == 0), stop=(k == NH - 1)),
                                    waits=w, incs=[("mm", 1)])
                            m_h = pr.mark("mm")
                            pr.emit("scalar",
                                    lambda e, ht=ht, b=bank, do=dst_off,
                                    sc=sc: e.activation(
                                        bp(do + ht * 256, 256),
                                        ps[b][:, 0:256], AF.Copy,
                                        scale=c_sb[:, sc + ht:sc + ht + 1]),
                                    waits=[("mm", m_h)], incs=[("s", 1)])
                            bank_guard[bank] = ("s", pr.mark("s"))
                        wp_guard[ua] = ("mm", pr.mark("mm"))
                        wp_guard[ub] = ("mm", pr.mark("mm"))
                        yield
                    pr.emit("scalar",
                            lambda e, do=dst_off: e.activation(
                                bp(do, NH * 256), bp(do, NH * 256),
                                AF.Gelu),
                            waits=[("s", pr.mark("s"))], incs=[("s", 1)])

                yield from h_blocks(h1b, BB_G + 0, BB_P, C_SH1, m_g, "v")
                m_p1 = pr.mark("s")
                yield from h_blocks(h2b, BB_P, BB_G, C_SH2, m_p1, "s")
                m_p2 = pr.mark("s")

                for blk in range(4):
                    ua = unit_rr[0] % 3
                    unit_rr[0] += 1
                    ub = unit_rr[0] % 3
                    unit_rr[0] += 1
                    mwa = wp_load(ua, w3b, blk * P * 32 * 256, 16, 32)
                    mwb = wp_load(ub, w3b, blk * P * 32 * 256 + 16 * 256,
                                  16, 32)
                    for mt in range(2):
                        fi = blk * 2 + mt
                        bank = 4 + (fi % 2)
                        for k in range(NH):
                            unit = ua if k < 16 else ub
                            kk = k % 16
                            w = []
                            if k == 0:
                                w = [("w%d" % ua, mwa), ("w%d" % ub, mwb),
                                     ("s", m_p2),
                                     bank_guard[bank]]
                            pr.emit("tensor",
                                    lambda e, k=k, kk=kk, mt=mt, b=bank,
                                    u=unit: e.matmul(
                                        ps[b][:, 0:256],
                                        bp(BB_WP + u * 4096 + kk * 256 +
                                           mt * P, P),
                                        bp(BB_G + k * 256, 256),
                                        start=(k == 0), stop=(k == NH - 1)),
                                    waits=w, incs=[("mm", 1)])
                        m_w3 = pr.mark("mm")
                        pr.emit("scalar",
                                lambda e, fi=fi, b=bank: e.activation(
                                    fp(FB_FF + fi * 256, 256),
                                    ps[b][:, 0:256], AF.Copy,
                                    scale=c_sb[:, C_S3 + fi:C_S3 + fi + 1]),
                                waits=[("mm", m_w3), ff_guard[fi]],
                                incs=[("s", 1)])
                        bank_guard[bank] = ("s", pr.mark("s"))
                        m_ev = pr.mark("s")
                        pr.emit("vector",
                                lambda e, fi=fi, hoff=hoff: e.tensor_tensor(
                                    fp(FB_FF + fi * 256, 256),
                                    fp(FB_FF + fi * 256, 256),
                                    fp(FB_X1 + fi * 512 + hoff, 256),
                                    ALU.add),
                                waits=[("s", m_ev)], incs=[("v", 1)])
                        m_out = pr.mark("v")
                        pr.dma("sync",
                               lambda g=g, fi=fi, hoff=hoff:
                               y_out[fi * P:(fi + 1) * P,
                                     t0 + hoff:t0 + hoff + 256],
                               lambda fi=fi: fp(FB_FF + fi * 256, 256),
                               "o%d" % fi, waits=[("v", m_out)])
                        ff_guard[fi] = ("o%d" % fi, pr.mark("o%d" % fi))
                    wp_guard[ua] = ("mm", pr.mark("mm"))
                    wp_guard[ub] = ("mm", pr.mark("mm"))
                    yield

        for j in range(NJ):
            u_guard[j] = ("spu", m_spu)
            dt_guard[j] = ("spdt", m_spdt)
        for j in range(2):
            sz_guard[j] = ("spsz", m_spsz)
        for u in range(3):
            wp_guard[u] = ("mm", 0)
        pr.emit("sync", lambda e: None, waits=list(m_a3_join))
        emit_loads(0)
        for s in range(NCH + 1):
            ffn = None
            if s >= 1:
                emit_back_head(s - 1)
                ffn = gen_back_ffn(s - 1)
            if s < NCH:
                emit_scan_head(s)
                for q in range(8):
                    emit_scan_q(s, q)
                    if ffn is not None:
                        for _ in range(14):
                            if next(ffn, "done") == "done":
                                ffn = None
                                break
                emit_scan_tail(s)
                if s + 1 < NCH:
                    emit_loads(s + 1)
                if debug and s == 0:
                    pr.dma("sync", lambda: dbg_y[:, :],
                           lambda: fp(FB_Y, NJ * 512),
                           "dbgs", waits=[("v", m_ymix[0])])
                    pr.dma("sync", lambda: dbg_ym[:, :],
                           lambda: bp(BB_YM, NJ * 512), "dbgs",
                           waits=[("v", m_ymix[0])])
            if ffn is not None:
                for _ in iter(lambda: next(ffn, "done"), "done"):
                    pass
            if debug and s == 1:
                pr.dma("sync", lambda: dbg_x1[:, :],
                       lambda: fp(FB_X1, NF * 512), "dbgs",
                       waits=[("v", m_lnx1_dbg[0])])
                pr.dma("sync", lambda: dbg_ln1[:, :],
                       lambda: bp(BB_LN1, NF * 512), "dbgs",
                       waits=[("v", m_lnx1_dbg[0])])

        final_waits = [("o%d" % fi, pr.mark("o%d" % fi))
                       for fi in range(NF)]
        if debug:
            final_waits.append(("dbgs", pr.mark("dbgs")))
        pr.emit("sync", lambda e: e.nop(), waits=final_waits)

        semh = {}
        with ExitStack() as semctx:
            for name in pr.semnames:
                semh[name] = semctx.enter_context(nc.semaphore(name))
            with nc.Block() as block:
                def runner(stream):
                    def run(eng_obj):
                        for waits, fn, incs in stream:
                            for s, val in waits:
                                eng_obj.wait_ge(semh[s], val)
                            instr = fn(eng_obj)
                            for s, d in incs:
                                instr.then_inc(semh[s], d)
                    return run

                block.sync(runner(pr.streams["sync"]))
                block.vector(runner(pr.streams["vector"]))
                block.scalar(runner(pr.streams["scalar"]))
                block.tensor(runner(pr.streams["tensor"]))
                block.gpsimd(runner(pr.streams["gpsimd"]))

    return nc



def _pack_cvec(conv_w, conv_b, dtb, A_log, D, nw, s1, s2, sh1, sh2, s3):
    c = np.zeros((P, 652), dtype=np.float32)
    c[:, 0:64] = conv_w.reshape(16, 128, 4).transpose(1, 0, 2).reshape(128, 64)
    c[:, 64:80] = conv_b.reshape(16, 128).T
    c[:, 80:96] = dtb.reshape(16, 128).T
    c[:, 96:352] = A_log.reshape(16, 128, 16).transpose(1, 0, 2).reshape(
        128, 256)
    c[:, 352:368] = D.reshape(16, 128).T
    c[:, 368:376] = nw.reshape(8, 128).T
    c[:, 376:408] = s1.reshape(32, 128).T
    c[:, 408:440] = s2.reshape(32, 128).T
    c[:, 440:472] = sh1.reshape(32, 128).T
    c[:, 472:504] = sh2.reshape(32, 128).T
    c[:, 504:512] = s3.reshape(8, 128).T
    c[:, 512] = 1.0
    c[:, 513] = EPS
    c[:, 514] = 1.0 + EPS
    c[:, 515] = EPS * EPS
    c[:, 516] = -0.5
    c[:, 517] = -1.0
    c[:, 518] = 1.0 / DM
    c[0, 520:648] = 1.0
    return c


def _blob_win(w):
    return np.ascontiguousarray(
        w.reshape(NF, P, 32, P).transpose(2, 1, 0, 3)).reshape(-1)


def _blob(wt, nblk):
    K, M = wt.shape
    assert M == nblk * 256
    nkt = K // P
    return np.ascontiguousarray(
        wt.reshape(nkt, P, nblk, 256).transpose(2, 1, 0, 3)).reshape(-1)


_NC_CACHE = {}


def kernel(**inp):
    f32 = np.float32
    x = np.ascontiguousarray(np.asarray(inp["L0_attn_x"], dtype=f32))
    B = x.shape[0]
    asnp = lambda k: np.ascontiguousarray(np.asarray(inp[k], dtype=f32))

    shared = dict(
        winp=_blob_win(asnp("L0_attn_in_proj_w")),
        wxp=asnp("L0_attn_x_proj_w"),
        wdt=asnp("L0_attn_dt_proj_w"),
        wout=_blob(asnp("L0_attn_out_proj_w"), 4),
        w1t=_blob(asnp("L0_ffn_w1_w").T, 16),
        w2t=_blob(asnp("L0_ffn_w2_w").T, 16),
        h1t=_blob(asnp("L0_ffn_h1_w").T, 16),
        h2t=_blob(asnp("L0_ffn_h2_w").T, 16),
        w3t=_blob(asnp("L0_ffn_w3_w").T, 4),
        cvec=_pack_cvec(
            asnp("L0_attn_conv_w"), asnp("L0_attn_conv_b"),
            asnp("L0_attn_dt_proj_b"), asnp("L0_attn_A_log"),
            asnp("L0_attn_D"), asnp("L0_attn_norm_w"),
            asnp("L0_ffn_w1_s"), asnp("L0_ffn_w2_s"),
            asnp("L0_ffn_h1_s"), asnp("L0_ffn_h2_s"),
            asnp("L0_ffn_w3_s")),
    )

    if "nc" not in _NC_CACHE:
        _NC_CACHE["nc"] = build_program()
    nc = _NC_CACHE["nc"]

    in_maps = []
    for b in range(B):
        m = dict(shared)
        m["xT"] = np.ascontiguousarray(x[b].T)
        in_maps.append(m)

    res = run_bass_kernel_spmd(nc, in_maps, core_ids=list(range(B)))
    out = np.stack([np.asarray(res.results[b]["y_out"]).T for b in range(B)])
    return out.astype(np.float32)


# version 23
# speedup vs baseline: 1.1577x; 1.1577x over previous
import numpy as np
from contextlib import ExitStack

import concourse.bass as bass
import concourse.mybir as mybir
from concourse.bass_utils import run_bass_kernel_spmd

F32 = mybir.dt.float32
BF16 = mybir.dt.bfloat16
AF = mybir.ActivationFunctionType
ALU = mybir.AluOpType

P = 128
T = 4096
DM = 1024
DI = 2048
DH = 4096
NST = 16
RNK = 64
NF = DM // P
NJ = DI // P
NH = DH // P
TC = 512
NCH = T // TC
TS = 256
EPS = 1e-5

ENGINES = ("sync", "vector", "scalar", "tensor", "gpsimd")

FB_Y = 0
FB_A = FB_Y + NJ * 512
FB_XF = FB_A + 3 * 512
FB_X1 = FB_XF + 1 * 512
FB_TS = FB_X1 + NF * 512
FB_TS2 = FB_TS
FB_BC1 = FB_TS + 512
FB_FF = FB_BC1 + 2 * 512
FB_SR = FB_FF + NF * 256
FB_END = FB_SR + 2 * 512
FA_URAW = 0
FA_CVA = FA_URAW + 520
FA_CVB = FA_CVA + 2048
FA_XDB = FA_CVB + 2048
FA_XS = FA_XDB + 4096
FA_WC = FA_XS + 1024
FA_SR = FA_WC + 1024
FA_END = FA_SR + 3 * 512
FW = max(FB_END, FA_END)

BB_U = 0
BB_DT = BB_U + NJ * 512
BB_G = BB_DT + NJ * 512
BB_P = BB_G + NH * 256
BB_YM = BB_P + NH * 256
BB_LN1 = BB_YM + NJ * 512
BB_SZ = BB_LN1 + NF * 512
BB_BB = BB_SZ + 2 * 512
BB_CB = BB_BB + 2 * 1024
BB_B = BB_CB + 2 * 1024
BB_H = BB_B + 1024
BB_HC = BB_H + 1024
BB_WP = BB_HC + 1024
BB_END = BB_WP + 4 * 4096
BA_LNM = 0
BA_WINB = BA_LNM + NF * 2048
BA_UBF = BA_WINB + 2 * 1024
BA_SZS = BA_UBF + 4096
BA_WXB = BA_SZS + 2048
BA_DTR = BA_WXB + NJ * 96
BA_WDT = BA_DTR + 4096
BA_DTS = BA_WDT + NJ * 128
BA_WC = BB_WP
BA_URAW = BA_DTS + 4096
BA_EV = BA_URAW + 4100
BA_END = BA_EV + 2048
BW = max(BB_END, BA_END)

assert FW * 4 + BW * 2 <= 216200, (FW * 4, BW * 2)


ENGINE_SEM = {"vector": "v", "scalar": "s", "tensor": "mm", "gpsimd": "gp"}


class Prog:
    def __init__(self, selfwait=False):
        self.streams = {e: [] for e in ENGINES}
        self.counts = {}
        self.semnames = []
        self.selfwait = selfwait

    def sem(self, name):
        if name not in self.counts:
            self.counts[name] = 0
            self.semnames.append(name)

    def mark(self, name):
        self.sem(name)
        return self.counts[name]

    def emit(self, eng, fn, waits=(), incs=()):
        if self.selfwait and eng in ENGINE_SEM:
            es = ENGINE_SEM[eng]
            self.sem(es)
            waits = tuple(waits) + ((es, self.counts[es]),)
        waits = tuple((s, v) for (s, v) in waits if v > 0)
        for s, v in waits:
            self.sem(s)
        for s, d in incs:
            self.sem(s)
            self.counts[s] += d
        self.streams[eng].append((waits, fn, tuple(incs)))

    def dma(self, eng, dst_fn, src_fn, sem, waits=()):
        self.emit(eng, lambda e, d=dst_fn, s=src_fn: e.dma_start(d(), s()),
                  waits=waits, incs=[(sem, 16)])
        return self.mark(sem)


def _ap(t, off, dims):
    return bass.AP(t, off, [list(d) for d in dims])


def build_program(selfwait=False, detect=False, debug=False):
    nc = bass.Bass(target_bir_lowering=False, dynamic_dma_scratch_size=8192,
                   detect_race_conditions=detect)

    xT = nc.dram_tensor("xT", [DM, T], F32, kind="ExternalInput")
    winp = nc.dram_tensor("winp", [32 * P * 1024], F32,
                          kind="ExternalInput")
    wxp = nc.dram_tensor("wxp", [DI, 96], F32, kind="ExternalInput")
    wdt = nc.dram_tensor("wdt", [RNK, DI], F32, kind="ExternalInput")
    wout = nc.dram_tensor("wout", [4 * P * 16 * 256], F32,
                          kind="ExternalInput")
    w1t = nc.dram_tensor("w1t", [16 * P * 8 * 256], F32,
                         kind="ExternalInput")
    w2t = nc.dram_tensor("w2t", [16 * P * 8 * 256], F32,
                         kind="ExternalInput")
    h1t = nc.dram_tensor("h1t", [16 * P * 32 * 256], F32,
                         kind="ExternalInput")
    h2t = nc.dram_tensor("h2t", [16 * P * 32 * 256], F32,
                         kind="ExternalInput")
    w3t = nc.dram_tensor("w3t", [4 * P * 32 * 256], F32,
                         kind="ExternalInput")
    cvec = nc.dram_tensor("cvec", [P, 652], F32, kind="ExternalInput")

    y_out = nc.dram_tensor("y_out", [DM, T], F32, kind="ExternalOutput")

    uraw_scr = nc.dram_tensor("uraw_scr", [DI, T], BF16, kind="Internal")
    u_scr = nc.dram_tensor("u_scr", [DI, T], BF16, kind="Internal")
    dt_scr = nc.dram_tensor("dt_scr", [DI, T], BF16, kind="Internal")
    sz_scr = nc.dram_tensor("sz_scr", [DI, T], BF16, kind="Internal")
    bc_scr = nc.dram_tensor("bc_scr", [32, T], BF16, kind="Internal")
    st_scr = nc.dram_tensor("st_scr", [8, T], F32, kind="Internal")
    winb = nc.dram_tensor("winb", [32 * P * 1024], BF16, kind="Internal")
    wxpb = nc.dram_tensor("wxpb", [DI, 96], BF16, kind="Internal")
    wdtb = nc.dram_tensor("wdtb", [RNK, DI], BF16, kind="Internal")
    op16 = nc.dram_tensor("op16", [4 * DI * 256], BF16, kind="Internal")
    w1b = nc.dram_tensor("w1b", [16 * DM * 256], BF16, kind="Internal")
    w2b = nc.dram_tensor("w2b", [16 * DM * 256], BF16, kind="Internal")
    h1b = nc.dram_tensor("h1b", [16 * DH * 256], BF16, kind="Internal")
    h2b = nc.dram_tensor("h2b", [16 * DH * 256], BF16, kind="Internal")
    w3b = nc.dram_tensor("w3b", [4 * DH * 256], BF16, kind="Internal")
    if debug:
        dbg_y = nc.dram_tensor("dbg_y", [P, NJ * 512], F32,
                               kind="ExternalOutput")
        dbg_ym = nc.dram_tensor("dbg_ym", [P, NJ * 512], BF16,
                                kind="ExternalOutput")
        dbg_x1 = nc.dram_tensor("dbg_x1", [P, NF * 512], F32,
                                kind="ExternalOutput")
        dbg_ln1 = nc.dram_tensor("dbg_ln1", [P, NF * 512], BF16,
                                 kind="ExternalOutput")
        dbg_u = nc.dram_tensor("dbg_u", [P, NJ * 512], BF16,
                               kind="ExternalOutput")
        dbg_dt = nc.dram_tensor("dbg_dt", [P, NJ * 512], BF16,
                                kind="ExternalOutput")
        dbg_bb = nc.dram_tensor("dbg_bb", [P, 2048], BF16,
                                kind="ExternalOutput")
        dbg_bh = nc.dram_tensor("dbg_bh", [P, 2048], F32,
                                kind="ExternalOutput")
        dbg_an = nc.dram_tensor("dbg_an", [P, 256], F32,
                                kind="ExternalOutput")
        dbg_a = nc.dram_tensor("dbg_a", [P, 1024], F32,
                               kind="ExternalOutput")
        dbg_st = nc.dram_tensor("dbg_st", [P, 256], F32,
                                kind="ExternalOutput")

    pr = Prog(selfwait=selfwait)
    for name in ("v", "s", "mm", "gp"):
        pr.sem(name)

    with ExitStack() as ctx:
        c_sb = ctx.enter_context(nc.sbuf_tensor("c_sb", [P, 652], F32))
        aneg = ctx.enter_context(nc.sbuf_tensor("aneg", [P, NJ * NST], F32))
        hs1 = ctx.enter_context(nc.sbuf_tensor("hs1", [P, NH], F32))
        k12 = ctx.enter_context(nc.sbuf_tensor("k12", [P, NH], F32))
        state = ctx.enter_context(nc.sbuf_tensor("state", [P, NJ * NST], F32))
        fpool = ctx.enter_context(nc.sbuf_tensor("fpool", [P, FW], F32))
        bpool = ctx.enter_context(nc.sbuf_tensor("bpool", [P, BW], BF16))
        ps = [ctx.enter_context(nc.psum_tensor(f"ps{i}", [P, 512], F32))
              for i in range(8)]

        bank_guard = [("v", 0)] * 8

        C_CW, C_CB, C_DTB, C_AL = 0, 64, 80, 96
        C_D, C_NW, C_S1, C_S2 = 352, 368, 376, 408
        C_SH1, C_SH2, C_S3, C_ONE = 440, 472, 504, 512
        C_EPS, C_1PE, C_EPS2, C_NHALF, C_NONE, C_RDM = 513, 514, 515, 516, 517, 518
        C_OROW = 520

        def fp(off, w, p0=0, pn=P):
            return fpool[p0:p0 + pn, off:off + w]

        def bp(off, w, p0=0, pn=P):
            return bpool[p0:p0 + pn, off:off + w]

        def cs0(col):
            return c_sb[0:1, col:col + 1]

        def csP(col):
            return c_sb[:, col:col + 1]

        m_cv = pr.dma("sync", lambda: c_sb[:, :], lambda: cvec[:, :], "cvld")
        pr.emit("vector", lambda e: e.memset(state[:, :], 0.0),
                incs=[("v", 1)])
        pr.emit("vector", lambda e: e.memset(bp(BA_URAW, 4), 0.0),
                incs=[("v", 1)])
        pr.emit("scalar",
                lambda e: e.activation(aneg[:, :], c_sb[:, C_AL:C_AL + 256],
                                       AF.Exp),
                waits=[("cvld", m_cv)], incs=[("s", 1)])
        pr.emit("vector",
                lambda e: e.tensor_scalar(aneg[:, :], aneg[:, :], -1.0, None,
                                          ALU.mult),
                waits=[("s", pr.mark("s"))], incs=[("v", 1)])
        pr.emit("vector",
                lambda e: e.tensor_scalar(hs1[:, :], c_sb[:, C_S1:C_S1 + NH],
                                          0.5, None, ALU.mult),
                waits=[("cvld", m_cv)], incs=[("v", 1)])
        pr.emit("vector",
                lambda e: e.scalar_tensor_tensor(
                    k12[:, :], c_sb[:, C_S1:C_S1 + NH], 0.5,
                    c_sb[:, C_S2:C_S2 + NH], ALU.mult, ALU.mult),
                waits=[("cvld", m_cv)], incs=[("v", 1)])

        sqs = fp(FA_WC, 512)
        xg = [("mm", 0), ("mm", 0)]
        for piece in range(8):
            t0 = piece * 512
            for fi in range(NF):
                sl = fi % 2
                buf = fp(FA_XS + sl * 512, 512)
                mload = pr.dma(
                    "sync", lambda b=buf: b,
                    lambda fi=fi, t0=t0: xT[fi * P:(fi + 1) * P, t0:t0 + 512],
                    "xa%d" % sl, waits=[xg[sl]])
                pr.emit("scalar",
                        lambda e, b=buf: e.activation(sqs, b, AF.Square),
                        waits=[("xa%d" % sl, mload), ("mm", pr.mark("mm"))],
                        incs=[("s", 1)])
                msq = pr.mark("s")
                w0 = [("xa%d" % sl, mload)] + (
                    [bank_guard[0], ("cvld", m_cv)] if fi == 0 else [])
                pr.emit("tensor",
                        lambda e, b=buf, fi=fi: e.matmul(
                            ps[0][0:1, :], c_sb[:, C_ONE:C_ONE + 1], b,
                            start=(fi == 0), stop=(fi == NF - 1)),
                        waits=w0, incs=[("mm", 1)])
                w1 = [("s", msq)] + (
                    [bank_guard[1], ("cvld", m_cv)] if fi == 0 else [])
                pr.emit("tensor",
                        lambda e, fi=fi: e.matmul(
                            ps[1][0:1, :], c_sb[:, C_ONE:C_ONE + 1], sqs,
                            start=(fi == 0), stop=(fi == NF - 1)),
                        waits=w1, incs=[("mm", 1)])
                xg[sl] = ("mm", pr.mark("mm"))
            m_stats = pr.mark("mm")
            mrow = fp(FA_SR, 512, 0, 1)
            vrow = fp(FA_SR + 512, 512, 0, 1)
            rrow = fp(FA_SR + 1024, 512, 0, 1)
            pr.emit("scalar",
                    lambda e: e.activation(mrow, ps[0][0:1, :], AF.Copy,
                                           scale=cs0(C_RDM)),
                    waits=[("mm", m_stats), ("spst", pr.mark("spst"))],
                    incs=[("s", 1)])
            pr.emit("scalar",
                    lambda e: e.activation(vrow, ps[1][0:1, :], AF.Copy,
                                           scale=cs0(C_RDM)),
                    incs=[("s", 1)])
            bank_guard[0] = ("s", pr.mark("s"))
            bank_guard[1] = ("s", pr.mark("s"))
            m_mv = pr.mark("s")
            pr.emit("scalar", lambda e: e.activation(rrow, mrow, AF.Square),
                    waits=[("spst", pr.mark("spst"))], incs=[("s", 1)])
            pr.emit("vector",
                    lambda e: e.tensor_tensor(vrow, vrow, rrow, ALU.subtract),
                    waits=[("s", pr.mark("s"))], incs=[("v", 1)])
            m_var = pr.mark("v")
            pr.dma("sync", lambda t0=t0: st_scr[0:1, t0:t0 + 512],
                   lambda: mrow, "spst", waits=[("s", m_mv)])
            pr.emit("scalar",
                    lambda e: e.activation(rrow, vrow, AF.Ln,
                                           bias=cs0(C_EPS)),
                    waits=[("v", m_var)], incs=[("s", 1)])
            pr.emit("scalar",
                    lambda e: e.activation(rrow, rrow, AF.Exp,
                                           scale=cs0(C_NHALF)),
                    incs=[("s", 1)])
            pr.dma("sync", lambda t0=t0: st_scr[1:2, t0:t0 + 512],
                   lambda: rrow, "spst", waits=[("s", pr.mark("s"))])
            m_r1w = pr.mark("spst")
            pr.emit("scalar",
                    lambda e: e.activation(rrow, vrow, AF.Ln,
                                           scale=cs0(C_1PE),
                                           bias=cs0(C_EPS2)),
                    waits=[("spst", m_r1w)], incs=[("s", 1)])
            pr.emit("scalar",
                    lambda e: e.activation(rrow, rrow, AF.Exp,
                                           scale=cs0(C_NHALF)),
                    incs=[("s", 1)])
            pr.dma("sync", lambda t0=t0: st_scr[2:3, t0:t0 + 512],
                   lambda: rrow, "spst", waits=[("s", pr.mark("s"))])
        m_stats_done = pr.mark("spst")

        def a3_ap(t, blk, nkt):
            dims = [[nkt * 256, P]]
            nch = (nkt * 256) // 2048
            if nch > 1:
                dims.append([2048, nch])
            dims.append([1, min(2048, nkt * 256)])
            return _ap(t, blk * P * nkt * 256, dims)

        a3_items = [(lambda: wdtb[:, :], lambda: wdt[:, :]),
                    (lambda: _ap(wxpb, 0, [[2048, 96], [1, 2048]]),
                     lambda: _ap(wxp, 0, [[2048, 96], [1, 2048]]))]
        for c in range(4):
            a3_items.append((
                lambda c=c: _ap(winb, c * P * 8192,
                                [[8192, P], [2048, 4], [1, 2048]]),
                lambda c=c: _ap(winp, c * P * 8192,
                                [[8192, P], [2048, 4], [1, 2048]])))
        for (srct, blob, nblk, K) in ((wout, op16, 4, DI), (w1t, w1b, 16, DM),
                                      (w2t, w2b, 16, DM), (h1t, h1b, 16, DH),
                                      (h2t, h2b, 16, DH), (w3t, w3b, 4, DH)):
            nkt = K // P
            for blk in range(nblk):
                a3_items.append((
                    lambda blob=blob, blk=blk, nkt=nkt: a3_ap(blob, blk, nkt),
                    lambda srct=srct, blk=blk, nkt=nkt: a3_ap(srct, blk,
                                                              nkt)))
        a3_prev = [0, 0]
        for i, (dfn, sfn) in enumerate(a3_items):
            r = i % 2
            w = [("a3_%d" % r, a3_prev[r])] if a3_prev[r] else []
            a3_prev[r] = pr.dma("gpsimd", dfn, sfn, "a3_%d" % r, waits=w)
        m_a3_join = (("a3_0", a3_prev[0]), ("a3_1", a3_prev[1]))

        pr.emit("sync", lambda e: None, waits=list(m_a3_join))
        ev_guard = ("spev", 0)
        szs_guard = ("spsz", 0)
        for th in range(2):
            t0h = th * 2048
            mb = fp(FA_CVA, 2048)
            rb = fp(FA_CVB, 2048)
            pr.dma("sync", lambda: mb,
                   lambda t0h=t0h: _ap(st_scr, t0h, [[0, P], [1, 2048]]),
                   "bca", waits=[("spst", m_stats_done),
                                 ("v", pr.mark("v")), ("s", pr.mark("s"))])
            m_bc = pr.dma("sync", lambda: rb,
                          lambda t0h=t0h: _ap(st_scr, 2 * T + t0h,
                                              [[0, P], [1, 2048]]),
                          "bca")
            for fi in range(NF):
                for pc in range(4):
                    sl = (fi * 4 + pc) % 2
                    buf = fp(FA_XS + sl * 512, 512)
                    mload = pr.dma(
                        "sync", lambda b=buf: b,
                        lambda fi=fi, t0h=t0h, pc=pc: xT[
                            fi * P:(fi + 1) * P,
                            t0h + pc * 512:t0h + pc * 512 + 512],
                        "xa%d" % sl, waits=[xg[sl]])
                    pr.emit("vector",
                            lambda e, b=buf, pc=pc: e.tensor_tensor(
                                fp(FA_URAW + 4, 512), b,
                                fp(FA_CVA + pc * 512, 512), ALU.subtract),
                            waits=[("xa%d" % sl, mload), ("bca", m_bc)],
                            incs=[("v", 1)])
                    pr.emit("vector",
                            lambda e, fi=fi, pc=pc: e.scalar_tensor_tensor(
                                bp(BA_LNM + fi * 2048 + pc * 512, 512),
                                fp(FA_URAW + 4, 512),
                                c_sb[:, C_NW + fi:C_NW + fi + 1],
                                fp(FA_CVB + pc * 512, 512),
                                ALU.mult, ALU.mult),
                            incs=[("v", 1)])
                    xg[sl] = ("v", pr.mark("v"))
            m_lnm = pr.mark("v")
            wb_guard = [("mm", 0), ("mm", 0)]
            for dtile in range(32):
                ws = dtile % 2
                mw = pr.dma(
                    "sync", lambda ws=ws: bp(BA_WINB + ws * 1024, 1024),
                    lambda d=dtile: _ap(winb, d * P * 1024,
                                        [[1024, P], [1, 1024]]),
                    "wia%d" % ws, waits=[wb_guard[ws]])
                for fi in range(NF):
                    for tsb in range(4):
                        w = []
                        if fi == 0:
                            w = [("wia%d" % ws, mw), ("v", m_lnm),
                                 bank_guard[tsb]]
                        pr.emit(
                            "tensor",
                            lambda e, fi=fi, tsb=tsb, ws=ws: e.matmul(
                                ps[tsb][:, :],
                                bp(BA_WINB + ws * 1024 + fi * P, P),
                                bp(BA_LNM + fi * 2048 + tsb * 512, 512),
                                start=(fi == 0), stop=(fi == NF - 1)),
                            waits=w, incs=[("mm", 1)])
                m_grp = pr.mark("mm")
                wb_guard[ws] = ("mm", m_grp)
                if dtile < NJ:
                    for tsb in range(4):
                        pr.emit("scalar",
                                lambda e, tsb=tsb: e.activation(
                                    bp(BA_EV + tsb * 512, 512),
                                    ps[tsb][:, :], AF.Copy),
                                waits=[("mm", m_grp), ev_guard]
                                if tsb == 0 else [("mm", m_grp)],
                                incs=[("s", 1)])
                        bank_guard[tsb] = ("s", pr.mark("s"))
                    m_ev = pr.mark("s")
                    pr.dma("sync",
                           lambda d=dtile, t0h=t0h: uraw_scr[
                               d * P:(d + 1) * P, t0h:t0h + 2048],
                           lambda: bp(BA_EV, 2048), "spev",
                           waits=[("s", m_ev)])
                    ev_guard = ("spev", pr.mark("spev"))
                else:
                    for tsb in range(4):
                        pr.emit("scalar",
                                lambda e, tsb=tsb: e.activation(
                                    bp(BA_SZS + tsb * 512, 512),
                                    ps[tsb][:, :], AF.Silu),
                                waits=[("mm", m_grp), szs_guard]
                                if tsb == 0 else [("mm", m_grp)],
                                incs=[("s", 1)])
                        bank_guard[tsb] = ("s", pr.mark("s"))
                    m_ev = pr.mark("s")
                    j = dtile - NJ
                    pr.dma("sync",
                           lambda j=j, t0h=t0h: sz_scr[
                               j * P:(j + 1) * P, t0h:t0h + 2048],
                           lambda: bp(BA_SZS, 2048), "spsz",
                           waits=[("s", m_ev)])
                    szs_guard = ("spsz", pr.mark("spsz"))
        m_a1a = pr.mark("spev")

        for j in range(NJ):
            pr.dma("sync", lambda j=j: bp(BA_WXB + j * 96, 96),
                   lambda j=j: wxpb[j * P:(j + 1) * P, :], "wlb")
        pr.dma("sync", lambda: bp(BA_WDT, NJ * P, 0, 64),
               lambda: wdtb[:, :], "wlb")

        uraw_guard = ("v", 0)
        for j in range(NJ):
            mload = pr.dma("sync", lambda: bp(BA_URAW + 4, T),
                           lambda j=j: uraw_scr[j * P:(j + 1) * P, :],
                           "url", waits=[("spev", m_a1a), uraw_guard])
            m_pc = []
            for pc2 in range(2):
                o = pc2 * 2048
                pr.emit("vector",
                        lambda e, j=j, o=o: e.tensor_scalar(
                            fp(FA_CVA, 2048), bp(BA_URAW + 1 + o, 2048),
                            c_sb[:, C_CW + j * 4:C_CW + j * 4 + 1], None,
                            ALU.mult),
                        waits=[("url", mload), ("s", pr.mark("s"))],
                        incs=[("v", 1)])
                pr.emit("vector",
                        lambda e, j=j, o=o: e.scalar_tensor_tensor(
                            fp(FA_CVB, 2048), bp(BA_URAW + 2 + o, 2048),
                            c_sb[:, C_CW + j * 4 + 1:C_CW + j * 4 + 2],
                            fp(FA_CVA, 2048), ALU.mult, ALU.add),
                        incs=[("v", 1)])
                pr.emit("vector",
                        lambda e, j=j, o=o: e.scalar_tensor_tensor(
                            fp(FA_CVA, 2048), bp(BA_URAW + 3 + o, 2048),
                            c_sb[:, C_CW + j * 4 + 2:C_CW + j * 4 + 3],
                            fp(FA_CVB, 2048), ALU.mult, ALU.add),
                        incs=[("v", 1)])
                pr.emit("vector",
                        lambda e, j=j, o=o: e.scalar_tensor_tensor(
                            fp(FA_CVB, 2048), bp(BA_URAW + 4 + o, 2048),
                            c_sb[:, C_CW + j * 4 + 3:C_CW + j * 4 + 4],
                            fp(FA_CVA, 2048), ALU.mult, ALU.add),
                        incs=[("v", 1)])
                m_conv = pr.mark("v")
                pr.emit("scalar",
                        lambda e, j=j, o=o: e.activation(
                            bp(BA_UBF + o, 2048), fp(FA_CVB, 2048), AF.Silu,
                            bias=c_sb[:, C_CB + j:C_CB + j + 1]),
                        waits=[("v", m_conv), ("mm", pr.mark("mm")),
                               ("spu", pr.mark("spu"))],
                        incs=[("s", 1)])
                m_pc.append(pr.mark("s"))
            uraw_guard = ("s", m_pc[-1])
            m_u = m_pc[-1]
            pr.dma("sync", lambda j=j: u_scr[j * P:(j + 1) * P, :],
                   lambda: bp(BA_UBF, T), "spu", waits=[("s", m_u)])
            for tsn in range(8):
                bank = 4 + (tsn % 2)
                w = [("s", m_u), bank_guard[bank]]
                pr.emit("tensor",
                        lambda e, j=j, tsn=tsn, b=bank: e.matmul(
                            ps[b][0:96, :], bp(BA_WXB + j * 96, 96),
                            bp(BA_UBF + tsn * 512, 512),
                            start=True, stop=True),
                        waits=w, incs=[("mm", 1)])
                m_mmx = pr.mark("mm")
                if j == 0:
                    pr.emit("vector",
                            lambda e, tsn=tsn, b=bank: e.tensor_copy(
                                fp(FA_XDB + tsn * 512, 512, 0, 96),
                                ps[b][0:96, :]),
                            waits=[("mm", m_mmx)], incs=[("v", 1)])
                else:
                    pr.emit("vector",
                            lambda e, tsn=tsn, b=bank: e.tensor_tensor(
                                fp(FA_XDB + tsn * 512, 512, 0, 96),
                                fp(FA_XDB + tsn * 512, 512, 0, 96),
                                ps[b][0:96, :], ALU.add),
                            waits=[("mm", m_mmx)], incs=[("v", 1)])
                bank_guard[bank] = ("v", pr.mark("v"))
        m_a1b = pr.mark("v")

        pr.emit("vector",
                lambda e: e.tensor_copy(bp(BA_DTS, T, 0, 32),
                                        fp(FA_XDB, T, 64, 32)),
                waits=[("v", m_a1b)], incs=[("v", 1)])
        m_bccast = pr.mark("v")
        pr.dma("sync", lambda: bc_scr[:, :],
               lambda: bp(BA_DTS, T, 0, 32), "spbc",
               waits=[("v", m_bccast)])
        m_bcw = pr.mark("spbc")
        pr.emit("vector",
                lambda e: e.tensor_copy(bp(BA_DTR, T, 0, 64),
                                        fp(FA_XDB, T, 0, 64)),
                waits=[("v", m_a1b)], incs=[("v", 1)])
        m_dtr = pr.mark("v")
        dts_guard = ("spbc", m_bcw)
        for j in range(NJ):
            for tsn in range(8):
                bank = tsn % 2
                pr.emit("tensor",
                        lambda e, j=j, tsn=tsn, b=bank: e.matmul(
                            ps[b][:, :], bp(BA_WDT + j * P, P, 0, 64),
                            bp(BA_DTR + tsn * 512, 512, 0, 64),
                            start=True, stop=True),
                        waits=[("v", m_dtr), bank_guard[bank]],
                        incs=[("mm", 1)])
                m_mmd = pr.mark("mm")
                col = tsn * 512
                pr.emit("scalar",
                        lambda e, b=bank, j=j: e.activation(
                            fp(FA_CVA, 512), ps[b][:, :], AF.Abs,
                            bias=c_sb[:, C_DTB + j:C_DTB + j + 1]),
                        waits=[("mm", m_mmd), ("v", pr.mark("v"))],
                        incs=[("s", 1)])
                pr.emit("scalar",
                        lambda e: e.activation(fp(FA_CVA, 512),
                                               fp(FA_CVA, 512), AF.Exp,
                                               scale=csP(C_NONE)),
                        incs=[("s", 1)])
                pr.emit("scalar",
                        lambda e: e.activation(fp(FA_CVA, 512),
                                               fp(FA_CVA, 512), AF.Ln,
                                               bias=1.0),
                        incs=[("s", 1)])
                pr.emit("scalar",
                        lambda e, b=bank, j=j: e.activation(
                            fp(FA_CVB, 512), ps[b][:, :], AF.Relu,
                            bias=c_sb[:, C_DTB + j:C_DTB + j + 1]),
                        incs=[("s", 1)])
                bank_guard[bank] = ("s", pr.mark("s"))
                m_sp4 = pr.mark("s")
                pr.emit("vector",
                        lambda e, col=col: e.tensor_tensor(
                            bp(BA_DTS + col, 512), fp(FA_CVA, 512),
                            fp(FA_CVB, 512), ALU.add),
                        waits=[("s", m_sp4), dts_guard], incs=[("v", 1)])
            m_dtj = pr.mark("v")
            pr.dma("sync", lambda j=j: dt_scr[j * P:(j + 1) * P, :],
                   lambda: bp(BA_DTS, T), "spdt", waits=[("v", m_dtj)])
            dts_guard = ("spdt", pr.mark("spdt"))

        m_spu = pr.mark("spu")
        m_spdt = pr.mark("spdt")
        m_spsz = pr.mark("spsz")

        u_marks = {}
        dt_marks = {}
        sz_marks = {}
        u_guard = [("v", 0)] * NJ
        dt_guard = [("s", 0)] * NJ
        sz_guard = [("v", 0)] * 2
        bb_guard = [("spbc", m_bcw)] * 2
        cb_guard = [("spbc", m_bcw)] * 2
        a_guard = [("v", 0)] * 4
        hc_guard = [("gp", 0)]
        ff_guard = [("o%d" % fi, 0) for fi in range(NF)]
        wp_guard = [("mm", 0)] * 4
        m_outproj = [0] * (NCH + 1)
        m_lnx1_dbg = [0] * (NCH + 1)
        m_lnx1_s = [0]
        m_y0 = [0]
        m_w = [0]
        unit_rr = [0]
        dbg_gate = [None, None]
        m_ymix = [0] * (NCH + 1)
        m_scan_end_v = [0] * (NCH + 1)
        m_scan_end_s = [0] * (NCH + 1)

        def u_off(j):
            return BB_U + j * 512

        def dt_off(j):
            return BB_DT + j * 512

        def emit_loads(g):
            gw = [dbg_gate[1]] if dbg_gate[1] else []
            dbg_gate[1] = None
            for j in range(NJ):
                u_marks[(g, j)] = pr.dma(
                    "sync", lambda j=j: bp(u_off(j), 512),
                    lambda g=g, j=j: u_scr[j * P:(j + 1) * P,
                                           g * TC:(g + 1) * TC],
                    "u%d" % j, waits=[u_guard[j]] + gw)
                dt_marks[(g, j)] = pr.dma(
                    "sync", lambda j=j: bp(dt_off(j), 512),
                    lambda g=g, j=j: dt_scr[j * P:(j + 1) * P,
                                            g * TC:(g + 1) * TC],
                    "d%d" % j, waits=[dt_guard[j]])
            for j in range(2):
                sz_marks[(g, j)] = pr.dma(
                    "sync", lambda j=j: bp(BB_SZ + (j % 2) * 512, 512),
                    lambda g=g, j=j: sz_scr[j * P:(j + 1) * P,
                                            g * TC:(g + 1) * TC],
                    "z%d" % (j % 2), waits=[sz_guard[j % 2]])

        def emit_scan_head(g):
            prev_ymix = m_ymix[g - 1] if g > 0 else 0
            for j in range(NJ):
                pr.emit("scalar",
                        lambda e, j=j: e.activation(
                            fp(FB_Y + j * 512, 512), bp(u_off(j), 512),
                            AF.Copy, scale=c_sb[:, C_D + j:C_D + j + 1]),
                        waits=[("u%d" % j, u_marks[(g, j)]),
                               ("v", prev_ymix)],
                        incs=[("s", 1)])
            m_y0[0] = pr.mark("s")
            for j in range(NJ):
                pr.emit("vector",
                        lambda e, j=j: e.tensor_tensor(
                            bp(u_off(j), 512), bp(u_off(j), 512),
                            bp(dt_off(j), 512), ALU.mult),
                        waits=[("u%d" % j, u_marks[(g, j)]),
                               ("d%d" % j, dt_marks[(g, j)]),
                               ("s", m_y0[0])],
                        incs=[("v", 1)])
            m_w[0] = pr.mark("v")
            if debug and g == 0:
                pr.dma("sync", lambda: dbg_st[:, :], lambda: state[:, :],
                       "dbgs", waits=[("v", m_w[0])])
                pr.dma("sync", lambda: dbg_an[:, :], lambda: aneg[:, :],
                       "dbgs")

        def emit_scan_q(g, q):
            qs = q % 2
            bbo = BB_BB + qs * 1024
            cbo = BB_CB + qs * 1024
            mq_b = pr.dma(
                "scalar", lambda o=bbo: bp(o, 1024),
                lambda g=g, q=q: _ap(
                    bc_scr, (q * 2) * T + g * TC,
                    [[0, P], [T, 2], [1, TC]]),
                "bq%d" % qs, waits=[bb_guard[qs]])
            mq_c = pr.dma(
                "scalar", lambda o=cbo: bp(o, 1024),
                lambda g=g, q=q: _ap(
                    bc_scr, (16 + q * 2) * T + g * TC,
                    [[0, P], [T, 2], [1, TC]]),
                "cq%d" % qs, waits=[cb_guard[qs]])
            for j in range(NJ):
                pr.emit("vector",
                        lambda e, j=j, bbo=bbo: e.tensor_tensor(
                            bp(BB_B, 1024),
                            _ap(bpool, u_off(j),
                                [[BW, P], [0, 2], [1, TC]]),
                            bp(bbo, 1024), ALU.mult),
                        waits=[("bq%d" % qs, mq_b), ("v", m_w[0])],
                        incs=[("v", 1)])
                for nn in range(2):
                    n = q * 2 + nn
                    asl = (j * 2 + nn) % 3
                    pr.emit("scalar",
                            lambda e, g=g, j=j, n=n, asl=asl:
                            e.activation(
                                fp(FB_A + asl * 512, 512),
                                bp(dt_off(j), 512),
                                AF.Exp,
                                scale=aneg[:, j * NST + n:
                                           j * NST + n + 1]),
                            waits=[("d%d" % j, dt_marks[(g, j)]),
                                   a_guard[asl]],
                            incs=[("s", 1)])
                    m_a = pr.mark("s")
                    pr.emit("vector",
                            lambda e, j=j, n=n, nn=nn, asl=asl:
                            e.tensor_tensor_scan(
                                bp(BB_H + nn * 512, 512),
                                fp(FB_A + asl * 512, 512),
                                bp(BB_B + nn * 512, 512),
                                state[:, j * NST + n:
                                      j * NST + n + 1],
                                ALU.mult, ALU.add),
                            waits=[("s", m_a)],
                            incs=[("v", 1)])
                    a_guard[asl] = ("v", pr.mark("v"))
                pr.emit("vector",
                        lambda e, cbo=cbo: e.tensor_tensor(
                            bp(BB_HC, 1024), bp(BB_H, 1024),
                            bp(cbo, 1024), ALU.mult),
                        waits=[("cq%d" % qs, mq_c), hc_guard[0]],
                        incs=[("v", 1)])
                m_hc = pr.mark("v")
                for nn in range(2):
                    n = q * 2 + nn
                    pr.emit("vector",
                            lambda e, j=j, n=n, nn=nn: e.tensor_copy(
                                state[:, j * NST + n:j * NST + n + 1],
                                bp(BB_H + nn * 512 + 511, 1)),
                            incs=[("v", 1)])
                if debug and g == 0 and q == 0 and j == 0:
                    pr.dma("sync", lambda: dbg_bb[:, 0:1024],
                           lambda: bp(BB_BB, 1024), "dbgs",
                           waits=[("v", m_hc)])
                    pr.dma("sync", lambda: dbg_bb[:, 1024:2048],
                           lambda: bp(BB_CB, 1024), "dbgs")
                    dbg_gate[0] = ("dbgs", pr.mark("dbgs"))
                for nn in range(2):
                    pr.emit("vector",
                            lambda e, j=j, nn=nn: e.tensor_tensor(
                                fp(FB_Y + j * 512, 512),
                                fp(FB_Y + j * 512, 512),
                                bp(BB_HC + nn * 512, 512),
                                ALU.add),
                            waits=[("s", m_y0[0])],
                            incs=[("v", 1)])
                hc_guard[0] = ("v", pr.mark("v"))
            bb_guard[qs] = ("v", pr.mark("v"))
            cb_guard[qs] = ("v", pr.mark("v"))

        def emit_scan_tail(g):
            m_adds = pr.mark("gp")
            for j in range(NJ):
                w = [("gp", m_adds), ("z%d" % (j % 2), sz_marks[(g, j)]),
                     ("mm", m_outproj[g - 1] if g > 0 else 0)]
                pr.emit("vector",
                        lambda e, g=g, j=j: e.tensor_tensor(
                            bp(BB_YM + j * 512, 512),
                            fp(FB_Y + j * 512, 512),
                            bp(BB_SZ + (j % 2) * 512, 512), ALU.mult),
                        waits=w, incs=[("v", 1)])
                sz_guard[j % 2] = ("v", pr.mark("v"))
                if j + 2 < NJ:
                    sz_marks[(g, j + 2)] = pr.dma(
                        "sync",
                        lambda j=j: bp(BB_SZ + (j % 2) * 512, 512),
                        lambda g=g, j=j: sz_scr[(j + 2) * P:(j + 3) * P,
                                                g * TC:(g + 1) * TC],
                        "z%d" % (j % 2), waits=[sz_guard[j % 2]])
            m_ymix[g] = pr.mark("v")
            for j in range(NJ):
                u_guard[j] = ("v", m_ymix[g])
                dt_guard[j] = ("s", pr.mark("s"))

        def wp_load(unit, blob, off_elems, kcnt, nkt, extra_waits=()):
            m = pr.dma(
                "sync", lambda u=unit, kcnt=kcnt: bp(BB_WP + u * 4096,
                                                     kcnt * 256),
                lambda blob=blob, o=off_elems, kcnt=kcnt, nkt=nkt: _ap(
                    blob, o, [[nkt * 256, P], [1, kcnt * 256]]),
                "w%d" % unit, waits=[wp_guard[unit]] + list(extra_waits))
            return m

        def emit_back_head(g):
            t0 = g * TC
            mb1 = pr.dma("sync", lambda: fp(FB_BC1, 512),
                         lambda g=g: _ap(st_scr, t0, [[0, P], [1, TC]]),
                         "bc1", waits=[("v", pr.mark("v")),
                                       ("spst", m_stats_done)])
            rb1 = pr.dma("sync", lambda: fp(FB_BC1 + 512, 512),
                         lambda g=g: _ap(st_scr, T + t0, [[0, P], [1, TC]]),
                         "bc1")
            for blk in range(4):
                unit = unit_rr[0] % 4
                unit_rr[0] += 1
                mw = wp_load(unit, op16, blk * P * 16 * 256, NJ, 16)
                for mt in range(2):
                    fi = blk * 2 + mt
                    bank = fi % 4
                    for j in range(NJ):
                        w = []
                        if j == 0:
                            w = [("w%d" % unit, mw), ("v", m_ymix[g]),
                                 bank_guard[bank]]
                        pr.emit("tensor",
                                lambda e, mt=mt, j=j, b=bank, u=unit:
                                e.matmul(
                                    ps[b][:, :],
                                    bp(BB_WP + u * 4096 + j * 256 + mt * P,
                                       P),
                                    bp(BB_YM + j * 512, 512),
                                    start=(j == 0), stop=(j == NJ - 1)),
                                waits=w, incs=[("mm", 1)])
                    m_op = pr.mark("mm")
                    xsl = 0
                    mxf = pr.dma(
                        "sync", lambda xsl=xsl: fp(FB_XF + xsl * 512, 512),
                        lambda g=g, fi=fi: xT[fi * P:(fi + 1) * P,
                                              t0:t0 + TC],
                        "x%d" % xsl, waits=[("v", pr.mark("v"))] if fi >= 1
                        else [("v", m_ymix[g])])
                    pr.emit("vector",
                            lambda e, xsl=xsl: e.tensor_tensor(
                                fp(FB_TS, 512), fp(FB_XF + xsl * 512, 512),
                                fp(FB_BC1, 512), ALU.subtract),
                            waits=[("x%d" % xsl, mxf), ("bc1", rb1),
                                   ("mm", m_op), ("s", pr.mark("s"))],
                            incs=[("v", 1)])
                    pr.emit("vector",
                            lambda e: e.tensor_tensor(
                                fp(FB_TS, 512), fp(FB_TS, 512),
                                fp(FB_BC1 + 512, 512), ALU.mult),
                            incs=[("v", 1)])
                    pr.emit("vector",
                            lambda e, xsl=xsl: e.tensor_tensor(
                                fp(FB_TS, 512), fp(FB_TS, 512),
                                fp(FB_XF + xsl * 512, 512), ALU.add),
                            incs=[("v", 1)])
                    pr.emit("vector",
                            lambda e, fi=fi, b=bank: e.tensor_tensor(
                                fp(FB_X1 + fi * 512, 512), fp(FB_TS, 512),
                                ps[b][:, :], ALU.add),
                            incs=[("v", 1)])
                    bank_guard[bank] = ("v", pr.mark("v"))
                wp_guard[unit] = ("mm", m_op)
            m_outproj[g] = pr.mark("mm")
            m_x1 = pr.mark("v")

            for fi in range(NF):
                pr.emit("scalar",
                        lambda e, fi=fi: e.activation(
                            fp(FB_TS, 512), fp(FB_X1 + fi * 512, 512),
                            AF.Square),
                        waits=[("v", m_x1), ("mm", pr.mark("mm"))],
                        incs=[("s", 1)])
                msq = pr.mark("s")
                w6 = [("v", m_x1)] + ([bank_guard[6]] if fi == 0 else [])
                pr.emit("tensor",
                        lambda e, fi=fi: e.matmul(
                            ps[6][0:1, :], c_sb[:, C_ONE:C_ONE + 1],
                            fp(FB_X1 + fi * 512, 512),
                            start=(fi == 0), stop=(fi == NF - 1)),
                        waits=w6, incs=[("mm", 1)])
                w7 = [("s", msq)] + ([bank_guard[7]] if fi == 0 else [])
                pr.emit("tensor",
                        lambda e, fi=fi: e.matmul(
                            ps[7][0:1, :], c_sb[:, C_ONE:C_ONE + 1],
                            fp(FB_TS, 512),
                            start=(fi == 0), stop=(fi == NF - 1)),
                        waits=w7, incs=[("mm", 1)])
            m_st2 = pr.mark("mm")
            m2row = fp(FB_SR, 512, 0, 1)
            r2row = fp(FB_SR + 512, 512, 0, 1)
            pr.emit("scalar",
                    lambda e: e.activation(m2row, ps[6][0:1, :], AF.Copy,
                                           scale=cs0(C_RDM)),
                    waits=[("mm", m_st2), ("st2", pr.mark("st2"))],
                    incs=[("s", 1)])
            pr.emit("scalar",
                    lambda e: e.activation(r2row, ps[7][0:1, :], AF.Copy,
                                           scale=cs0(C_RDM)),
                    incs=[("s", 1)])
            bank_guard[6] = ("s", pr.mark("s"))
            bank_guard[7] = ("s", pr.mark("s"))
            pr.emit("scalar",
                    lambda e: e.activation(fp(FB_TS, 512, 0, 1), m2row,
                                           AF.Square),
                    incs=[("s", 1)])
            m_s2 = pr.mark("s")
            pr.emit("vector",
                    lambda e: e.tensor_tensor(r2row, r2row,
                                              fp(FB_TS, 512, 0, 1),
                                              ALU.subtract),
                    waits=[("s", m_s2)], incs=[("v", 1)])
            m_v2 = pr.mark("v")
            pr.emit("scalar",
                    lambda e: e.activation(r2row, r2row, AF.Ln,
                                           bias=cs0(C_EPS)),
                    waits=[("v", m_v2)], incs=[("s", 1)])
            pr.emit("scalar",
                    lambda e: e.activation(r2row, r2row, AF.Exp,
                                           scale=cs0(C_NHALF)),
                    incs=[("s", 1)])
            m_rr2 = pr.mark("s")
            pr.emit("tensor",
                    lambda e: e.matmul(ps[6][:, :],
                                       c_sb[0:1, C_OROW:C_OROW + P],
                                       m2row, start=True, stop=True),
                    waits=[("s", m_rr2), bank_guard[6]], incs=[("mm", 1)])
            pr.emit("tensor",
                    lambda e: e.matmul(ps[7][:, :],
                                       c_sb[0:1, C_OROW:C_OROW + P],
                                       r2row, start=True, stop=True),
                    waits=[("s", m_rr2), bank_guard[7]], incs=[("mm", 1)])
            m_bc2 = pr.mark("mm")
            for fi in range(NF):
                pr.emit("vector",
                        lambda e, fi=fi: e.tensor_tensor(
                            fp(FB_TS, 512), fp(FB_X1 + fi * 512, 512),
                            ps[6][:, :], ALU.subtract),
                        waits=[("mm", m_bc2)], incs=[("v", 1)])
                pr.emit("vector",
                        lambda e, fi=fi: e.tensor_tensor(
                            bp(BB_LN1 + fi * 512, 512), fp(FB_TS, 512),
                            ps[7][:, :], ALU.mult),
                        incs=[("v", 1)])
            bank_guard[6] = ("v", pr.mark("v"))
            bank_guard[7] = ("v", pr.mark("v"))
            m_lnx1_s[0] = pr.mark("v")
            m_lnx1_dbg[g] = m_lnx1_s[0]


        def gen_back_ffn(g):
            t0 = g * TC
            m_lnx1 = m_lnx1_s[0]
            for hf in range(2):
                hoff = hf * 256
                for blk in range(16):
                    unit = unit_rr[0] % 4
                    unit_rr[0] += 1
                    mw1 = pr.dma(
                        "sync", lambda u=unit: bp(BB_WP + u * 4096,
                                                  NF * 256),
                        lambda blk=blk: _ap(w1b, blk * P * 8 * 256,
                                            [[8 * 256, P], [1, NF * 256]]),
                        "w%d" % unit, waits=[wp_guard[unit]])
                    mw2 = pr.dma(
                        "sync", lambda u=unit: bp(BB_WP + u * 4096 +
                                                  NF * 256, NF * 256),
                        lambda blk=blk: _ap(w2b, blk * P * 8 * 256,
                                            [[8 * 256, P], [1, NF * 256]]),
                        "w%d" % unit)
                    for mt in range(2):
                        ht = blk * 2 + mt
                        b1 = (ht % 3) * 2
                        b2 = b1 + 1
                        for k in range(NF):
                            w = []
                            if k == 0:
                                w = [("w%d" % unit, mw2), ("v", m_lnx1),
                                     bank_guard[b1]]
                            pr.emit("tensor",
                                    lambda e, k=k, mt=mt, b=b1, u=unit,
                                    hoff=hoff: e.matmul(
                                        ps[b][:, 0:256],
                                        bp(BB_WP + u * 4096 + k * 256 +
                                           mt * P, P),
                                        bp(BB_LN1 + k * 512 + hoff, 256),
                                        start=(k == 0), stop=(k == NF - 1)),
                                    waits=w, incs=[("mm", 1)])
                        for k in range(NF):
                            w = []
                            if k == 0:
                                w = [("w%d" % unit, mw2), bank_guard[b2]]
                            pr.emit("tensor",
                                    lambda e, k=k, mt=mt, b=b2, u=unit,
                                    hoff=hoff: e.matmul(
                                        ps[b][:, 0:256],
                                        bp(BB_WP + u * 4096 + NF * 256 +
                                           k * 256 + mt * P, P),
                                        bp(BB_LN1 + k * 512 + hoff, 256),
                                        start=(k == 0), stop=(k == NF - 1)),
                                    waits=w, incs=[("mm", 1)])
                        m_a12 = pr.mark("mm")
                        pr.emit("scalar",
                                lambda e, ht=ht, b=b1: e.activation(
                                    fp(FB_TS2, 256), ps[b][:, 0:256],
                                    AF.Tanh, scale=hs1[:, ht:ht + 1]),
                                waits=[("mm", m_a12), ("v", pr.mark("v"))],
                                incs=[("s", 1)])
                        m_t = pr.mark("s")
                        pr.emit("vector",
                                lambda e, b=b1: e.scalar_tensor_tensor(
                                    fp(FB_TS2, 256), fp(FB_TS2, 256), 1.0,
                                    ps[b][:, 0:256], ALU.add, ALU.mult),
                                waits=[("s", m_t)], incs=[("v", 1)])
                        bank_guard[b1] = ("v", pr.mark("v"))
                        pr.emit("vector",
                                lambda e, ht=ht, b=b2:
                                e.scalar_tensor_tensor(
                                    bp(BB_G + ht * 256, 256),
                                    fp(FB_TS2, 256),
                                    k12[:, ht:ht + 1], ps[b][:, 0:256],
                                    ALU.mult, ALU.mult),
                                incs=[("v", 1)])
                        bank_guard[b2] = ("v", pr.mark("v"))
                    wp_guard[unit] = ("mm", pr.mark("mm"))
                    yield
                m_g = pr.mark("v")

                def h_blocks(blob, src_off, dst_off, sc, m_in, m_in_sem):
                    for blk in range(16):
                        ua = unit_rr[0] % 4
                        unit_rr[0] += 1
                        ub = unit_rr[0] % 4
                        unit_rr[0] += 1
                        mwa = wp_load(ua, blob, blk * P * 32 * 256, 16, 32)
                        mwb = wp_load(ub, blob,
                                      blk * P * 32 * 256 + 16 * 256, 16, 32)
                        for mt in range(2):
                            ht = blk * 2 + mt
                            bank = ht % 6
                            for k in range(NH):
                                unit = ua if k < 16 else ub
                                kk = k % 16
                                w = []
                                if k == 0:
                                    w = [("w%d" % ua, mwa),
                                         ("w%d" % ub, mwb),
                                         (m_in_sem, m_in),
                                         bank_guard[bank]]
                                pr.emit(
                                    "tensor",
                                    lambda e, k=k, kk=kk, mt=mt, b=bank,
                                    u=unit, so=src_off: e.matmul(
                                        ps[b][:, 0:256],
                                        bp(BB_WP + u * 4096 + kk * 256 +
                                           mt * P, P),
                                        bp(so + k * 256, 256),
                                        start=(k == 0), stop=(k == NH - 1)),
                                    waits=w, incs=[("mm", 1)])
                            m_h = pr.mark("mm")
                            pr.emit("scalar",
                                    lambda e, ht=ht, b=bank, do=dst_off,
                                    sc=sc: e.activation(
                                        bp(do + ht * 256, 256),
                                        ps[b][:, 0:256], AF.Copy,
                                        scale=c_sb[:, sc + ht:sc + ht + 1]),
                                    waits=[("mm", m_h)], incs=[("s", 1)])
                            bank_guard[bank] = ("s", pr.mark("s"))
                        wp_guard[ua] = ("mm", pr.mark("mm"))
                        wp_guard[ub] = ("mm", pr.mark("mm"))
                        yield
                    pr.emit("scalar",
                            lambda e, do=dst_off: e.activation(
                                bp(do, NH * 256), bp(do, NH * 256),
                                AF.Gelu),
                            waits=[("s", pr.mark("s"))], incs=[("s", 1)])

                yield from h_blocks(h1b, BB_G + 0, BB_P, C_SH1, m_g, "v")
                m_p1 = pr.mark("s")
                yield from h_blocks(h2b, BB_P, BB_G, C_SH2, m_p1, "s")
                m_p2 = pr.mark("s")

                for blk in range(4):
                    ua = unit_rr[0] % 4
                    unit_rr[0] += 1
                    ub = unit_rr[0] % 4
                    unit_rr[0] += 1
                    mwa = wp_load(ua, w3b, blk * P * 32 * 256, 16, 32)
                    mwb = wp_load(ub, w3b, blk * P * 32 * 256 + 16 * 256,
                                  16, 32)
                    for mt in range(2):
                        fi = blk * 2 + mt
                        bank = 4 + (fi % 2)
                        for k in range(NH):
                            unit = ua if k < 16 else ub
                            kk = k % 16
                            w = []
                            if k == 0:
                                w = [("w%d" % ua, mwa), ("w%d" % ub, mwb),
                                     ("s", m_p2),
                                     bank_guard[bank]]
                            pr.emit("tensor",
                                    lambda e, k=k, kk=kk, mt=mt, b=bank,
                                    u=unit: e.matmul(
                                        ps[b][:, 0:256],
                                        bp(BB_WP + u * 4096 + kk * 256 +
                                           mt * P, P),
                                        bp(BB_G + k * 256, 256),
                                        start=(k == 0), stop=(k == NH - 1)),
                                    waits=w, incs=[("mm", 1)])
                        m_w3 = pr.mark("mm")
                        pr.emit("scalar",
                                lambda e, fi=fi, b=bank: e.activation(
                                    fp(FB_FF + fi * 256, 256),
                                    ps[b][:, 0:256], AF.Copy,
                                    scale=c_sb[:, C_S3 + fi:C_S3 + fi + 1]),
                                waits=[("mm", m_w3), ff_guard[fi]],
                                incs=[("s", 1)])
                        bank_guard[bank] = ("s", pr.mark("s"))
                        m_ev = pr.mark("s")
                        pr.emit("vector",
                                lambda e, fi=fi, hoff=hoff: e.tensor_tensor(
                                    fp(FB_FF + fi * 256, 256),
                                    fp(FB_FF + fi * 256, 256),
                                    fp(FB_X1 + fi * 512 + hoff, 256),
                                    ALU.add),
                                waits=[("s", m_ev)], incs=[("v", 1)])
                        m_out = pr.mark("v")
                        pr.dma("sync",
                               lambda g=g, fi=fi, hoff=hoff:
                               y_out[fi * P:(fi + 1) * P,
                                     t0 + hoff:t0 + hoff + 256],
                               lambda fi=fi: fp(FB_FF + fi * 256, 256),
                               "o%d" % fi, waits=[("v", m_out)])
                        ff_guard[fi] = ("o%d" % fi, pr.mark("o%d" % fi))
                    wp_guard[ua] = ("mm", pr.mark("mm"))
                    wp_guard[ub] = ("mm", pr.mark("mm"))
                    yield

        for j in range(NJ):
            u_guard[j] = ("spu", m_spu)
            dt_guard[j] = ("spdt", m_spdt)
        for j in range(2):
            sz_guard[j] = ("spsz", m_spsz)
        for u in range(4):
            wp_guard[u] = ("mm", 0)
        pr.emit("sync", lambda e: None, waits=list(m_a3_join))
        emit_loads(0)
        for s in range(NCH + 1):
            ffn = None
            if s >= 1:
                emit_back_head(s - 1)
                ffn = gen_back_ffn(s - 1)
            if s < NCH:
                emit_scan_head(s)
                for q in range(8):
                    emit_scan_q(s, q)
                    if ffn is not None:
                        for _ in range(14):
                            if next(ffn, "done") == "done":
                                ffn = None
                                break
                emit_scan_tail(s)
                if s + 1 < NCH:
                    emit_loads(s + 1)
                if debug and s == 0:
                    pr.dma("sync", lambda: dbg_y[:, :],
                           lambda: fp(FB_Y, NJ * 512),
                           "dbgs", waits=[("v", m_ymix[0])])
                    pr.dma("sync", lambda: dbg_ym[:, :],
                           lambda: bp(BB_YM, NJ * 512), "dbgs",
                           waits=[("v", m_ymix[0])])
            if ffn is not None:
                for _ in iter(lambda: next(ffn, "done"), "done"):
                    pass
            if debug and s == 1:
                pr.dma("sync", lambda: dbg_x1[:, :],
                       lambda: fp(FB_X1, NF * 512), "dbgs",
                       waits=[("v", m_lnx1_dbg[0])])
                pr.dma("sync", lambda: dbg_ln1[:, :],
                       lambda: bp(BB_LN1, NF * 512), "dbgs",
                       waits=[("v", m_lnx1_dbg[0])])

        final_waits = [("o%d" % fi, pr.mark("o%d" % fi))
                       for fi in range(NF)]
        if debug:
            final_waits.append(("dbgs", pr.mark("dbgs")))
        pr.emit("sync", lambda e: e.nop(), waits=final_waits)

        semh = {}
        with ExitStack() as semctx:
            for name in pr.semnames:
                semh[name] = semctx.enter_context(nc.semaphore(name))
            with nc.Block() as block:
                def runner(stream):
                    def run(eng_obj):
                        for waits, fn, incs in stream:
                            for s, val in waits:
                                eng_obj.wait_ge(semh[s], val)
                            instr = fn(eng_obj)
                            for s, d in incs:
                                instr.then_inc(semh[s], d)
                    return run

                block.sync(runner(pr.streams["sync"]))
                block.vector(runner(pr.streams["vector"]))
                block.scalar(runner(pr.streams["scalar"]))
                block.tensor(runner(pr.streams["tensor"]))
                block.gpsimd(runner(pr.streams["gpsimd"]))

    return nc



def _pack_cvec(conv_w, conv_b, dtb, A_log, D, nw, s1, s2, sh1, sh2, s3):
    c = np.zeros((P, 652), dtype=np.float32)
    c[:, 0:64] = conv_w.reshape(16, 128, 4).transpose(1, 0, 2).reshape(128, 64)
    c[:, 64:80] = conv_b.reshape(16, 128).T
    c[:, 80:96] = dtb.reshape(16, 128).T
    c[:, 96:352] = A_log.reshape(16, 128, 16).transpose(1, 0, 2).reshape(
        128, 256)
    c[:, 352:368] = D.reshape(16, 128).T
    c[:, 368:376] = nw.reshape(8, 128).T
    c[:, 376:408] = s1.reshape(32, 128).T
    c[:, 408:440] = s2.reshape(32, 128).T
    c[:, 440:472] = sh1.reshape(32, 128).T
    c[:, 472:504] = sh2.reshape(32, 128).T
    c[:, 504:512] = s3.reshape(8, 128).T
    c[:, 512] = 1.0
    c[:, 513] = EPS
    c[:, 514] = 1.0 + EPS
    c[:, 515] = EPS * EPS
    c[:, 516] = -0.5
    c[:, 517] = -1.0
    c[:, 518] = 1.0 / DM
    c[0, 520:648] = 1.0
    return c


def _blob_win(w):
    return np.ascontiguousarray(
        w.reshape(NF, P, 32, P).transpose(2, 1, 0, 3)).reshape(-1)


def _blob(wt, nblk):
    K, M = wt.shape
    assert M == nblk * 256
    nkt = K // P
    return np.ascontiguousarray(
        wt.reshape(nkt, P, nblk, 256).transpose(2, 1, 0, 3)).reshape(-1)


_NC_CACHE = {}


def kernel(**inp):
    f32 = np.float32
    x = np.ascontiguousarray(np.asarray(inp["L0_attn_x"], dtype=f32))
    B = x.shape[0]
    asnp = lambda k: np.ascontiguousarray(np.asarray(inp[k], dtype=f32))

    shared = dict(
        winp=_blob_win(asnp("L0_attn_in_proj_w")),
        wxp=asnp("L0_attn_x_proj_w"),
        wdt=asnp("L0_attn_dt_proj_w"),
        wout=_blob(asnp("L0_attn_out_proj_w"), 4),
        w1t=_blob(asnp("L0_ffn_w1_w").T, 16),
        w2t=_blob(asnp("L0_ffn_w2_w").T, 16),
        h1t=_blob(asnp("L0_ffn_h1_w").T, 16),
        h2t=_blob(asnp("L0_ffn_h2_w").T, 16),
        w3t=_blob(asnp("L0_ffn_w3_w").T, 4),
        cvec=_pack_cvec(
            asnp("L0_attn_conv_w"), asnp("L0_attn_conv_b"),
            asnp("L0_attn_dt_proj_b"), asnp("L0_attn_A_log"),
            asnp("L0_attn_D"), asnp("L0_attn_norm_w"),
            asnp("L0_ffn_w1_s"), asnp("L0_ffn_w2_s"),
            asnp("L0_ffn_h1_s"), asnp("L0_ffn_h2_s"),
            asnp("L0_ffn_w3_s")),
    )

    if "nc" not in _NC_CACHE:
        _NC_CACHE["nc"] = build_program()
    nc = _NC_CACHE["nc"]

    in_maps = []
    for b in range(B):
        m = dict(shared)
        m["xT"] = np.ascontiguousarray(x[b].T)
        in_maps.append(m)

    res = run_bass_kernel_spmd(nc, in_maps, core_ids=list(range(B)))
    out = np.stack([np.asarray(res.results[b]["y_out"]).T for b in range(B)])
    return out.astype(np.float32)


# version 24
# speedup vs baseline: 1.1653x; 1.0065x over previous
import numpy as np
from contextlib import ExitStack

import concourse.bass as bass
import concourse.mybir as mybir
from concourse.bass_utils import run_bass_kernel_spmd

F32 = mybir.dt.float32
BF16 = mybir.dt.bfloat16
AF = mybir.ActivationFunctionType
ALU = mybir.AluOpType

P = 128
T = 4096
DM = 1024
DI = 2048
DH = 4096
NST = 16
RNK = 64
NF = DM // P
NJ = DI // P
NH = DH // P
TC = 512
NCH = T // TC
TS = 256
EPS = 1e-5

ENGINES = ("sync", "vector", "scalar", "tensor", "gpsimd")

FB_Y = 0
FB_A = FB_Y + NJ * 512
FB_XF = FB_A + 3 * 512
FB_X1 = FB_XF + 1 * 512
FB_TS = FB_X1 + NF * 512
FB_TS2 = FB_TS
FB_BC1 = FB_TS + 512
FB_FF = FB_BC1 + 2 * 512
FB_SR = FB_FF + NF * 256
FB_END = FB_SR + 2 * 512
FA_URAW = 0
FA_CVA = FA_URAW + 520
FA_CVB = FA_CVA + 2048
FA_XDB = FA_CVB + 2048
FA_XS = FA_XDB + 4096
FA_WC = FA_XS + 1024
FA_SR = FA_WC + 1024
FA_END = FA_SR + 3 * 512
FW = max(FB_END, FA_END)

BB_U = 0
BB_DT = BB_U + NJ * 512
BB_G = BB_DT + NJ * 512
BB_P = BB_G + NH * 256
BB_YM = BB_P + NH * 256
BB_LN1 = BB_YM + NJ * 512
BB_SZ = BB_LN1 + NF * 512
BB_BB = BB_SZ + 2 * 512
BB_CB = BB_BB + 2 * 1024
BB_B = BB_CB + 2 * 1024
BB_H = BB_B + 1024
BB_HC = BB_H + 1024
BB_WP = BB_HC + 1024
BB_END = BB_WP + 4 * 4096
BA_LNM = 0
BA_WINB = BA_LNM + NF * 2048
BA_UBF = BA_WINB + 2 * 1024
BA_SZS = BA_UBF + 4096
BA_WXB = BA_SZS + 2048
BA_DTR = BA_WXB + NJ * 96
BA_WDT = BA_DTR + 4096
BA_DTS = BA_WDT + NJ * 128
BA_WC = BB_WP
BA_URAW = BA_DTS + 4096
BA_EV = BA_URAW + 4100
BA_END = BA_EV + 2048
BW = max(BB_END, BA_END)

assert FW * 4 + BW * 2 <= 216200, (FW * 4, BW * 2)


ENGINE_SEM = {"vector": "v", "scalar": "s", "tensor": "mm", "gpsimd": "gp"}


class Prog:
    def __init__(self, selfwait=False):
        self.streams = {e: [] for e in ENGINES}
        self.counts = {}
        self.semnames = []
        self.selfwait = selfwait

    def sem(self, name):
        if name not in self.counts:
            self.counts[name] = 0
            self.semnames.append(name)

    def mark(self, name):
        self.sem(name)
        return self.counts[name]

    def emit(self, eng, fn, waits=(), incs=()):
        if self.selfwait and eng in ENGINE_SEM:
            es = ENGINE_SEM[eng]
            self.sem(es)
            waits = tuple(waits) + ((es, self.counts[es]),)
        waits = tuple((s, v) for (s, v) in waits if v > 0)
        for s, v in waits:
            self.sem(s)
        for s, d in incs:
            self.sem(s)
            self.counts[s] += d
        self.streams[eng].append((waits, fn, tuple(incs)))

    def dma(self, eng, dst_fn, src_fn, sem, waits=()):
        self.emit(eng, lambda e, d=dst_fn, s=src_fn: e.dma_start(d(), s()),
                  waits=waits, incs=[(sem, 16)])
        return self.mark(sem)


def _ap(t, off, dims):
    return bass.AP(t, off, [list(d) for d in dims])


def build_program(selfwait=False, detect=False, debug=False):
    nc = bass.Bass(target_bir_lowering=False, dynamic_dma_scratch_size=8192,
                   detect_race_conditions=detect)

    xT = nc.dram_tensor("xT", [DM, T], F32, kind="ExternalInput")
    winp = nc.dram_tensor("winp", [32 * P * 1024], F32,
                          kind="ExternalInput")
    wxp = nc.dram_tensor("wxp", [DI, 96], F32, kind="ExternalInput")
    wdt = nc.dram_tensor("wdt", [RNK, DI], F32, kind="ExternalInput")
    wout = nc.dram_tensor("wout", [4 * P * 16 * 256], F32,
                          kind="ExternalInput")
    w1t = nc.dram_tensor("w1t", [16 * P * 8 * 256], F32,
                         kind="ExternalInput")
    w2t = nc.dram_tensor("w2t", [16 * P * 8 * 256], F32,
                         kind="ExternalInput")
    h1t = nc.dram_tensor("h1t", [16 * P * 32 * 256], F32,
                         kind="ExternalInput")
    h2t = nc.dram_tensor("h2t", [16 * P * 32 * 256], F32,
                         kind="ExternalInput")
    w3t = nc.dram_tensor("w3t", [4 * P * 32 * 256], F32,
                         kind="ExternalInput")
    cvec = nc.dram_tensor("cvec", [P, 652], F32, kind="ExternalInput")

    y_out = nc.dram_tensor("y_out", [DM, T], F32, kind="ExternalOutput")

    uraw_scr = nc.dram_tensor("uraw_scr", [DI, T], BF16, kind="Internal")
    u_scr = nc.dram_tensor("u_scr", [DI, T], BF16, kind="Internal")
    dt_scr = nc.dram_tensor("dt_scr", [DI, T], BF16, kind="Internal")
    sz_scr = nc.dram_tensor("sz_scr", [DI, T], BF16, kind="Internal")
    bc_scr = nc.dram_tensor("bc_scr", [32, T], BF16, kind="Internal")
    st_scr = nc.dram_tensor("st_scr", [8, T], F32, kind="Internal")
    winb = nc.dram_tensor("winb", [32 * P * 1024], BF16, kind="Internal")
    wxpb = nc.dram_tensor("wxpb", [DI, 96], BF16, kind="Internal")
    wdtb = nc.dram_tensor("wdtb", [RNK, DI], BF16, kind="Internal")
    op16 = nc.dram_tensor("op16", [4 * DI * 256], BF16, kind="Internal")
    w1b = nc.dram_tensor("w1b", [16 * DM * 256], BF16, kind="Internal")
    w2b = nc.dram_tensor("w2b", [16 * DM * 256], BF16, kind="Internal")
    h1b = nc.dram_tensor("h1b", [16 * DH * 256], BF16, kind="Internal")
    h2b = nc.dram_tensor("h2b", [16 * DH * 256], BF16, kind="Internal")
    w3b = nc.dram_tensor("w3b", [4 * DH * 256], BF16, kind="Internal")
    if debug:
        dbg_y = nc.dram_tensor("dbg_y", [P, NJ * 512], F32,
                               kind="ExternalOutput")
        dbg_ym = nc.dram_tensor("dbg_ym", [P, NJ * 512], BF16,
                                kind="ExternalOutput")
        dbg_x1 = nc.dram_tensor("dbg_x1", [P, NF * 512], F32,
                                kind="ExternalOutput")
        dbg_ln1 = nc.dram_tensor("dbg_ln1", [P, NF * 512], BF16,
                                 kind="ExternalOutput")
        dbg_u = nc.dram_tensor("dbg_u", [P, NJ * 512], BF16,
                               kind="ExternalOutput")
        dbg_dt = nc.dram_tensor("dbg_dt", [P, NJ * 512], BF16,
                                kind="ExternalOutput")
        dbg_bb = nc.dram_tensor("dbg_bb", [P, 2048], BF16,
                                kind="ExternalOutput")
        dbg_bh = nc.dram_tensor("dbg_bh", [P, 2048], F32,
                                kind="ExternalOutput")
        dbg_an = nc.dram_tensor("dbg_an", [P, 256], F32,
                                kind="ExternalOutput")
        dbg_a = nc.dram_tensor("dbg_a", [P, 1024], F32,
                               kind="ExternalOutput")
        dbg_st = nc.dram_tensor("dbg_st", [P, 256], F32,
                                kind="ExternalOutput")

    pr = Prog(selfwait=selfwait)
    for name in ("v", "s", "mm", "gp"):
        pr.sem(name)

    with ExitStack() as ctx:
        c_sb = ctx.enter_context(nc.sbuf_tensor("c_sb", [P, 652], F32))
        aneg = ctx.enter_context(nc.sbuf_tensor("aneg", [P, NJ * NST], F32))
        hs1 = ctx.enter_context(nc.sbuf_tensor("hs1", [P, NH], F32))
        k12 = ctx.enter_context(nc.sbuf_tensor("k12", [P, NH], F32))
        state = ctx.enter_context(nc.sbuf_tensor("state", [P, NJ * NST], F32))
        fpool = ctx.enter_context(nc.sbuf_tensor("fpool", [P, FW], F32))
        bpool = ctx.enter_context(nc.sbuf_tensor("bpool", [P, BW], BF16))
        ps = [ctx.enter_context(nc.psum_tensor(f"ps{i}", [P, 512], F32))
              for i in range(8)]

        bank_guard = [("v", 0)] * 8

        C_CW, C_CB, C_DTB, C_AL = 0, 64, 80, 96
        C_D, C_NW, C_S1, C_S2 = 352, 368, 376, 408
        C_SH1, C_SH2, C_S3, C_ONE = 440, 472, 504, 512
        C_EPS, C_1PE, C_EPS2, C_NHALF, C_NONE, C_RDM = 513, 514, 515, 516, 517, 518
        C_OROW = 520

        def fp(off, w, p0=0, pn=P):
            return fpool[p0:p0 + pn, off:off + w]

        def bp(off, w, p0=0, pn=P):
            return bpool[p0:p0 + pn, off:off + w]

        def cs0(col):
            return c_sb[0:1, col:col + 1]

        def csP(col):
            return c_sb[:, col:col + 1]

        m_cv = pr.dma("sync", lambda: c_sb[:, :], lambda: cvec[:, :], "cvld")
        pr.emit("vector", lambda e: e.memset(state[:, :], 0.0),
                incs=[("v", 1)])
        pr.emit("vector", lambda e: e.memset(bp(BA_URAW, 4), 0.0),
                incs=[("v", 1)])
        pr.emit("scalar",
                lambda e: e.activation(aneg[:, :], c_sb[:, C_AL:C_AL + 256],
                                       AF.Exp),
                waits=[("cvld", m_cv)], incs=[("s", 1)])
        pr.emit("vector",
                lambda e: e.tensor_scalar(aneg[:, :], aneg[:, :], -1.0, None,
                                          ALU.mult),
                waits=[("s", pr.mark("s"))], incs=[("v", 1)])
        pr.emit("vector",
                lambda e: e.tensor_scalar(hs1[:, :], c_sb[:, C_S1:C_S1 + NH],
                                          0.5, None, ALU.mult),
                waits=[("cvld", m_cv)], incs=[("v", 1)])
        pr.emit("vector",
                lambda e: e.scalar_tensor_tensor(
                    k12[:, :], c_sb[:, C_S1:C_S1 + NH], 0.5,
                    c_sb[:, C_S2:C_S2 + NH], ALU.mult, ALU.mult),
                waits=[("cvld", m_cv)], incs=[("v", 1)])

        sqs = fp(FA_WC, 512)
        xg = [("mm", 0), ("mm", 0)]
        for piece in range(8):
            t0 = piece * 512
            for fi in range(NF):
                sl = fi % 2
                buf = fp(FA_XS + sl * 512, 512)
                mload = pr.dma(
                    "sync", lambda b=buf: b,
                    lambda fi=fi, t0=t0: xT[fi * P:(fi + 1) * P, t0:t0 + 512],
                    "xa%d" % sl, waits=[xg[sl]])
                pr.emit("scalar",
                        lambda e, b=buf: e.activation(sqs, b, AF.Square),
                        waits=[("xa%d" % sl, mload), ("mm", pr.mark("mm"))],
                        incs=[("s", 1)])
                msq = pr.mark("s")
                w0 = [("xa%d" % sl, mload)] + (
                    [bank_guard[0], ("cvld", m_cv)] if fi == 0 else [])
                pr.emit("tensor",
                        lambda e, b=buf, fi=fi: e.matmul(
                            ps[0][0:1, :], c_sb[:, C_ONE:C_ONE + 1], b,
                            start=(fi == 0), stop=(fi == NF - 1)),
                        waits=w0, incs=[("mm", 1)])
                w1 = [("s", msq)] + (
                    [bank_guard[1], ("cvld", m_cv)] if fi == 0 else [])
                pr.emit("tensor",
                        lambda e, fi=fi: e.matmul(
                            ps[1][0:1, :], c_sb[:, C_ONE:C_ONE + 1], sqs,
                            start=(fi == 0), stop=(fi == NF - 1)),
                        waits=w1, incs=[("mm", 1)])
                xg[sl] = ("mm", pr.mark("mm"))
            m_stats = pr.mark("mm")
            mrow = fp(FA_SR, 512, 0, 1)
            vrow = fp(FA_SR + 512, 512, 0, 1)
            rrow = fp(FA_SR + 1024, 512, 0, 1)
            pr.emit("scalar",
                    lambda e: e.activation(mrow, ps[0][0:1, :], AF.Copy,
                                           scale=cs0(C_RDM)),
                    waits=[("mm", m_stats), ("spst", pr.mark("spst"))],
                    incs=[("s", 1)])
            pr.emit("scalar",
                    lambda e: e.activation(vrow, ps[1][0:1, :], AF.Copy,
                                           scale=cs0(C_RDM)),
                    incs=[("s", 1)])
            bank_guard[0] = ("s", pr.mark("s"))
            bank_guard[1] = ("s", pr.mark("s"))
            m_mv = pr.mark("s")
            pr.emit("scalar", lambda e: e.activation(rrow, mrow, AF.Square),
                    waits=[("spst", pr.mark("spst"))], incs=[("s", 1)])
            pr.emit("vector",
                    lambda e: e.tensor_tensor(vrow, vrow, rrow, ALU.subtract),
                    waits=[("s", pr.mark("s"))], incs=[("v", 1)])
            m_var = pr.mark("v")
            pr.dma("sync", lambda t0=t0: st_scr[0:1, t0:t0 + 512],
                   lambda: mrow, "spst", waits=[("s", m_mv)])
            pr.emit("scalar",
                    lambda e: e.activation(rrow, vrow, AF.Ln,
                                           bias=cs0(C_EPS)),
                    waits=[("v", m_var)], incs=[("s", 1)])
            pr.emit("scalar",
                    lambda e: e.activation(rrow, rrow, AF.Exp,
                                           scale=cs0(C_NHALF)),
                    incs=[("s", 1)])
            pr.dma("sync", lambda t0=t0: st_scr[1:2, t0:t0 + 512],
                   lambda: rrow, "spst", waits=[("s", pr.mark("s"))])
            m_r1w = pr.mark("spst")
            pr.emit("scalar",
                    lambda e: e.activation(rrow, vrow, AF.Ln,
                                           scale=cs0(C_1PE),
                                           bias=cs0(C_EPS2)),
                    waits=[("spst", m_r1w)], incs=[("s", 1)])
            pr.emit("scalar",
                    lambda e: e.activation(rrow, rrow, AF.Exp,
                                           scale=cs0(C_NHALF)),
                    incs=[("s", 1)])
            pr.dma("sync", lambda t0=t0: st_scr[2:3, t0:t0 + 512],
                   lambda: rrow, "spst", waits=[("s", pr.mark("s"))])
        m_stats_done = pr.mark("spst")

        def a3_ap(t, blk, nkt):
            dims = [[nkt * 256, P]]
            nch = (nkt * 256) // 2048
            if nch > 1:
                dims.append([2048, nch])
            dims.append([1, min(2048, nkt * 256)])
            return _ap(t, blk * P * nkt * 256, dims)

        a3_items = [(lambda: wdtb[:, :], lambda: wdt[:, :]),
                    (lambda: _ap(wxpb, 0, [[2048, 96], [1, 2048]]),
                     lambda: _ap(wxp, 0, [[2048, 96], [1, 2048]]))]
        for c in range(4):
            a3_items.append((
                lambda c=c: _ap(winb, c * P * 8192,
                                [[8192, P], [2048, 4], [1, 2048]]),
                lambda c=c: _ap(winp, c * P * 8192,
                                [[8192, P], [2048, 4], [1, 2048]])))
        for (srct, blob, nblk, K) in ((wout, op16, 4, DI), (w1t, w1b, 16, DM),
                                      (w2t, w2b, 16, DM), (h1t, h1b, 16, DH),
                                      (h2t, h2b, 16, DH), (w3t, w3b, 4, DH)):
            nkt = K // P
            for blk in range(nblk):
                a3_items.append((
                    lambda blob=blob, blk=blk, nkt=nkt: a3_ap(blob, blk, nkt),
                    lambda srct=srct, blk=blk, nkt=nkt: a3_ap(srct, blk,
                                                              nkt)))
        a3_prev = [0, 0]
        for i, (dfn, sfn) in enumerate(a3_items):
            r = i % 2
            w = [("a3_%d" % r, a3_prev[r])] if a3_prev[r] else []
            a3_prev[r] = pr.dma("gpsimd", dfn, sfn, "a3_%d" % r, waits=w)
        m_a3_join = (("a3_0", a3_prev[0]), ("a3_1", a3_prev[1]))

        pr.emit("sync", lambda e: None, waits=list(m_a3_join))
        ev_guard = ("spev", 0)
        szs_guard = ("spsz", 0)
        for th in range(2):
            t0h = th * 2048
            mb = fp(FA_CVA, 2048)
            rb = fp(FA_CVB, 2048)
            pr.dma("sync", lambda: mb,
                   lambda t0h=t0h: _ap(st_scr, t0h, [[0, P], [1, 2048]]),
                   "bca", waits=[("spst", m_stats_done),
                                 ("v", pr.mark("v")), ("s", pr.mark("s"))])
            m_bc = pr.dma("sync", lambda: rb,
                          lambda t0h=t0h: _ap(st_scr, 2 * T + t0h,
                                              [[0, P], [1, 2048]]),
                          "bca")
            for fi in range(NF):
                for pc in range(4):
                    sl = (fi * 4 + pc) % 2
                    buf = fp(FA_XS + sl * 512, 512)
                    mload = pr.dma(
                        "sync", lambda b=buf: b,
                        lambda fi=fi, t0h=t0h, pc=pc: xT[
                            fi * P:(fi + 1) * P,
                            t0h + pc * 512:t0h + pc * 512 + 512],
                        "xa%d" % sl, waits=[xg[sl]])
                    pr.emit("vector",
                            lambda e, b=buf, pc=pc: e.tensor_tensor(
                                fp(FA_URAW + 4, 512), b,
                                fp(FA_CVA + pc * 512, 512), ALU.subtract),
                            waits=[("xa%d" % sl, mload), ("bca", m_bc)],
                            incs=[("v", 1)])
                    pr.emit("vector",
                            lambda e, fi=fi, pc=pc: e.scalar_tensor_tensor(
                                bp(BA_LNM + fi * 2048 + pc * 512, 512),
                                fp(FA_URAW + 4, 512),
                                c_sb[:, C_NW + fi:C_NW + fi + 1],
                                fp(FA_CVB + pc * 512, 512),
                                ALU.mult, ALU.mult),
                            incs=[("v", 1)])
                    xg[sl] = ("v", pr.mark("v"))
            m_lnm = pr.mark("v")
            wb_guard = [("mm", 0), ("mm", 0)]
            for dtile in range(32):
                ws = dtile % 2
                mw = pr.dma(
                    "sync", lambda ws=ws: bp(BA_WINB + ws * 1024, 1024),
                    lambda d=dtile: _ap(winb, d * P * 1024,
                                        [[1024, P], [1, 1024]]),
                    "wia%d" % ws, waits=[wb_guard[ws]])
                for fi in range(NF):
                    for tsb in range(4):
                        w = []
                        if fi == 0:
                            w = [("wia%d" % ws, mw), ("v", m_lnm),
                                 bank_guard[tsb]]
                        pr.emit(
                            "tensor",
                            lambda e, fi=fi, tsb=tsb, ws=ws: e.matmul(
                                ps[tsb][:, :],
                                bp(BA_WINB + ws * 1024 + fi * P, P),
                                bp(BA_LNM + fi * 2048 + tsb * 512, 512),
                                start=(fi == 0), stop=(fi == NF - 1)),
                            waits=w, incs=[("mm", 1)])
                m_grp = pr.mark("mm")
                wb_guard[ws] = ("mm", m_grp)
                if dtile < NJ:
                    for tsb in range(4):
                        pr.emit("scalar",
                                lambda e, tsb=tsb: e.activation(
                                    bp(BA_EV + tsb * 512, 512),
                                    ps[tsb][:, :], AF.Copy),
                                waits=[("mm", m_grp), ev_guard]
                                if tsb == 0 else [("mm", m_grp)],
                                incs=[("s", 1)])
                        bank_guard[tsb] = ("s", pr.mark("s"))
                    m_ev = pr.mark("s")
                    pr.dma("sync",
                           lambda d=dtile, t0h=t0h: uraw_scr[
                               d * P:(d + 1) * P, t0h:t0h + 2048],
                           lambda: bp(BA_EV, 2048), "spev",
                           waits=[("s", m_ev)])
                    ev_guard = ("spev", pr.mark("spev"))
                else:
                    for tsb in range(4):
                        pr.emit("scalar",
                                lambda e, tsb=tsb: e.activation(
                                    bp(BA_SZS + tsb * 512, 512),
                                    ps[tsb][:, :], AF.Silu),
                                waits=[("mm", m_grp), szs_guard]
                                if tsb == 0 else [("mm", m_grp)],
                                incs=[("s", 1)])
                        bank_guard[tsb] = ("s", pr.mark("s"))
                    m_ev = pr.mark("s")
                    j = dtile - NJ
                    pr.dma("sync",
                           lambda j=j, t0h=t0h: sz_scr[
                               j * P:(j + 1) * P, t0h:t0h + 2048],
                           lambda: bp(BA_SZS, 2048), "spsz",
                           waits=[("s", m_ev)])
                    szs_guard = ("spsz", pr.mark("spsz"))
        m_a1a = pr.mark("spev")

        for j in range(NJ):
            pr.dma("sync", lambda j=j: bp(BA_WXB + j * 96, 96),
                   lambda j=j: wxpb[j * P:(j + 1) * P, :], "wlb")
        pr.dma("sync", lambda: bp(BA_WDT, NJ * P, 0, 64),
               lambda: wdtb[:, :], "wlb")

        uraw_guard = ("v", 0)
        for j in range(NJ):
            mload = pr.dma("sync", lambda: bp(BA_URAW + 4, T),
                           lambda j=j: uraw_scr[j * P:(j + 1) * P, :],
                           "url", waits=[("spev", m_a1a), uraw_guard])
            m_pc = []
            for pc2 in range(2):
                o = pc2 * 2048
                pr.emit("vector",
                        lambda e, j=j, o=o: e.tensor_scalar(
                            fp(FA_CVA, 2048), bp(BA_URAW + 1 + o, 2048),
                            c_sb[:, C_CW + j * 4:C_CW + j * 4 + 1], None,
                            ALU.mult),
                        waits=[("url", mload), ("s", pr.mark("s"))],
                        incs=[("v", 1)])
                pr.emit("vector",
                        lambda e, j=j, o=o: e.scalar_tensor_tensor(
                            fp(FA_CVB, 2048), bp(BA_URAW + 2 + o, 2048),
                            c_sb[:, C_CW + j * 4 + 1:C_CW + j * 4 + 2],
                            fp(FA_CVA, 2048), ALU.mult, ALU.add),
                        incs=[("v", 1)])
                pr.emit("vector",
                        lambda e, j=j, o=o: e.scalar_tensor_tensor(
                            fp(FA_CVA, 2048), bp(BA_URAW + 3 + o, 2048),
                            c_sb[:, C_CW + j * 4 + 2:C_CW + j * 4 + 3],
                            fp(FA_CVB, 2048), ALU.mult, ALU.add),
                        incs=[("v", 1)])
                pr.emit("vector",
                        lambda e, j=j, o=o: e.scalar_tensor_tensor(
                            fp(FA_CVB, 2048), bp(BA_URAW + 4 + o, 2048),
                            c_sb[:, C_CW + j * 4 + 3:C_CW + j * 4 + 4],
                            fp(FA_CVA, 2048), ALU.mult, ALU.add),
                        incs=[("v", 1)])
                m_conv = pr.mark("v")
                pr.emit("scalar",
                        lambda e, j=j, o=o: e.activation(
                            bp(BA_UBF + o, 2048), fp(FA_CVB, 2048), AF.Silu,
                            bias=c_sb[:, C_CB + j:C_CB + j + 1]),
                        waits=[("v", m_conv), ("mm", pr.mark("mm")),
                               ("spu", pr.mark("spu"))],
                        incs=[("s", 1)])
                m_pc.append(pr.mark("s"))
            uraw_guard = ("s", m_pc[-1])
            m_u = m_pc[-1]
            pr.dma("sync", lambda j=j: u_scr[j * P:(j + 1) * P, :],
                   lambda: bp(BA_UBF, T), "spu", waits=[("s", m_u)])
            for tsn in range(8):
                bank = 4 + (tsn % 2)
                w = [("s", m_u), bank_guard[bank]]
                pr.emit("tensor",
                        lambda e, j=j, tsn=tsn, b=bank: e.matmul(
                            ps[b][0:96, :], bp(BA_WXB + j * 96, 96),
                            bp(BA_UBF + tsn * 512, 512),
                            start=True, stop=True),
                        waits=w, incs=[("mm", 1)])
                m_mmx = pr.mark("mm")
                if j == 0:
                    pr.emit("vector",
                            lambda e, tsn=tsn, b=bank: e.tensor_copy(
                                fp(FA_XDB + tsn * 512, 512, 0, 96),
                                ps[b][0:96, :]),
                            waits=[("mm", m_mmx)], incs=[("v", 1)])
                else:
                    pr.emit("vector",
                            lambda e, tsn=tsn, b=bank: e.tensor_tensor(
                                fp(FA_XDB + tsn * 512, 512, 0, 96),
                                fp(FA_XDB + tsn * 512, 512, 0, 96),
                                ps[b][0:96, :], ALU.add),
                            waits=[("mm", m_mmx)], incs=[("v", 1)])
                bank_guard[bank] = ("v", pr.mark("v"))
        m_a1b = pr.mark("v")

        pr.emit("vector",
                lambda e: e.tensor_copy(bp(BA_DTS, T, 0, 32),
                                        fp(FA_XDB, T, 64, 32)),
                waits=[("v", m_a1b)], incs=[("v", 1)])
        m_bccast = pr.mark("v")
        pr.dma("sync", lambda: bc_scr[:, :],
               lambda: bp(BA_DTS, T, 0, 32), "spbc",
               waits=[("v", m_bccast)])
        m_bcw = pr.mark("spbc")
        pr.emit("vector",
                lambda e: e.tensor_copy(bp(BA_DTR, T, 0, 64),
                                        fp(FA_XDB, T, 0, 64)),
                waits=[("v", m_a1b)], incs=[("v", 1)])
        m_dtr = pr.mark("v")
        dts_guard = ("spbc", m_bcw)
        for j in range(NJ):
            for tsn in range(8):
                bank = tsn % 2
                pr.emit("tensor",
                        lambda e, j=j, tsn=tsn, b=bank: e.matmul(
                            ps[b][:, :], bp(BA_WDT + j * P, P, 0, 64),
                            bp(BA_DTR + tsn * 512, 512, 0, 64),
                            start=True, stop=True),
                        waits=[("v", m_dtr), bank_guard[bank]],
                        incs=[("mm", 1)])
                m_mmd = pr.mark("mm")
                col = tsn * 512
                pr.emit("scalar",
                        lambda e, b=bank, j=j: e.activation(
                            fp(FA_CVA, 512), ps[b][:, :], AF.Abs,
                            bias=c_sb[:, C_DTB + j:C_DTB + j + 1]),
                        waits=[("mm", m_mmd), ("v", pr.mark("v"))],
                        incs=[("s", 1)])
                pr.emit("scalar",
                        lambda e: e.activation(fp(FA_CVA, 512),
                                               fp(FA_CVA, 512), AF.Exp,
                                               scale=csP(C_NONE)),
                        incs=[("s", 1)])
                pr.emit("scalar",
                        lambda e: e.activation(fp(FA_CVA, 512),
                                               fp(FA_CVA, 512), AF.Ln,
                                               bias=1.0),
                        incs=[("s", 1)])
                pr.emit("scalar",
                        lambda e, b=bank, j=j: e.activation(
                            fp(FA_CVB, 512), ps[b][:, :], AF.Relu,
                            bias=c_sb[:, C_DTB + j:C_DTB + j + 1]),
                        incs=[("s", 1)])
                bank_guard[bank] = ("s", pr.mark("s"))
                m_sp4 = pr.mark("s")
                pr.emit("vector",
                        lambda e, col=col: e.tensor_tensor(
                            bp(BA_DTS + col, 512), fp(FA_CVA, 512),
                            fp(FA_CVB, 512), ALU.add),
                        waits=[("s", m_sp4), dts_guard], incs=[("v", 1)])
            m_dtj = pr.mark("v")
            pr.dma("sync", lambda j=j: dt_scr[j * P:(j + 1) * P, :],
                   lambda: bp(BA_DTS, T), "spdt", waits=[("v", m_dtj)])
            dts_guard = ("spdt", pr.mark("spdt"))

        m_spu = pr.mark("spu")
        m_spdt = pr.mark("spdt")
        m_spsz = pr.mark("spsz")

        u_marks = {}
        m_ymj = {}
        m_lnf = {}
        dt_marks = {}
        sz_marks = {}
        u_guard = [("v", 0)] * NJ
        dt_guard = [("s", 0)] * NJ
        sz_guard = [("v", 0)] * 2
        bb_guard = [("spbc", m_bcw)] * 2
        cb_guard = [("spbc", m_bcw)] * 2
        a_guard = [("v", 0)] * 4
        hc_guard = [("gp", 0)]
        ff_guard = [("o%d" % fi, 0) for fi in range(NF)]
        wp_guard = [("mm", 0)] * 4
        m_outproj = [0] * (NCH + 1)
        m_lnx1_dbg = [0] * (NCH + 1)
        m_lnx1_s = [0]
        m_y0 = [0]
        m_w = [0]
        unit_rr = [0]
        dbg_gate = [None, None]
        m_ymix = [0] * (NCH + 1)
        m_scan_end_v = [0] * (NCH + 1)
        m_scan_end_s = [0] * (NCH + 1)

        def u_off(j):
            return BB_U + j * 512

        def dt_off(j):
            return BB_DT + j * 512

        def emit_loads(g):
            gw = [dbg_gate[1]] if dbg_gate[1] else []
            dbg_gate[1] = None
            for j in range(NJ):
                u_marks[(g, j)] = pr.dma(
                    "sync", lambda j=j: bp(u_off(j), 512),
                    lambda g=g, j=j: u_scr[j * P:(j + 1) * P,
                                           g * TC:(g + 1) * TC],
                    "u%d" % j, waits=[u_guard[j]] + gw)
                dt_marks[(g, j)] = pr.dma(
                    "sync", lambda j=j: bp(dt_off(j), 512),
                    lambda g=g, j=j: dt_scr[j * P:(j + 1) * P,
                                            g * TC:(g + 1) * TC],
                    "d%d" % j, waits=[dt_guard[j]])
            for j in range(2):
                sz_marks[(g, j)] = pr.dma(
                    "sync", lambda j=j: bp(BB_SZ + (j % 2) * 512, 512),
                    lambda g=g, j=j: sz_scr[j * P:(j + 1) * P,
                                            g * TC:(g + 1) * TC],
                    "z%d" % (j % 2), waits=[sz_guard[j % 2]])

        def emit_scan_head(g):
            prev_ymix = m_ymix[g - 1] if g > 0 else 0
            for j in range(NJ):
                pr.emit("scalar",
                        lambda e, j=j: e.activation(
                            fp(FB_Y + j * 512, 512), bp(u_off(j), 512),
                            AF.Copy, scale=c_sb[:, C_D + j:C_D + j + 1]),
                        waits=[("u%d" % j, u_marks[(g, j)]),
                               ("v", prev_ymix)],
                        incs=[("s", 1)])
            m_y0[0] = pr.mark("s")
            for j in range(NJ):
                pr.emit("vector",
                        lambda e, j=j: e.tensor_tensor(
                            bp(u_off(j), 512), bp(u_off(j), 512),
                            bp(dt_off(j), 512), ALU.mult),
                        waits=[("u%d" % j, u_marks[(g, j)]),
                               ("d%d" % j, dt_marks[(g, j)]),
                               ("s", m_y0[0])],
                        incs=[("v", 1)])
            m_w[0] = pr.mark("v")
            if debug and g == 0:
                pr.dma("sync", lambda: dbg_st[:, :], lambda: state[:, :],
                       "dbgs", waits=[("v", m_w[0])])
                pr.dma("sync", lambda: dbg_an[:, :], lambda: aneg[:, :],
                       "dbgs")

        def emit_scan_q(g, q):
            qs = q % 2
            bbo = BB_BB + qs * 1024
            cbo = BB_CB + qs * 1024
            mq_b = pr.dma(
                "scalar", lambda o=bbo: bp(o, 1024),
                lambda g=g, q=q: _ap(
                    bc_scr, (q * 2) * T + g * TC,
                    [[0, P], [T, 2], [1, TC]]),
                "bq%d" % qs, waits=[bb_guard[qs]])
            mq_c = pr.dma(
                "scalar", lambda o=cbo: bp(o, 1024),
                lambda g=g, q=q: _ap(
                    bc_scr, (16 + q * 2) * T + g * TC,
                    [[0, P], [T, 2], [1, TC]]),
                "cq%d" % qs, waits=[cb_guard[qs]])
            for j in range(NJ):
                pr.emit("vector",
                        lambda e, j=j, bbo=bbo: e.tensor_tensor(
                            bp(BB_B, 1024),
                            _ap(bpool, u_off(j),
                                [[BW, P], [0, 2], [1, TC]]),
                            bp(bbo, 1024), ALU.mult),
                        waits=[("bq%d" % qs, mq_b), ("v", m_w[0])],
                        incs=[("v", 1)])
                for nn in range(2):
                    n = q * 2 + nn
                    asl = (j * 2 + nn) % 3
                    pr.emit("scalar",
                            lambda e, g=g, j=j, n=n, asl=asl:
                            e.activation(
                                fp(FB_A + asl * 512, 512),
                                bp(dt_off(j), 512),
                                AF.Exp,
                                scale=aneg[:, j * NST + n:
                                           j * NST + n + 1]),
                            waits=[("d%d" % j, dt_marks[(g, j)]),
                                   a_guard[asl]],
                            incs=[("s", 1)])
                    m_a = pr.mark("s")
                    pr.emit("vector",
                            lambda e, j=j, n=n, nn=nn, asl=asl:
                            e.tensor_tensor_scan(
                                bp(BB_H + nn * 512, 512),
                                fp(FB_A + asl * 512, 512),
                                bp(BB_B + nn * 512, 512),
                                state[:, j * NST + n:
                                      j * NST + n + 1],
                                ALU.mult, ALU.add),
                            waits=[("s", m_a)],
                            incs=[("v", 1)])
                    a_guard[asl] = ("v", pr.mark("v"))
                pr.emit("vector",
                        lambda e, cbo=cbo: e.tensor_tensor(
                            bp(BB_HC, 1024), bp(BB_H, 1024),
                            bp(cbo, 1024), ALU.mult),
                        waits=[("cq%d" % qs, mq_c), hc_guard[0]],
                        incs=[("v", 1)])
                m_hc = pr.mark("v")
                for nn in range(2):
                    n = q * 2 + nn
                    pr.emit("vector",
                            lambda e, j=j, n=n, nn=nn: e.tensor_copy(
                                state[:, j * NST + n:j * NST + n + 1],
                                bp(BB_H + nn * 512 + 511, 1)),
                            incs=[("v", 1)])
                if debug and g == 0 and q == 0 and j == 0:
                    pr.dma("sync", lambda: dbg_bb[:, 0:1024],
                           lambda: bp(BB_BB, 1024), "dbgs",
                           waits=[("v", m_hc)])
                    pr.dma("sync", lambda: dbg_bb[:, 1024:2048],
                           lambda: bp(BB_CB, 1024), "dbgs")
                    dbg_gate[0] = ("dbgs", pr.mark("dbgs"))
                for nn in range(2):
                    pr.emit("vector",
                            lambda e, j=j, nn=nn: e.tensor_tensor(
                                fp(FB_Y + j * 512, 512),
                                fp(FB_Y + j * 512, 512),
                                bp(BB_HC + nn * 512, 512),
                                ALU.add),
                            waits=[("s", m_y0[0])],
                            incs=[("v", 1)])
                hc_guard[0] = ("v", pr.mark("v"))
            bb_guard[qs] = ("v", pr.mark("v"))
            cb_guard[qs] = ("v", pr.mark("v"))

        def emit_scan_tail(g):
            m_adds = pr.mark("gp")
            for j in range(NJ):
                w = [("gp", m_adds), ("z%d" % (j % 2), sz_marks[(g, j)]),
                     ("mm", m_outproj[g - 1] if g > 0 else 0)]
                pr.emit("vector",
                        lambda e, g=g, j=j: e.tensor_tensor(
                            bp(BB_YM + j * 512, 512),
                            fp(FB_Y + j * 512, 512),
                            bp(BB_SZ + (j % 2) * 512, 512), ALU.mult),
                        waits=w, incs=[("v", 1)])
                sz_guard[j % 2] = ("v", pr.mark("v"))
                m_ymj[(g, j)] = pr.mark("v")
                if j + 2 < NJ:
                    sz_marks[(g, j + 2)] = pr.dma(
                        "sync",
                        lambda j=j: bp(BB_SZ + (j % 2) * 512, 512),
                        lambda g=g, j=j: sz_scr[(j + 2) * P:(j + 3) * P,
                                                g * TC:(g + 1) * TC],
                        "z%d" % (j % 2), waits=[sz_guard[j % 2]])
            m_ymix[g] = pr.mark("v")
            for j in range(NJ):
                u_guard[j] = ("v", m_ymix[g])
                dt_guard[j] = ("s", pr.mark("s"))

        def wp_load(unit, blob, off_elems, kcnt, nkt, extra_waits=()):
            m = pr.dma(
                "sync", lambda u=unit, kcnt=kcnt: bp(BB_WP + u * 4096,
                                                     kcnt * 256),
                lambda blob=blob, o=off_elems, kcnt=kcnt, nkt=nkt: _ap(
                    blob, o, [[nkt * 256, P], [1, kcnt * 256]]),
                "w%d" % unit, waits=[wp_guard[unit]] + list(extra_waits))
            return m

        def emit_back_head(g):
            t0 = g * TC
            mb1 = pr.dma("sync", lambda: fp(FB_BC1, 512),
                         lambda g=g: _ap(st_scr, t0, [[0, P], [1, TC]]),
                         "bc1", waits=[("v", pr.mark("v")),
                                       ("spst", m_stats_done)])
            rb1 = pr.dma("sync", lambda: fp(FB_BC1 + 512, 512),
                         lambda g=g: _ap(st_scr, T + t0, [[0, P], [1, TC]]),
                         "bc1")
            for blk in range(4):
                unit = unit_rr[0] % 4
                unit_rr[0] += 1
                mw = wp_load(unit, op16, blk * P * 16 * 256, NJ, 16)
                for mt in range(2):
                    fi = blk * 2 + mt
                    bank = fi % 4
                    for j in range(NJ):
                        w = []
                        if mt == 0:
                            w = [("v", m_ymj[(g, j)])]
                        if j == 0:
                            w = w + [("w%d" % unit, mw)] + (
                                [bank_guard[bank]])
                        pr.emit("tensor",
                                lambda e, mt=mt, j=j, b=bank, u=unit:
                                e.matmul(
                                    ps[b][:, :],
                                    bp(BB_WP + u * 4096 + j * 256 + mt * P,
                                       P),
                                    bp(BB_YM + j * 512, 512),
                                    start=(j == 0), stop=(j == NJ - 1)),
                                waits=w, incs=[("mm", 1)])
                    m_op = pr.mark("mm")
                    xsl = 0
                    mxf = pr.dma(
                        "sync", lambda xsl=xsl: fp(FB_XF + xsl * 512, 512),
                        lambda g=g, fi=fi: xT[fi * P:(fi + 1) * P,
                                              t0:t0 + TC],
                        "x%d" % xsl, waits=[("v", pr.mark("v"))] if fi >= 1
                        else [("v", m_ymix[g])])
                    pr.emit("vector",
                            lambda e, xsl=xsl: e.tensor_tensor(
                                fp(FB_TS, 512), fp(FB_XF + xsl * 512, 512),
                                fp(FB_BC1, 512), ALU.subtract),
                            waits=[("x%d" % xsl, mxf), ("bc1", rb1),
                                   ("mm", m_op), ("s", pr.mark("s"))],
                            incs=[("v", 1)])
                    pr.emit("vector",
                            lambda e: e.tensor_tensor(
                                fp(FB_TS, 512), fp(FB_TS, 512),
                                fp(FB_BC1 + 512, 512), ALU.mult),
                            incs=[("v", 1)])
                    pr.emit("vector",
                            lambda e, xsl=xsl: e.tensor_tensor(
                                fp(FB_TS, 512), fp(FB_TS, 512),
                                fp(FB_XF + xsl * 512, 512), ALU.add),
                            incs=[("v", 1)])
                    pr.emit("vector",
                            lambda e, fi=fi, b=bank: e.tensor_tensor(
                                fp(FB_X1 + fi * 512, 512), fp(FB_TS, 512),
                                ps[b][:, :], ALU.add),
                            incs=[("v", 1)])
                    bank_guard[bank] = ("v", pr.mark("v"))
                wp_guard[unit] = ("mm", m_op)
            m_outproj[g] = pr.mark("mm")
            m_x1 = pr.mark("v")

            for fi in range(NF):
                pr.emit("scalar",
                        lambda e, fi=fi: e.activation(
                            fp(FB_TS, 512), fp(FB_X1 + fi * 512, 512),
                            AF.Square),
                        waits=[("v", m_x1), ("mm", pr.mark("mm"))],
                        incs=[("s", 1)])
                msq = pr.mark("s")
                w6 = [("v", m_x1)] + ([bank_guard[6]] if fi == 0 else [])
                pr.emit("tensor",
                        lambda e, fi=fi: e.matmul(
                            ps[6][0:1, :], c_sb[:, C_ONE:C_ONE + 1],
                            fp(FB_X1 + fi * 512, 512),
                            start=(fi == 0), stop=(fi == NF - 1)),
                        waits=w6, incs=[("mm", 1)])
                w7 = [("s", msq)] + ([bank_guard[7]] if fi == 0 else [])
                pr.emit("tensor",
                        lambda e, fi=fi: e.matmul(
                            ps[7][0:1, :], c_sb[:, C_ONE:C_ONE + 1],
                            fp(FB_TS, 512),
                            start=(fi == 0), stop=(fi == NF - 1)),
                        waits=w7, incs=[("mm", 1)])
            m_st2 = pr.mark("mm")
            m2row = fp(FB_SR, 512, 0, 1)
            r2row = fp(FB_SR + 512, 512, 0, 1)
            pr.emit("scalar",
                    lambda e: e.activation(m2row, ps[6][0:1, :], AF.Copy,
                                           scale=cs0(C_RDM)),
                    waits=[("mm", m_st2), ("st2", pr.mark("st2"))],
                    incs=[("s", 1)])
            pr.emit("scalar",
                    lambda e: e.activation(r2row, ps[7][0:1, :], AF.Copy,
                                           scale=cs0(C_RDM)),
                    incs=[("s", 1)])
            bank_guard[6] = ("s", pr.mark("s"))
            bank_guard[7] = ("s", pr.mark("s"))
            pr.emit("scalar",
                    lambda e: e.activation(fp(FB_TS, 512, 0, 1), m2row,
                                           AF.Square),
                    incs=[("s", 1)])
            m_s2 = pr.mark("s")
            pr.emit("vector",
                    lambda e: e.tensor_tensor(r2row, r2row,
                                              fp(FB_TS, 512, 0, 1),
                                              ALU.subtract),
                    waits=[("s", m_s2)], incs=[("v", 1)])
            m_v2 = pr.mark("v")
            pr.emit("scalar",
                    lambda e: e.activation(r2row, r2row, AF.Ln,
                                           bias=cs0(C_EPS)),
                    waits=[("v", m_v2)], incs=[("s", 1)])
            pr.emit("scalar",
                    lambda e: e.activation(r2row, r2row, AF.Exp,
                                           scale=cs0(C_NHALF)),
                    incs=[("s", 1)])
            m_rr2 = pr.mark("s")
            pr.emit("tensor",
                    lambda e: e.matmul(ps[6][:, :],
                                       c_sb[0:1, C_OROW:C_OROW + P],
                                       m2row, start=True, stop=True),
                    waits=[("s", m_rr2), bank_guard[6]], incs=[("mm", 1)])
            pr.emit("tensor",
                    lambda e: e.matmul(ps[7][:, :],
                                       c_sb[0:1, C_OROW:C_OROW + P],
                                       r2row, start=True, stop=True),
                    waits=[("s", m_rr2), bank_guard[7]], incs=[("mm", 1)])
            m_bc2 = pr.mark("mm")
            for fi in range(NF):
                pr.emit("vector",
                        lambda e, fi=fi: e.tensor_tensor(
                            fp(FB_TS, 512), fp(FB_X1 + fi * 512, 512),
                            ps[6][:, :], ALU.subtract),
                        waits=[("mm", m_bc2)], incs=[("v", 1)])
                pr.emit("vector",
                        lambda e, fi=fi: e.tensor_tensor(
                            bp(BB_LN1 + fi * 512, 512), fp(FB_TS, 512),
                            ps[7][:, :], ALU.mult),
                        incs=[("v", 1)])
                m_lnf[fi] = pr.mark("v")
            bank_guard[6] = ("v", pr.mark("v"))
            bank_guard[7] = ("v", pr.mark("v"))
            m_lnx1_s[0] = pr.mark("v")
            m_lnx1_dbg[g] = m_lnx1_s[0]


        def gen_back_ffn(g):
            t0 = g * TC
            m_lnx1 = m_lnx1_s[0]
            for hf in range(2):
                hoff = hf * 256
                for blk in range(16):
                    unit = unit_rr[0] % 4
                    unit_rr[0] += 1
                    mw1 = pr.dma(
                        "sync", lambda u=unit: bp(BB_WP + u * 4096,
                                                  NF * 256),
                        lambda blk=blk: _ap(w1b, blk * P * 8 * 256,
                                            [[8 * 256, P], [1, NF * 256]]),
                        "w%d" % unit, waits=[wp_guard[unit]])
                    mw2 = pr.dma(
                        "sync", lambda u=unit: bp(BB_WP + u * 4096 +
                                                  NF * 256, NF * 256),
                        lambda blk=blk: _ap(w2b, blk * P * 8 * 256,
                                            [[8 * 256, P], [1, NF * 256]]),
                        "w%d" % unit)
                    for mt in range(2):
                        ht = blk * 2 + mt
                        b1 = (ht % 3) * 2
                        b2 = b1 + 1
                        for k in range(NF):
                            w = [("v", m_lnf[k])]
                            if k == 0:
                                w = w + [("w%d" % unit, mw2),
                                         bank_guard[b1]]
                            pr.emit("tensor",
                                    lambda e, k=k, mt=mt, b=b1, u=unit,
                                    hoff=hoff: e.matmul(
                                        ps[b][:, 0:256],
                                        bp(BB_WP + u * 4096 + k * 256 +
                                           mt * P, P),
                                        bp(BB_LN1 + k * 512 + hoff, 256),
                                        start=(k == 0), stop=(k == NF - 1)),
                                    waits=w, incs=[("mm", 1)])
                        for k in range(NF):
                            w = []
                            if k == 0:
                                w = [("w%d" % unit, mw2), bank_guard[b2]]
                            pr.emit("tensor",
                                    lambda e, k=k, mt=mt, b=b2, u=unit,
                                    hoff=hoff: e.matmul(
                                        ps[b][:, 0:256],
                                        bp(BB_WP + u * 4096 + NF * 256 +
                                           k * 256 + mt * P, P),
                                        bp(BB_LN1 + k * 512 + hoff, 256),
                                        start=(k == 0), stop=(k == NF - 1)),
                                    waits=w, incs=[("mm", 1)])
                        m_a12 = pr.mark("mm")
                        pr.emit("scalar",
                                lambda e, ht=ht, b=b1: e.activation(
                                    fp(FB_TS2, 256), ps[b][:, 0:256],
                                    AF.Tanh, scale=hs1[:, ht:ht + 1]),
                                waits=[("mm", m_a12), ("v", pr.mark("v"))],
                                incs=[("s", 1)])
                        m_t = pr.mark("s")
                        pr.emit("vector",
                                lambda e, b=b1: e.scalar_tensor_tensor(
                                    fp(FB_TS2, 256), fp(FB_TS2, 256), 1.0,
                                    ps[b][:, 0:256], ALU.add, ALU.mult),
                                waits=[("s", m_t)], incs=[("v", 1)])
                        bank_guard[b1] = ("v", pr.mark("v"))
                        pr.emit("vector",
                                lambda e, ht=ht, b=b2:
                                e.scalar_tensor_tensor(
                                    bp(BB_G + ht * 256, 256),
                                    fp(FB_TS2, 256),
                                    k12[:, ht:ht + 1], ps[b][:, 0:256],
                                    ALU.mult, ALU.mult),
                                incs=[("v", 1)])
                        bank_guard[b2] = ("v", pr.mark("v"))
                    wp_guard[unit] = ("mm", pr.mark("mm"))
                    yield
                m_g = pr.mark("v")

                def h_blocks(blob, src_off, dst_off, sc, m_in, m_in_sem):
                    for blk in range(16):
                        ua = unit_rr[0] % 4
                        unit_rr[0] += 1
                        ub = unit_rr[0] % 4
                        unit_rr[0] += 1
                        mwa = wp_load(ua, blob, blk * P * 32 * 256, 16, 32)
                        mwb = wp_load(ub, blob,
                                      blk * P * 32 * 256 + 16 * 256, 16, 32)
                        for mt in range(2):
                            ht = blk * 2 + mt
                            bank = ht % 6
                            for k in range(NH):
                                unit = ua if k < 16 else ub
                                kk = k % 16
                                w = []
                                if k == 0:
                                    w = [("w%d" % ua, mwa),
                                         ("w%d" % ub, mwb),
                                         (m_in_sem, m_in),
                                         bank_guard[bank]]
                                pr.emit(
                                    "tensor",
                                    lambda e, k=k, kk=kk, mt=mt, b=bank,
                                    u=unit, so=src_off: e.matmul(
                                        ps[b][:, 0:256],
                                        bp(BB_WP + u * 4096 + kk * 256 +
                                           mt * P, P),
                                        bp(so + k * 256, 256),
                                        start=(k == 0), stop=(k == NH - 1)),
                                    waits=w, incs=[("mm", 1)])
                            m_h = pr.mark("mm")
                            pr.emit("scalar",
                                    lambda e, ht=ht, b=bank, do=dst_off,
                                    sc=sc: e.activation(
                                        bp(do + ht * 256, 256),
                                        ps[b][:, 0:256], AF.Copy,
                                        scale=c_sb[:, sc + ht:sc + ht + 1]),
                                    waits=[("mm", m_h)], incs=[("s", 1)])
                            bank_guard[bank] = ("s", pr.mark("s"))
                        wp_guard[ua] = ("mm", pr.mark("mm"))
                        wp_guard[ub] = ("mm", pr.mark("mm"))
                        yield
                    pr.emit("scalar",
                            lambda e, do=dst_off: e.activation(
                                bp(do, NH * 256), bp(do, NH * 256),
                                AF.Gelu),
                            waits=[("s", pr.mark("s"))], incs=[("s", 1)])

                yield from h_blocks(h1b, BB_G + 0, BB_P, C_SH1, m_g, "v")
                m_p1 = pr.mark("s")
                yield from h_blocks(h2b, BB_P, BB_G, C_SH2, m_p1, "s")
                m_p2 = pr.mark("s")

                for blk in range(4):
                    ua = unit_rr[0] % 4
                    unit_rr[0] += 1
                    ub = unit_rr[0] % 4
                    unit_rr[0] += 1
                    mwa = wp_load(ua, w3b, blk * P * 32 * 256, 16, 32)
                    mwb = wp_load(ub, w3b, blk * P * 32 * 256 + 16 * 256,
                                  16, 32)
                    for mt in range(2):
                        fi = blk * 2 + mt
                        bank = 4 + (fi % 2)
                        for k in range(NH):
                            unit = ua if k < 16 else ub
                            kk = k % 16
                            w = []
                            if k == 0:
                                w = [("w%d" % ua, mwa), ("w%d" % ub, mwb),
                                     ("s", m_p2),
                                     bank_guard[bank]]
                            pr.emit("tensor",
                                    lambda e, k=k, kk=kk, mt=mt, b=bank,
                                    u=unit: e.matmul(
                                        ps[b][:, 0:256],
                                        bp(BB_WP + u * 4096 + kk * 256 +
                                           mt * P, P),
                                        bp(BB_G + k * 256, 256),
                                        start=(k == 0), stop=(k == NH - 1)),
                                    waits=w, incs=[("mm", 1)])
                        m_w3 = pr.mark("mm")
                        pr.emit("scalar",
                                lambda e, fi=fi, b=bank: e.activation(
                                    fp(FB_FF + fi * 256, 256),
                                    ps[b][:, 0:256], AF.Copy,
                                    scale=c_sb[:, C_S3 + fi:C_S3 + fi + 1]),
                                waits=[("mm", m_w3), ff_guard[fi]],
                                incs=[("s", 1)])
                        bank_guard[bank] = ("s", pr.mark("s"))
                        m_ev = pr.mark("s")
                        pr.emit("vector",
                                lambda e, fi=fi, hoff=hoff: e.tensor_tensor(
                                    fp(FB_FF + fi * 256, 256),
                                    fp(FB_FF + fi * 256, 256),
                                    fp(FB_X1 + fi * 512 + hoff, 256),
                                    ALU.add),
                                waits=[("s", m_ev)], incs=[("v", 1)])
                        m_out = pr.mark("v")
                        pr.dma("sync",
                               lambda g=g, fi=fi, hoff=hoff:
                               y_out[fi * P:(fi + 1) * P,
                                     t0 + hoff:t0 + hoff + 256],
                               lambda fi=fi: fp(FB_FF + fi * 256, 256),
                               "o%d" % fi, waits=[("v", m_out)])
                        ff_guard[fi] = ("o%d" % fi, pr.mark("o%d" % fi))
                    wp_guard[ua] = ("mm", pr.mark("mm"))
                    wp_guard[ub] = ("mm", pr.mark("mm"))
                    yield

        for j in range(NJ):
            u_guard[j] = ("spu", m_spu)
            dt_guard[j] = ("spdt", m_spdt)
        for j in range(2):
            sz_guard[j] = ("spsz", m_spsz)
        for u in range(4):
            wp_guard[u] = ("mm", 0)
        pr.emit("sync", lambda e: None, waits=list(m_a3_join))
        emit_loads(0)
        for s in range(NCH + 1):
            ffn = None
            if s >= 1:
                emit_back_head(s - 1)
                ffn = gen_back_ffn(s - 1)
            if s < NCH:
                emit_scan_head(s)
                for q in range(8):
                    emit_scan_q(s, q)
                    if ffn is not None:
                        for _ in range(14):
                            if next(ffn, "done") == "done":
                                ffn = None
                                break
                emit_scan_tail(s)
                if s + 1 < NCH:
                    emit_loads(s + 1)
                if debug and s == 0:
                    pr.dma("sync", lambda: dbg_y[:, :],
                           lambda: fp(FB_Y, NJ * 512),
                           "dbgs", waits=[("v", m_ymix[0])])
                    pr.dma("sync", lambda: dbg_ym[:, :],
                           lambda: bp(BB_YM, NJ * 512), "dbgs",
                           waits=[("v", m_ymix[0])])
            if ffn is not None:
                for _ in iter(lambda: next(ffn, "done"), "done"):
                    pass
            if debug and s == 1:
                pr.dma("sync", lambda: dbg_x1[:, :],
                       lambda: fp(FB_X1, NF * 512), "dbgs",
                       waits=[("v", m_lnx1_dbg[0])])
                pr.dma("sync", lambda: dbg_ln1[:, :],
                       lambda: bp(BB_LN1, NF * 512), "dbgs",
                       waits=[("v", m_lnx1_dbg[0])])

        final_waits = [("o%d" % fi, pr.mark("o%d" % fi))
                       for fi in range(NF)]
        if debug:
            final_waits.append(("dbgs", pr.mark("dbgs")))
        pr.emit("sync", lambda e: e.nop(), waits=final_waits)

        semh = {}
        with ExitStack() as semctx:
            for name in pr.semnames:
                semh[name] = semctx.enter_context(nc.semaphore(name))
            with nc.Block() as block:
                def runner(stream):
                    def run(eng_obj):
                        for waits, fn, incs in stream:
                            for s, val in waits:
                                eng_obj.wait_ge(semh[s], val)
                            instr = fn(eng_obj)
                            for s, d in incs:
                                instr.then_inc(semh[s], d)
                    return run

                block.sync(runner(pr.streams["sync"]))
                block.vector(runner(pr.streams["vector"]))
                block.scalar(runner(pr.streams["scalar"]))
                block.tensor(runner(pr.streams["tensor"]))
                block.gpsimd(runner(pr.streams["gpsimd"]))

    return nc



def _pack_cvec(conv_w, conv_b, dtb, A_log, D, nw, s1, s2, sh1, sh2, s3):
    c = np.zeros((P, 652), dtype=np.float32)
    c[:, 0:64] = conv_w.reshape(16, 128, 4).transpose(1, 0, 2).reshape(128, 64)
    c[:, 64:80] = conv_b.reshape(16, 128).T
    c[:, 80:96] = dtb.reshape(16, 128).T
    c[:, 96:352] = A_log.reshape(16, 128, 16).transpose(1, 0, 2).reshape(
        128, 256)
    c[:, 352:368] = D.reshape(16, 128).T
    c[:, 368:376] = nw.reshape(8, 128).T
    c[:, 376:408] = s1.reshape(32, 128).T
    c[:, 408:440] = s2.reshape(32, 128).T
    c[:, 440:472] = sh1.reshape(32, 128).T
    c[:, 472:504] = sh2.reshape(32, 128).T
    c[:, 504:512] = s3.reshape(8, 128).T
    c[:, 512] = 1.0
    c[:, 513] = EPS
    c[:, 514] = 1.0 + EPS
    c[:, 515] = EPS * EPS
    c[:, 516] = -0.5
    c[:, 517] = -1.0
    c[:, 518] = 1.0 / DM
    c[0, 520:648] = 1.0
    return c


def _blob_win(w):
    return np.ascontiguousarray(
        w.reshape(NF, P, 32, P).transpose(2, 1, 0, 3)).reshape(-1)


def _blob(wt, nblk):
    K, M = wt.shape
    assert M == nblk * 256
    nkt = K // P
    return np.ascontiguousarray(
        wt.reshape(nkt, P, nblk, 256).transpose(2, 1, 0, 3)).reshape(-1)


_NC_CACHE = {}


def kernel(**inp):
    f32 = np.float32
    x = np.ascontiguousarray(np.asarray(inp["L0_attn_x"], dtype=f32))
    B = x.shape[0]
    asnp = lambda k: np.ascontiguousarray(np.asarray(inp[k], dtype=f32))

    shared = dict(
        winp=_blob_win(asnp("L0_attn_in_proj_w")),
        wxp=asnp("L0_attn_x_proj_w"),
        wdt=asnp("L0_attn_dt_proj_w"),
        wout=_blob(asnp("L0_attn_out_proj_w"), 4),
        w1t=_blob(asnp("L0_ffn_w1_w").T, 16),
        w2t=_blob(asnp("L0_ffn_w2_w").T, 16),
        h1t=_blob(asnp("L0_ffn_h1_w").T, 16),
        h2t=_blob(asnp("L0_ffn_h2_w").T, 16),
        w3t=_blob(asnp("L0_ffn_w3_w").T, 4),
        cvec=_pack_cvec(
            asnp("L0_attn_conv_w"), asnp("L0_attn_conv_b"),
            asnp("L0_attn_dt_proj_b"), asnp("L0_attn_A_log"),
            asnp("L0_attn_D"), asnp("L0_attn_norm_w"),
            asnp("L0_ffn_w1_s"), asnp("L0_ffn_w2_s"),
            asnp("L0_ffn_h1_s"), asnp("L0_ffn_h2_s"),
            asnp("L0_ffn_w3_s")),
    )

    if "nc" not in _NC_CACHE:
        _NC_CACHE["nc"] = build_program()
    nc = _NC_CACHE["nc"]

    in_maps = []
    for b in range(B):
        m = dict(shared)
        m["xT"] = np.ascontiguousarray(x[b].T)
        in_maps.append(m)

    res = run_bass_kernel_spmd(nc, in_maps, core_ids=list(range(B)))
    out = np.stack([np.asarray(res.results[b]["y_out"]).T for b in range(B)])
    return out.astype(np.float32)


# version 25 (final)
# speedup vs baseline: 1.1778x; 1.0107x over previous
import numpy as np
from contextlib import ExitStack

import concourse.bass as bass
import concourse.mybir as mybir
from concourse.bass_utils import run_bass_kernel_spmd

F32 = mybir.dt.float32
BF16 = mybir.dt.bfloat16
AF = mybir.ActivationFunctionType
ALU = mybir.AluOpType

P = 128
T = 4096
DM = 1024
DI = 2048
DH = 4096
NST = 16
RNK = 64
NF = DM // P
NJ = DI // P
NH = DH // P
TC = 512
NCH = T // TC
TS = 256
EPS = 1e-5

ENGINES = ("sync", "vector", "scalar", "tensor", "gpsimd")

FB_Y = 0
FB_A = FB_Y + NJ * 512
FB_XF = FB_A + 3 * 512
FB_X1 = FB_XF + 1 * 512
FB_TS = FB_X1 + NF * 512
FB_TS2 = FB_TS
FB_BC1 = FB_TS + 512
FB_FF = FB_BC1 + 2 * 512
FB_SR = FB_FF + NF * 256
FB_END = FB_SR + 2 * 512
FA_URAW = 0
FA_CVA = FA_URAW + 520
FA_CVB = FA_CVA + 2048
FA_XDB = FA_CVB + 2048
FA_XS = FA_XDB + 4096
FA_WC = FA_XS + 1024
FA_SR = FA_WC + 1024
FA_END = FA_SR + 3 * 512
FW = max(FB_END, FA_END)

BB_U = 0
BB_DT = BB_U + NJ * 512
BB_G = BB_DT + NJ * 512
BB_P = BB_G + NH * 256
BB_YM = BB_P + NH * 256
BB_LN1 = BB_YM + NJ * 512
BB_SZ = BB_LN1 + NF * 512
BB_BB = BB_SZ + 2 * 512
BB_CB = BB_BB + 2 * 1024
BB_B = BB_CB + 2 * 1024
BB_H = BB_B + 1024
BB_HC = BB_H + 1024
BB_WP = BB_HC + 1024
BB_END = BB_WP + 4 * 4096
BA_LNM = 0
BA_WINB = BA_LNM + NF * 2048
BA_UBF = BA_WINB + 2 * 1024
BA_SZS = BA_UBF + 4096
BA_WXB = BA_SZS + 2048
BA_DTR = BA_WXB + NJ * 96
BA_WDT = BA_DTR + 4096
BA_DTS = BA_WDT + NJ * 128
BA_WC = BB_WP
BA_URAW = BA_DTS + 4096
BA_EV = BA_URAW + 4100
BA_END = BA_EV + 2048
BW = max(BB_END, BA_END)

assert FW * 4 + BW * 2 <= 216200, (FW * 4, BW * 2)


ENGINE_SEM = {"vector": "v", "scalar": "s", "tensor": "mm", "gpsimd": "gp"}


class Prog:
    def __init__(self, selfwait=False):
        self.streams = {e: [] for e in ENGINES}
        self.counts = {}
        self.semnames = []
        self.selfwait = selfwait

    def sem(self, name):
        if name not in self.counts:
            self.counts[name] = 0
            self.semnames.append(name)

    def mark(self, name):
        self.sem(name)
        return self.counts[name]

    def emit(self, eng, fn, waits=(), incs=()):
        if self.selfwait and eng in ENGINE_SEM:
            es = ENGINE_SEM[eng]
            self.sem(es)
            waits = tuple(waits) + ((es, self.counts[es]),)
        waits = tuple((s, v) for (s, v) in waits if v > 0)
        for s, v in waits:
            self.sem(s)
        for s, d in incs:
            self.sem(s)
            self.counts[s] += d
        self.streams[eng].append((waits, fn, tuple(incs)))

    def dma(self, eng, dst_fn, src_fn, sem, waits=()):
        self.emit(eng, lambda e, d=dst_fn, s=src_fn: e.dma_start(d(), s()),
                  waits=waits, incs=[(sem, 16)])
        return self.mark(sem)


def _ap(t, off, dims):
    return bass.AP(t, off, [list(d) for d in dims])


def build_program(selfwait=False, detect=False, debug=False):
    nc = bass.Bass(target_bir_lowering=False, dynamic_dma_scratch_size=8192,
                   detect_race_conditions=detect)

    xT = nc.dram_tensor("xT", [DM, T], F32, kind="ExternalInput")
    winp = nc.dram_tensor("winp", [32 * P * 1024], F32,
                          kind="ExternalInput")
    wxp = nc.dram_tensor("wxp", [DI, 96], F32, kind="ExternalInput")
    wdt = nc.dram_tensor("wdt", [RNK, DI], F32, kind="ExternalInput")
    wout = nc.dram_tensor("wout", [4 * P * 16 * 256], F32,
                          kind="ExternalInput")
    w1t = nc.dram_tensor("w1t", [16 * P * 8 * 256], F32,
                         kind="ExternalInput")
    w2t = nc.dram_tensor("w2t", [16 * P * 8 * 256], F32,
                         kind="ExternalInput")
    h1t = nc.dram_tensor("h1t", [16 * P * 32 * 256], F32,
                         kind="ExternalInput")
    h2t = nc.dram_tensor("h2t", [16 * P * 32 * 256], F32,
                         kind="ExternalInput")
    w3t = nc.dram_tensor("w3t", [4 * P * 32 * 256], F32,
                         kind="ExternalInput")
    cvec = nc.dram_tensor("cvec", [P, 652], F32, kind="ExternalInput")

    y_out = nc.dram_tensor("y_out", [DM, T], F32, kind="ExternalOutput")

    uraw_scr = nc.dram_tensor("uraw_scr", [DI, T], BF16, kind="Internal")
    u_scr = nc.dram_tensor("u_scr", [DI, T], BF16, kind="Internal")
    dt_scr = nc.dram_tensor("dt_scr", [DI, T], BF16, kind="Internal")
    sz_scr = nc.dram_tensor("sz_scr", [DI, T], BF16, kind="Internal")
    bc_scr = nc.dram_tensor("bc_scr", [32, T], BF16, kind="Internal")
    st_scr = nc.dram_tensor("st_scr", [8, T], F32, kind="Internal")
    winb = nc.dram_tensor("winb", [32 * P * 1024], BF16, kind="Internal")
    wxpb = nc.dram_tensor("wxpb", [DI, 96], BF16, kind="Internal")
    wdtb = nc.dram_tensor("wdtb", [RNK, DI], BF16, kind="Internal")
    op16 = nc.dram_tensor("op16", [4 * DI * 256], BF16, kind="Internal")
    w1b = nc.dram_tensor("w1b", [16 * DM * 256], BF16, kind="Internal")
    w2b = nc.dram_tensor("w2b", [16 * DM * 256], BF16, kind="Internal")
    h1b = nc.dram_tensor("h1b", [16 * DH * 256], BF16, kind="Internal")
    h2b = nc.dram_tensor("h2b", [16 * DH * 256], BF16, kind="Internal")
    w3b = nc.dram_tensor("w3b", [4 * DH * 256], BF16, kind="Internal")
    if debug:
        dbg_y = nc.dram_tensor("dbg_y", [P, NJ * 512], F32,
                               kind="ExternalOutput")
        dbg_ym = nc.dram_tensor("dbg_ym", [P, NJ * 512], BF16,
                                kind="ExternalOutput")
        dbg_x1 = nc.dram_tensor("dbg_x1", [P, NF * 512], F32,
                                kind="ExternalOutput")
        dbg_ln1 = nc.dram_tensor("dbg_ln1", [P, NF * 512], BF16,
                                 kind="ExternalOutput")
        dbg_u = nc.dram_tensor("dbg_u", [P, NJ * 512], BF16,
                               kind="ExternalOutput")
        dbg_dt = nc.dram_tensor("dbg_dt", [P, NJ * 512], BF16,
                                kind="ExternalOutput")
        dbg_bb = nc.dram_tensor("dbg_bb", [P, 2048], BF16,
                                kind="ExternalOutput")
        dbg_bh = nc.dram_tensor("dbg_bh", [P, 2048], F32,
                                kind="ExternalOutput")
        dbg_an = nc.dram_tensor("dbg_an", [P, 256], F32,
                                kind="ExternalOutput")
        dbg_a = nc.dram_tensor("dbg_a", [P, 1024], F32,
                               kind="ExternalOutput")
        dbg_st = nc.dram_tensor("dbg_st", [P, 256], F32,
                                kind="ExternalOutput")

    pr = Prog(selfwait=selfwait)
    for name in ("v", "s", "mm", "gp"):
        pr.sem(name)

    with ExitStack() as ctx:
        c_sb = ctx.enter_context(nc.sbuf_tensor("c_sb", [P, 652], F32))
        aneg = ctx.enter_context(nc.sbuf_tensor("aneg", [P, NJ * NST], F32))
        hs1 = ctx.enter_context(nc.sbuf_tensor("hs1", [P, NH], F32))
        k12 = ctx.enter_context(nc.sbuf_tensor("k12", [P, NH], F32))
        state = ctx.enter_context(nc.sbuf_tensor("state", [P, NJ * NST], F32))
        fpool = ctx.enter_context(nc.sbuf_tensor("fpool", [P, FW], F32))
        bpool = ctx.enter_context(nc.sbuf_tensor("bpool", [P, BW], BF16))
        ps = [ctx.enter_context(nc.psum_tensor(f"ps{i}", [P, 512], F32))
              for i in range(8)]

        bank_guard = [("v", 0)] * 8

        C_CW, C_CB, C_DTB, C_AL = 0, 64, 80, 96
        C_D, C_NW, C_S1, C_S2 = 352, 368, 376, 408
        C_SH1, C_SH2, C_S3, C_ONE = 440, 472, 504, 512
        C_EPS, C_1PE, C_EPS2, C_NHALF, C_NONE, C_RDM = 513, 514, 515, 516, 517, 518
        C_OROW = 520

        def fp(off, w, p0=0, pn=P):
            return fpool[p0:p0 + pn, off:off + w]

        def bp(off, w, p0=0, pn=P):
            return bpool[p0:p0 + pn, off:off + w]

        def cs0(col):
            return c_sb[0:1, col:col + 1]

        def csP(col):
            return c_sb[:, col:col + 1]

        m_cv = pr.dma("sync", lambda: c_sb[:, :], lambda: cvec[:, :], "cvld")
        pr.emit("vector", lambda e: e.memset(state[:, :], 0.0),
                incs=[("v", 1)])
        pr.emit("vector", lambda e: e.memset(bp(BA_URAW, 4), 0.0),
                incs=[("v", 1)])
        pr.emit("scalar",
                lambda e: e.activation(aneg[:, :], c_sb[:, C_AL:C_AL + 256],
                                       AF.Exp),
                waits=[("cvld", m_cv)], incs=[("s", 1)])
        pr.emit("vector",
                lambda e: e.tensor_scalar(aneg[:, :], aneg[:, :], -1.0, None,
                                          ALU.mult),
                waits=[("s", pr.mark("s"))], incs=[("v", 1)])
        pr.emit("vector",
                lambda e: e.tensor_scalar(hs1[:, :], c_sb[:, C_S1:C_S1 + NH],
                                          0.5, None, ALU.mult),
                waits=[("cvld", m_cv)], incs=[("v", 1)])
        pr.emit("vector",
                lambda e: e.scalar_tensor_tensor(
                    k12[:, :], c_sb[:, C_S1:C_S1 + NH], 0.5,
                    c_sb[:, C_S2:C_S2 + NH], ALU.mult, ALU.mult),
                waits=[("cvld", m_cv)], incs=[("v", 1)])

        sqs = fp(FA_WC, 512)
        xg = [("mm", 0), ("mm", 0)]
        for piece in range(8):
            t0 = piece * 512
            for fi in range(NF):
                sl = fi % 2
                buf = fp(FA_XS + sl * 512, 512)
                mload = pr.dma(
                    "sync", lambda b=buf: b,
                    lambda fi=fi, t0=t0: xT[fi * P:(fi + 1) * P, t0:t0 + 512],
                    "xa%d" % sl, waits=[xg[sl]])
                pr.emit("scalar",
                        lambda e, b=buf: e.activation(sqs, b, AF.Square),
                        waits=[("xa%d" % sl, mload), ("mm", pr.mark("mm"))],
                        incs=[("s", 1)])
                msq = pr.mark("s")
                w0 = [("xa%d" % sl, mload)] + (
                    [bank_guard[0], ("cvld", m_cv)] if fi == 0 else [])
                pr.emit("tensor",
                        lambda e, b=buf, fi=fi: e.matmul(
                            ps[0][0:1, :], c_sb[:, C_ONE:C_ONE + 1], b,
                            start=(fi == 0), stop=(fi == NF - 1)),
                        waits=w0, incs=[("mm", 1)])
                w1 = [("s", msq)] + (
                    [bank_guard[1], ("cvld", m_cv)] if fi == 0 else [])
                pr.emit("tensor",
                        lambda e, fi=fi: e.matmul(
                            ps[1][0:1, :], c_sb[:, C_ONE:C_ONE + 1], sqs,
                            start=(fi == 0), stop=(fi == NF - 1)),
                        waits=w1, incs=[("mm", 1)])
                xg[sl] = ("mm", pr.mark("mm"))
            m_stats = pr.mark("mm")
            mrow = fp(FA_SR, 512, 0, 1)
            vrow = fp(FA_SR + 512, 512, 0, 1)
            rrow = fp(FA_SR + 1024, 512, 0, 1)
            pr.emit("scalar",
                    lambda e: e.activation(mrow, ps[0][0:1, :], AF.Copy,
                                           scale=cs0(C_RDM)),
                    waits=[("mm", m_stats), ("spst", pr.mark("spst"))],
                    incs=[("s", 1)])
            pr.emit("scalar",
                    lambda e: e.activation(vrow, ps[1][0:1, :], AF.Copy,
                                           scale=cs0(C_RDM)),
                    incs=[("s", 1)])
            bank_guard[0] = ("s", pr.mark("s"))
            bank_guard[1] = ("s", pr.mark("s"))
            m_mv = pr.mark("s")
            pr.emit("scalar", lambda e: e.activation(rrow, mrow, AF.Square),
                    waits=[("spst", pr.mark("spst"))], incs=[("s", 1)])
            pr.emit("vector",
                    lambda e: e.tensor_tensor(vrow, vrow, rrow, ALU.subtract),
                    waits=[("s", pr.mark("s"))], incs=[("v", 1)])
            m_var = pr.mark("v")
            pr.dma("sync", lambda t0=t0: st_scr[0:1, t0:t0 + 512],
                   lambda: mrow, "spst", waits=[("s", m_mv)])
            pr.emit("scalar",
                    lambda e: e.activation(rrow, vrow, AF.Ln,
                                           bias=cs0(C_EPS)),
                    waits=[("v", m_var)], incs=[("s", 1)])
            pr.emit("scalar",
                    lambda e: e.activation(rrow, rrow, AF.Exp,
                                           scale=cs0(C_NHALF)),
                    incs=[("s", 1)])
            pr.dma("sync", lambda t0=t0: st_scr[1:2, t0:t0 + 512],
                   lambda: rrow, "spst", waits=[("s", pr.mark("s"))])
            m_r1w = pr.mark("spst")
            pr.emit("scalar",
                    lambda e: e.activation(rrow, vrow, AF.Ln,
                                           scale=cs0(C_1PE),
                                           bias=cs0(C_EPS2)),
                    waits=[("spst", m_r1w)], incs=[("s", 1)])
            pr.emit("scalar",
                    lambda e: e.activation(rrow, rrow, AF.Exp,
                                           scale=cs0(C_NHALF)),
                    incs=[("s", 1)])
            pr.dma("sync", lambda t0=t0: st_scr[2:3, t0:t0 + 512],
                   lambda: rrow, "spst", waits=[("s", pr.mark("s"))])
        m_stats_done = pr.mark("spst")

        def a3_ap(t, blk, nkt):
            dims = [[nkt * 256, P]]
            nch = (nkt * 256) // 2048
            if nch > 1:
                dims.append([2048, nch])
            dims.append([1, min(2048, nkt * 256)])
            return _ap(t, blk * P * nkt * 256, dims)

        a3_items = [(lambda: wdtb[:, :], lambda: wdt[:, :]),
                    (lambda: _ap(wxpb, 0, [[2048, 96], [1, 2048]]),
                     lambda: _ap(wxp, 0, [[2048, 96], [1, 2048]]))]
        for c in range(4):
            a3_items.append((
                lambda c=c: _ap(winb, c * P * 8192,
                                [[8192, P], [2048, 4], [1, 2048]]),
                lambda c=c: _ap(winp, c * P * 8192,
                                [[8192, P], [2048, 4], [1, 2048]])))
        for (srct, blob, nblk, K) in ((wout, op16, 4, DI), (w1t, w1b, 16, DM),
                                      (w2t, w2b, 16, DM), (h1t, h1b, 16, DH),
                                      (h2t, h2b, 16, DH), (w3t, w3b, 4, DH)):
            nkt = K // P
            for blk in range(nblk):
                a3_items.append((
                    lambda blob=blob, blk=blk, nkt=nkt: a3_ap(blob, blk, nkt),
                    lambda srct=srct, blk=blk, nkt=nkt: a3_ap(srct, blk,
                                                              nkt)))
        a3_prev = [0, 0]
        for i, (dfn, sfn) in enumerate(a3_items):
            r = i % 2
            w = [("a3_%d" % r, a3_prev[r])] if a3_prev[r] else []
            a3_prev[r] = pr.dma("gpsimd", dfn, sfn, "a3_%d" % r, waits=w)
        m_a3_join = (("a3_0", a3_prev[0]), ("a3_1", a3_prev[1]))

        pr.emit("sync", lambda e: None, waits=list(m_a3_join))
        ev_guard = ("spev", 0)
        szs_guard = ("spsz", 0)
        for th in range(2):
            t0h = th * 2048
            mb = fp(FA_CVA, 2048)
            rb = fp(FA_CVB, 2048)
            pr.dma("sync", lambda: mb,
                   lambda t0h=t0h: _ap(st_scr, t0h, [[0, P], [1, 2048]]),
                   "bca", waits=[("spst", m_stats_done),
                                 ("v", pr.mark("v")), ("s", pr.mark("s"))])
            m_bc = pr.dma("sync", lambda: rb,
                          lambda t0h=t0h: _ap(st_scr, 2 * T + t0h,
                                              [[0, P], [1, 2048]]),
                          "bca")
            for fi in range(NF):
                for pc in range(4):
                    sl = (fi * 4 + pc) % 2
                    buf = fp(FA_XS + sl * 512, 512)
                    mload = pr.dma(
                        "sync", lambda b=buf: b,
                        lambda fi=fi, t0h=t0h, pc=pc: xT[
                            fi * P:(fi + 1) * P,
                            t0h + pc * 512:t0h + pc * 512 + 512],
                        "xa%d" % sl, waits=[xg[sl]])
                    pr.emit("vector",
                            lambda e, b=buf, pc=pc: e.tensor_tensor(
                                fp(FA_URAW + 4, 512), b,
                                fp(FA_CVA + pc * 512, 512), ALU.subtract),
                            waits=[("xa%d" % sl, mload), ("bca", m_bc)],
                            incs=[("v", 1)])
                    pr.emit("vector",
                            lambda e, fi=fi, pc=pc: e.scalar_tensor_tensor(
                                bp(BA_LNM + fi * 2048 + pc * 512, 512),
                                fp(FA_URAW + 4, 512),
                                c_sb[:, C_NW + fi:C_NW + fi + 1],
                                fp(FA_CVB + pc * 512, 512),
                                ALU.mult, ALU.mult),
                            incs=[("v", 1)])
                    xg[sl] = ("v", pr.mark("v"))
            m_lnm = pr.mark("v")
            wb_guard = [("mm", 0), ("mm", 0)]
            for dtile in range(32):
                ws = dtile % 2
                mw = pr.dma(
                    "sync", lambda ws=ws: bp(BA_WINB + ws * 1024, 1024),
                    lambda d=dtile: _ap(winb, d * P * 1024,
                                        [[1024, P], [1, 1024]]),
                    "wia%d" % ws, waits=[wb_guard[ws]])
                for fi in range(NF):
                    for tsb in range(4):
                        w = []
                        if fi == 0:
                            w = [("wia%d" % ws, mw), ("v", m_lnm),
                                 bank_guard[tsb]]
                        pr.emit(
                            "tensor",
                            lambda e, fi=fi, tsb=tsb, ws=ws: e.matmul(
                                ps[tsb][:, :],
                                bp(BA_WINB + ws * 1024 + fi * P, P),
                                bp(BA_LNM + fi * 2048 + tsb * 512, 512),
                                start=(fi == 0), stop=(fi == NF - 1)),
                            waits=w, incs=[("mm", 1)])
                m_grp = pr.mark("mm")
                wb_guard[ws] = ("mm", m_grp)
                if dtile < NJ:
                    for tsb in range(4):
                        pr.emit("scalar",
                                lambda e, tsb=tsb: e.activation(
                                    bp(BA_EV + tsb * 512, 512),
                                    ps[tsb][:, :], AF.Copy),
                                waits=[("mm", m_grp), ev_guard]
                                if tsb == 0 else [("mm", m_grp)],
                                incs=[("s", 1)])
                        bank_guard[tsb] = ("s", pr.mark("s"))
                    m_ev = pr.mark("s")
                    pr.dma("sync",
                           lambda d=dtile, t0h=t0h: uraw_scr[
                               d * P:(d + 1) * P, t0h:t0h + 2048],
                           lambda: bp(BA_EV, 2048), "spev",
                           waits=[("s", m_ev)])
                    ev_guard = ("spev", pr.mark("spev"))
                else:
                    for tsb in range(4):
                        pr.emit("scalar",
                                lambda e, tsb=tsb: e.activation(
                                    bp(BA_SZS + tsb * 512, 512),
                                    ps[tsb][:, :], AF.Silu),
                                waits=[("mm", m_grp), szs_guard]
                                if tsb == 0 else [("mm", m_grp)],
                                incs=[("s", 1)])
                        bank_guard[tsb] = ("s", pr.mark("s"))
                    m_ev = pr.mark("s")
                    j = dtile - NJ
                    pr.dma("sync",
                           lambda j=j, t0h=t0h: sz_scr[
                               j * P:(j + 1) * P, t0h:t0h + 2048],
                           lambda: bp(BA_SZS, 2048), "spsz",
                           waits=[("s", m_ev)])
                    szs_guard = ("spsz", pr.mark("spsz"))
        m_a1a = pr.mark("spev")

        for j in range(NJ):
            pr.dma("sync", lambda j=j: bp(BA_WXB + j * 96, 96),
                   lambda j=j: wxpb[j * P:(j + 1) * P, :], "wlb")
        pr.dma("sync", lambda: bp(BA_WDT, NJ * P, 0, 64),
               lambda: wdtb[:, :], "wlb")

        uraw_guard = ("v", 0)
        for j in range(NJ):
            mload = pr.dma("sync", lambda: bp(BA_URAW + 4, T),
                           lambda j=j: uraw_scr[j * P:(j + 1) * P, :],
                           "url", waits=[("spev", m_a1a), uraw_guard])
            m_pc = []
            for pc2 in range(2):
                o = pc2 * 2048
                pr.emit("vector",
                        lambda e, j=j, o=o: e.tensor_scalar(
                            fp(FA_CVA, 2048), bp(BA_URAW + 1 + o, 2048),
                            c_sb[:, C_CW + j * 4:C_CW + j * 4 + 1], None,
                            ALU.mult),
                        waits=[("url", mload), ("s", pr.mark("s"))],
                        incs=[("v", 1)])
                pr.emit("vector",
                        lambda e, j=j, o=o: e.scalar_tensor_tensor(
                            fp(FA_CVB, 2048), bp(BA_URAW + 2 + o, 2048),
                            c_sb[:, C_CW + j * 4 + 1:C_CW + j * 4 + 2],
                            fp(FA_CVA, 2048), ALU.mult, ALU.add),
                        incs=[("v", 1)])
                pr.emit("vector",
                        lambda e, j=j, o=o: e.scalar_tensor_tensor(
                            fp(FA_CVA, 2048), bp(BA_URAW + 3 + o, 2048),
                            c_sb[:, C_CW + j * 4 + 2:C_CW + j * 4 + 3],
                            fp(FA_CVB, 2048), ALU.mult, ALU.add),
                        incs=[("v", 1)])
                pr.emit("vector",
                        lambda e, j=j, o=o: e.scalar_tensor_tensor(
                            fp(FA_CVB, 2048), bp(BA_URAW + 4 + o, 2048),
                            c_sb[:, C_CW + j * 4 + 3:C_CW + j * 4 + 4],
                            fp(FA_CVA, 2048), ALU.mult, ALU.add),
                        incs=[("v", 1)])
                m_conv = pr.mark("v")
                pr.emit("scalar",
                        lambda e, j=j, o=o: e.activation(
                            bp(BA_UBF + o, 2048), fp(FA_CVB, 2048), AF.Silu,
                            bias=c_sb[:, C_CB + j:C_CB + j + 1]),
                        waits=[("v", m_conv), ("mm", pr.mark("mm")),
                               ("spu", pr.mark("spu"))],
                        incs=[("s", 1)])
                m_pc.append(pr.mark("s"))
            uraw_guard = ("s", m_pc[-1])
            m_u = m_pc[-1]
            pr.dma("sync", lambda j=j: u_scr[j * P:(j + 1) * P, :],
                   lambda: bp(BA_UBF, T), "spu", waits=[("s", m_u)])
            for tsn in range(8):
                bank = 4 + (tsn % 2)
                w = [("s", m_u), bank_guard[bank]]
                pr.emit("tensor",
                        lambda e, j=j, tsn=tsn, b=bank: e.matmul(
                            ps[b][0:96, :], bp(BA_WXB + j * 96, 96),
                            bp(BA_UBF + tsn * 512, 512),
                            start=True, stop=True),
                        waits=w, incs=[("mm", 1)])
                m_mmx = pr.mark("mm")
                if j == 0:
                    pr.emit("vector",
                            lambda e, tsn=tsn, b=bank: e.tensor_copy(
                                fp(FA_XDB + tsn * 512, 512, 0, 96),
                                ps[b][0:96, :]),
                            waits=[("mm", m_mmx)], incs=[("v", 1)])
                else:
                    pr.emit("vector",
                            lambda e, tsn=tsn, b=bank: e.tensor_tensor(
                                fp(FA_XDB + tsn * 512, 512, 0, 96),
                                fp(FA_XDB + tsn * 512, 512, 0, 96),
                                ps[b][0:96, :], ALU.add),
                            waits=[("mm", m_mmx)], incs=[("v", 1)])
                bank_guard[bank] = ("v", pr.mark("v"))
        m_a1b = pr.mark("v")

        pr.emit("vector",
                lambda e: e.tensor_copy(bp(BA_DTS, T, 0, 32),
                                        fp(FA_XDB, T, 64, 32)),
                waits=[("v", m_a1b)], incs=[("v", 1)])
        m_bccast = pr.mark("v")
        pr.dma("sync", lambda: bc_scr[:, :],
               lambda: bp(BA_DTS, T, 0, 32), "spbc",
               waits=[("v", m_bccast)])
        m_bcw = pr.mark("spbc")
        pr.emit("vector",
                lambda e: e.tensor_copy(bp(BA_DTR, T, 0, 64),
                                        fp(FA_XDB, T, 0, 64)),
                waits=[("v", m_a1b)], incs=[("v", 1)])
        m_dtr = pr.mark("v")
        dts_guard = ("spbc", m_bcw)
        for j in range(NJ):
            for tsn in range(8):
                bank = tsn % 2
                pr.emit("tensor",
                        lambda e, j=j, tsn=tsn, b=bank: e.matmul(
                            ps[b][:, :], bp(BA_WDT + j * P, P, 0, 64),
                            bp(BA_DTR + tsn * 512, 512, 0, 64),
                            start=True, stop=True),
                        waits=[("v", m_dtr), bank_guard[bank]],
                        incs=[("mm", 1)])
                m_mmd = pr.mark("mm")
                col = tsn * 512
                pr.emit("scalar",
                        lambda e, b=bank, j=j: e.activation(
                            fp(FA_CVA, 512), ps[b][:, :], AF.Abs,
                            bias=c_sb[:, C_DTB + j:C_DTB + j + 1]),
                        waits=[("mm", m_mmd), ("v", pr.mark("v"))],
                        incs=[("s", 1)])
                pr.emit("scalar",
                        lambda e: e.activation(fp(FA_CVA, 512),
                                               fp(FA_CVA, 512), AF.Exp,
                                               scale=csP(C_NONE)),
                        incs=[("s", 1)])
                pr.emit("scalar",
                        lambda e: e.activation(fp(FA_CVA, 512),
                                               fp(FA_CVA, 512), AF.Ln,
                                               bias=1.0),
                        incs=[("s", 1)])
                pr.emit("scalar",
                        lambda e, b=bank, j=j: e.activation(
                            fp(FA_CVB, 512), ps[b][:, :], AF.Relu,
                            bias=c_sb[:, C_DTB + j:C_DTB + j + 1]),
                        incs=[("s", 1)])
                bank_guard[bank] = ("s", pr.mark("s"))
                m_sp4 = pr.mark("s")
                pr.emit("vector",
                        lambda e, col=col: e.tensor_tensor(
                            bp(BA_DTS + col, 512), fp(FA_CVA, 512),
                            fp(FA_CVB, 512), ALU.add),
                        waits=[("s", m_sp4), dts_guard], incs=[("v", 1)])
            m_dtj = pr.mark("v")
            pr.dma("sync", lambda j=j: dt_scr[j * P:(j + 1) * P, :],
                   lambda: bp(BA_DTS, T), "spdt", waits=[("v", m_dtj)])
            dts_guard = ("spdt", pr.mark("spdt"))

        m_spu = pr.mark("spu")
        m_spdt = pr.mark("spdt")
        m_spsz = pr.mark("spsz")

        u_marks = {}
        dt_marks = {}
        sz_marks = {}
        u_guard = [("v", 0)] * NJ
        dt_guard = [("s", 0)] * NJ
        sz_guard = [("v", 0)] * 2
        bb_guard = [("spbc", m_bcw)] * 2
        cb_guard = [("spbc", m_bcw)] * 2
        a_guard = [("v", 0)] * 4
        hc_guard = [("gp", 0)]
        ff_guard = [("o%d" % fi, 0) for fi in range(NF)]
        wp_guard = [("mm", 0)] * 4
        m_outproj = [0] * (NCH + 1)
        m_lnx1_dbg = [0] * (NCH + 1)
        m_lnx1_s = [0]
        m_y0 = [0]
        m_w = [0]
        unit_rr = [0]
        dbg_gate = [None, None]
        m_ymix = [0] * (NCH + 1)
        m_scan_end_v = [0] * (NCH + 1)
        m_scan_end_s = [0] * (NCH + 1)

        def u_off(j):
            return BB_U + j * 512

        def dt_off(j):
            return BB_DT + j * 512

        def emit_loads(g):
            gw = [dbg_gate[1]] if dbg_gate[1] else []
            dbg_gate[1] = None
            for j in range(NJ):
                u_marks[(g, j)] = pr.dma(
                    "sync", lambda j=j: bp(u_off(j), 512),
                    lambda g=g, j=j: u_scr[j * P:(j + 1) * P,
                                           g * TC:(g + 1) * TC],
                    "u%d" % j, waits=[u_guard[j]] + gw)
                dt_marks[(g, j)] = pr.dma(
                    "sync", lambda j=j: bp(dt_off(j), 512),
                    lambda g=g, j=j: dt_scr[j * P:(j + 1) * P,
                                            g * TC:(g + 1) * TC],
                    "d%d" % j, waits=[dt_guard[j]])
            for j in range(2):
                sz_marks[(g, j)] = pr.dma(
                    "sync", lambda j=j: bp(BB_SZ + (j % 2) * 512, 512),
                    lambda g=g, j=j: sz_scr[j * P:(j + 1) * P,
                                            g * TC:(g + 1) * TC],
                    "z%d" % (j % 2), waits=[sz_guard[j % 2]])

        def emit_scan_head(g):
            prev_ymix = m_ymix[g - 1] if g > 0 else 0
            for j in range(NJ):
                pr.emit("scalar",
                        lambda e, j=j: e.activation(
                            fp(FB_Y + j * 512, 512), bp(u_off(j), 512),
                            AF.Copy, scale=c_sb[:, C_D + j:C_D + j + 1]),
                        waits=[("u%d" % j, u_marks[(g, j)]),
                               ("v", prev_ymix)],
                        incs=[("s", 1)])
            m_y0[0] = pr.mark("s")
            for j in range(NJ):
                pr.emit("vector",
                        lambda e, j=j: e.tensor_tensor(
                            bp(u_off(j), 512), bp(u_off(j), 512),
                            bp(dt_off(j), 512), ALU.mult),
                        waits=[("u%d" % j, u_marks[(g, j)]),
                               ("d%d" % j, dt_marks[(g, j)]),
                               ("s", m_y0[0])],
                        incs=[("v", 1)])
            m_w[0] = pr.mark("v")
            if debug and g == 0:
                pr.dma("sync", lambda: dbg_st[:, :], lambda: state[:, :],
                       "dbgs", waits=[("v", m_w[0])])
                pr.dma("sync", lambda: dbg_an[:, :], lambda: aneg[:, :],
                       "dbgs")

        def emit_scan_q(g, q):
            qs = q % 2
            bbo = BB_BB + qs * 1024
            cbo = BB_CB + qs * 1024
            mq_b = pr.dma(
                "scalar", lambda o=bbo: bp(o, 1024),
                lambda g=g, q=q: _ap(
                    bc_scr, (q * 2) * T + g * TC,
                    [[0, P], [T, 2], [1, TC]]),
                "bq%d" % qs, waits=[bb_guard[qs]])
            mq_c = pr.dma(
                "scalar", lambda o=cbo: bp(o, 1024),
                lambda g=g, q=q: _ap(
                    bc_scr, (16 + q * 2) * T + g * TC,
                    [[0, P], [T, 2], [1, TC]]),
                "cq%d" % qs, waits=[cb_guard[qs]])
            for j in range(NJ):
                pr.emit("vector",
                        lambda e, j=j, bbo=bbo: e.tensor_tensor(
                            bp(BB_B, 1024),
                            _ap(bpool, u_off(j),
                                [[BW, P], [0, 2], [1, TC]]),
                            bp(bbo, 1024), ALU.mult),
                        waits=[("bq%d" % qs, mq_b), ("v", m_w[0])],
                        incs=[("v", 1)])
                for nn in range(2):
                    n = q * 2 + nn
                    asl = (j * 2 + nn) % 3
                    pr.emit("scalar",
                            lambda e, g=g, j=j, n=n, asl=asl:
                            e.activation(
                                fp(FB_A + asl * 512, 512),
                                bp(dt_off(j), 512),
                                AF.Exp,
                                scale=aneg[:, j * NST + n:
                                           j * NST + n + 1]),
                            waits=[("d%d" % j, dt_marks[(g, j)]),
                                   a_guard[asl]],
                            incs=[("s", 1)])
                    m_a = pr.mark("s")
                    pr.emit("vector",
                            lambda e, j=j, n=n, nn=nn, asl=asl:
                            e.tensor_tensor_scan(
                                bp(BB_H + nn * 512, 512),
                                fp(FB_A + asl * 512, 512),
                                bp(BB_B + nn * 512, 512),
                                state[:, j * NST + n:
                                      j * NST + n + 1],
                                ALU.mult, ALU.add),
                            waits=[("s", m_a)],
                            incs=[("v", 1)])
                    a_guard[asl] = ("v", pr.mark("v"))
                pr.emit("vector",
                        lambda e, cbo=cbo: e.tensor_tensor(
                            bp(BB_HC, 1024), bp(BB_H, 1024),
                            bp(cbo, 1024), ALU.mult),
                        waits=[("cq%d" % qs, mq_c), hc_guard[0]],
                        incs=[("v", 1)])
                m_hc = pr.mark("v")
                for nn in range(2):
                    n = q * 2 + nn
                    pr.emit("vector",
                            lambda e, j=j, n=n, nn=nn: e.tensor_copy(
                                state[:, j * NST + n:j * NST + n + 1],
                                bp(BB_H + nn * 512 + 511, 1)),
                            incs=[("v", 1)])
                if debug and g == 0 and q == 0 and j == 0:
                    pr.dma("sync", lambda: dbg_bb[:, 0:1024],
                           lambda: bp(BB_BB, 1024), "dbgs",
                           waits=[("v", m_hc)])
                    pr.dma("sync", lambda: dbg_bb[:, 1024:2048],
                           lambda: bp(BB_CB, 1024), "dbgs")
                    dbg_gate[0] = ("dbgs", pr.mark("dbgs"))
                for nn in range(2):
                    pr.emit("vector",
                            lambda e, j=j, nn=nn: e.tensor_tensor(
                                fp(FB_Y + j * 512, 512),
                                fp(FB_Y + j * 512, 512),
                                bp(BB_HC + nn * 512, 512),
                                ALU.add),
                            waits=[("s", m_y0[0])],
                            incs=[("v", 1)])
                hc_guard[0] = ("v", pr.mark("v"))
            bb_guard[qs] = ("v", pr.mark("v"))
            cb_guard[qs] = ("v", pr.mark("v"))

        def emit_scan_tail(g):
            m_adds = pr.mark("gp")
            for j in range(NJ):
                w = [("gp", m_adds), ("z%d" % (j % 2), sz_marks[(g, j)]),
                     ("mm", m_outproj[g - 1] if g > 0 else 0)]
                pr.emit("vector",
                        lambda e, g=g, j=j: e.tensor_tensor(
                            bp(BB_YM + j * 512, 512),
                            fp(FB_Y + j * 512, 512),
                            bp(BB_SZ + (j % 2) * 512, 512), ALU.mult),
                        waits=w, incs=[("v", 1)])
                sz_guard[j % 2] = ("v", pr.mark("v"))
                if j + 2 < NJ:
                    sz_marks[(g, j + 2)] = pr.dma(
                        "sync",
                        lambda j=j: bp(BB_SZ + (j % 2) * 512, 512),
                        lambda g=g, j=j: sz_scr[(j + 2) * P:(j + 3) * P,
                                                g * TC:(g + 1) * TC],
                        "z%d" % (j % 2), waits=[sz_guard[j % 2]])
            m_ymix[g] = pr.mark("v")
            for j in range(NJ):
                u_guard[j] = ("v", m_ymix[g])
                dt_guard[j] = ("s", pr.mark("s"))

        def wp_load(unit, blob, off_elems, kcnt, nkt, extra_waits=()):
            m = pr.dma(
                "sync", lambda u=unit, kcnt=kcnt: bp(BB_WP + u * 4096,
                                                     kcnt * 256),
                lambda blob=blob, o=off_elems, kcnt=kcnt, nkt=nkt: _ap(
                    blob, o, [[nkt * 256, P], [1, kcnt * 256]]),
                "w%d" % unit, waits=[wp_guard[unit]] + list(extra_waits))
            return m

        def emit_back_head(g):
            t0 = g * TC
            mb1 = pr.dma("sync", lambda: fp(FB_BC1, 512),
                         lambda g=g: _ap(st_scr, t0, [[0, P], [1, TC]]),
                         "bc1", waits=[("v", pr.mark("v")),
                                       ("spst", m_stats_done)])
            rb1 = pr.dma("sync", lambda: fp(FB_BC1 + 512, 512),
                         lambda g=g: _ap(st_scr, T + t0, [[0, P], [1, TC]]),
                         "bc1")
            for blk in range(4):
                unit = unit_rr[0] % 4
                unit_rr[0] += 1
                mw = wp_load(unit, op16, blk * P * 16 * 256, NJ, 16)
                for mt in range(2):
                    fi = blk * 2 + mt
                    bank = fi % 4
                    for j in range(NJ):
                        w = []
                        if j == 0:
                            w = [("w%d" % unit, mw), ("v", m_ymix[g]),
                                 bank_guard[bank]]
                        pr.emit("tensor",
                                lambda e, mt=mt, j=j, b=bank, u=unit:
                                e.matmul(
                                    ps[b][:, :],
                                    bp(BB_WP + u * 4096 + j * 256 + mt * P,
                                       P),
                                    bp(BB_YM + j * 512, 512),
                                    start=(j == 0), stop=(j == NJ - 1)),
                                waits=w, incs=[("mm", 1)])
                    m_op = pr.mark("mm")
                    xsl = 0
                    mxf = pr.dma(
                        "sync", lambda xsl=xsl: fp(FB_XF + xsl * 512, 512),
                        lambda g=g, fi=fi: xT[fi * P:(fi + 1) * P,
                                              t0:t0 + TC],
                        "x%d" % xsl, waits=[("v", pr.mark("v"))] if fi >= 1
                        else [("v", m_ymix[g])])
                    pr.emit("vector",
                            lambda e, xsl=xsl: e.tensor_tensor(
                                fp(FB_TS, 512), fp(FB_XF + xsl * 512, 512),
                                fp(FB_BC1, 512), ALU.subtract),
                            waits=[("x%d" % xsl, mxf), ("bc1", rb1),
                                   ("mm", m_op), ("s", pr.mark("s"))],
                            incs=[("v", 1)])
                    pr.emit("vector",
                            lambda e: e.tensor_tensor(
                                fp(FB_TS, 512), fp(FB_TS, 512),
                                fp(FB_BC1 + 512, 512), ALU.mult),
                            incs=[("v", 1)])
                    pr.emit("vector",
                            lambda e, xsl=xsl: e.tensor_tensor(
                                fp(FB_TS, 512), fp(FB_TS, 512),
                                fp(FB_XF + xsl * 512, 512), ALU.add),
                            incs=[("v", 1)])
                    pr.emit("vector",
                            lambda e, fi=fi, b=bank: e.tensor_tensor(
                                fp(FB_X1 + fi * 512, 512), fp(FB_TS, 512),
                                ps[b][:, :], ALU.add),
                            incs=[("v", 1)])
                    bank_guard[bank] = ("v", pr.mark("v"))
                wp_guard[unit] = ("mm", m_op)
            m_outproj[g] = pr.mark("mm")
            m_x1 = pr.mark("v")

            for fi in range(NF):
                pr.emit("scalar",
                        lambda e, fi=fi: e.activation(
                            fp(FB_TS, 512), fp(FB_X1 + fi * 512, 512),
                            AF.Square),
                        waits=[("v", m_x1), ("mm", pr.mark("mm"))],
                        incs=[("s", 1)])
                msq = pr.mark("s")
                w6 = [("v", m_x1)] + ([bank_guard[6]] if fi == 0 else [])
                pr.emit("tensor",
                        lambda e, fi=fi: e.matmul(
                            ps[6][0:1, :], c_sb[:, C_ONE:C_ONE + 1],
                            fp(FB_X1 + fi * 512, 512),
                            start=(fi == 0), stop=(fi == NF - 1)),
                        waits=w6, incs=[("mm", 1)])
                w7 = [("s", msq)] + ([bank_guard[7]] if fi == 0 else [])
                pr.emit("tensor",
                        lambda e, fi=fi: e.matmul(
                            ps[7][0:1, :], c_sb[:, C_ONE:C_ONE + 1],
                            fp(FB_TS, 512),
                            start=(fi == 0), stop=(fi == NF - 1)),
                        waits=w7, incs=[("mm", 1)])
            m_st2 = pr.mark("mm")
            m2row = fp(FB_SR, 512, 0, 1)
            r2row = fp(FB_SR + 512, 512, 0, 1)
            pr.emit("scalar",
                    lambda e: e.activation(m2row, ps[6][0:1, :], AF.Copy,
                                           scale=cs0(C_RDM)),
                    waits=[("mm", m_st2), ("st2", pr.mark("st2"))],
                    incs=[("s", 1)])
            pr.emit("scalar",
                    lambda e: e.activation(r2row, ps[7][0:1, :], AF.Copy,
                                           scale=cs0(C_RDM)),
                    incs=[("s", 1)])
            bank_guard[6] = ("s", pr.mark("s"))
            bank_guard[7] = ("s", pr.mark("s"))
            pr.emit("scalar",
                    lambda e: e.activation(fp(FB_TS, 512, 0, 1), m2row,
                                           AF.Square),
                    incs=[("s", 1)])
            m_s2 = pr.mark("s")
            pr.emit("vector",
                    lambda e: e.tensor_tensor(r2row, r2row,
                                              fp(FB_TS, 512, 0, 1),
                                              ALU.subtract),
                    waits=[("s", m_s2)], incs=[("v", 1)])
            m_v2 = pr.mark("v")
            pr.emit("scalar",
                    lambda e: e.activation(r2row, r2row, AF.Ln,
                                           bias=cs0(C_EPS)),
                    waits=[("v", m_v2)], incs=[("s", 1)])
            pr.emit("scalar",
                    lambda e: e.activation(r2row, r2row, AF.Exp,
                                           scale=cs0(C_NHALF)),
                    incs=[("s", 1)])
            m_rr2 = pr.mark("s")
            pr.emit("tensor",
                    lambda e: e.matmul(ps[6][:, :],
                                       c_sb[0:1, C_OROW:C_OROW + P],
                                       m2row, start=True, stop=True),
                    waits=[("s", m_rr2), bank_guard[6]], incs=[("mm", 1)])
            pr.emit("tensor",
                    lambda e: e.matmul(ps[7][:, :],
                                       c_sb[0:1, C_OROW:C_OROW + P],
                                       r2row, start=True, stop=True),
                    waits=[("s", m_rr2), bank_guard[7]], incs=[("mm", 1)])
            m_bc2 = pr.mark("mm")
            for fi in range(NF):
                pr.emit("vector",
                        lambda e, fi=fi: e.tensor_tensor(
                            fp(FB_TS, 512), fp(FB_X1 + fi * 512, 512),
                            ps[6][:, :], ALU.subtract),
                        waits=[("mm", m_bc2)], incs=[("v", 1)])
                pr.emit("vector",
                        lambda e, fi=fi: e.tensor_tensor(
                            bp(BB_LN1 + fi * 512, 512), fp(FB_TS, 512),
                            ps[7][:, :], ALU.mult),
                        incs=[("v", 1)])
            bank_guard[6] = ("v", pr.mark("v"))
            bank_guard[7] = ("v", pr.mark("v"))
            m_lnx1_s[0] = pr.mark("v")
            m_lnx1_dbg[g] = m_lnx1_s[0]


        def gen_back_ffn(g):
            t0 = g * TC
            m_lnx1 = m_lnx1_s[0]
            for hf in range(2):
                hoff = hf * 256
                for blk in range(16):
                    unit = unit_rr[0] % 4
                    unit_rr[0] += 1
                    mw1 = pr.dma(
                        "sync", lambda u=unit: bp(BB_WP + u * 4096,
                                                  NF * 256),
                        lambda blk=blk: _ap(w1b, blk * P * 8 * 256,
                                            [[8 * 256, P], [1, NF * 256]]),
                        "w%d" % unit, waits=[wp_guard[unit]])
                    mw2 = pr.dma(
                        "sync", lambda u=unit: bp(BB_WP + u * 4096 +
                                                  NF * 256, NF * 256),
                        lambda blk=blk: _ap(w2b, blk * P * 8 * 256,
                                            [[8 * 256, P], [1, NF * 256]]),
                        "w%d" % unit)
                    for mt in range(2):
                        ht = blk * 2 + mt
                        b1 = (ht % 3) * 2
                        b2 = b1 + 1
                        for k in range(NF):
                            w = []
                            if k == 0:
                                w = [("w%d" % unit, mw2), ("v", m_lnx1),
                                     bank_guard[b1]]
                            pr.emit("tensor",
                                    lambda e, k=k, mt=mt, b=b1, u=unit,
                                    hoff=hoff: e.matmul(
                                        ps[b][:, 0:256],
                                        bp(BB_WP + u * 4096 + k * 256 +
                                           mt * P, P),
                                        bp(BB_LN1 + k * 512 + hoff, 256),
                                        start=(k == 0), stop=(k == NF - 1)),
                                    waits=w, incs=[("mm", 1)])
                        for k in range(NF):
                            w = []
                            if k == 0:
                                w = [("w%d" % unit, mw2), bank_guard[b2]]
                            pr.emit("tensor",
                                    lambda e, k=k, mt=mt, b=b2, u=unit,
                                    hoff=hoff: e.matmul(
                                        ps[b][:, 0:256],
                                        bp(BB_WP + u * 4096 + NF * 256 +
                                           k * 256 + mt * P, P),
                                        bp(BB_LN1 + k * 512 + hoff, 256),
                                        start=(k == 0), stop=(k == NF - 1)),
                                    waits=w, incs=[("mm", 1)])
                        m_a12 = pr.mark("mm")
                        pr.emit("scalar",
                                lambda e, ht=ht, b=b1: e.activation(
                                    fp(FB_TS2, 256), ps[b][:, 0:256],
                                    AF.Tanh, scale=hs1[:, ht:ht + 1]),
                                waits=[("mm", m_a12), ("v", pr.mark("v"))],
                                incs=[("s", 1)])
                        m_t = pr.mark("s")
                        pr.emit("vector",
                                lambda e, b=b1: e.scalar_tensor_tensor(
                                    fp(FB_TS2, 256), fp(FB_TS2, 256), 1.0,
                                    ps[b][:, 0:256], ALU.add, ALU.mult),
                                waits=[("s", m_t)], incs=[("v", 1)])
                        bank_guard[b1] = ("v", pr.mark("v"))
                        pr.emit("vector",
                                lambda e, ht=ht, b=b2:
                                e.scalar_tensor_tensor(
                                    bp(BB_G + ht * 256, 256),
                                    fp(FB_TS2, 256),
                                    k12[:, ht:ht + 1], ps[b][:, 0:256],
                                    ALU.mult, ALU.mult),
                                incs=[("v", 1)])
                        bank_guard[b2] = ("v", pr.mark("v"))
                    wp_guard[unit] = ("mm", pr.mark("mm"))
                    yield
                m_g = pr.mark("v")

                def h_blocks(blob, src_off, dst_off, sc, m_in, m_in_sem):
                    for blk in range(16):
                        ua = unit_rr[0] % 4
                        unit_rr[0] += 1
                        ub = unit_rr[0] % 4
                        unit_rr[0] += 1
                        mwa = wp_load(ua, blob, blk * P * 32 * 256, 16, 32)
                        mwb = wp_load(ub, blob,
                                      blk * P * 32 * 256 + 16 * 256, 16, 32)
                        for mt in range(2):
                            ht = blk * 2 + mt
                            bank = ht % 6
                            for k in range(NH):
                                unit = ua if k < 16 else ub
                                kk = k % 16
                                w = []
                                if k == 0:
                                    w = [("w%d" % ua, mwa),
                                         ("w%d" % ub, mwb),
                                         (m_in_sem, m_in),
                                         bank_guard[bank]]
                                pr.emit(
                                    "tensor",
                                    lambda e, k=k, kk=kk, mt=mt, b=bank,
                                    u=unit, so=src_off: e.matmul(
                                        ps[b][:, 0:256],
                                        bp(BB_WP + u * 4096 + kk * 256 +
                                           mt * P, P),
                                        bp(so + k * 256, 256),
                                        start=(k == 0), stop=(k == NH - 1)),
                                    waits=w, incs=[("mm", 1)])
                            m_h = pr.mark("mm")
                            pr.emit("scalar",
                                    lambda e, ht=ht, b=bank, do=dst_off,
                                    sc=sc: e.activation(
                                        bp(do + ht * 256, 256),
                                        ps[b][:, 0:256], AF.Copy,
                                        scale=c_sb[:, sc + ht:sc + ht + 1]),
                                    waits=[("mm", m_h)], incs=[("s", 1)])
                            bank_guard[bank] = ("s", pr.mark("s"))
                        wp_guard[ua] = ("mm", pr.mark("mm"))
                        wp_guard[ub] = ("mm", pr.mark("mm"))
                        yield
                    pr.emit("scalar",
                            lambda e, do=dst_off: e.activation(
                                bp(do, NH * 256), bp(do, NH * 256),
                                AF.Gelu),
                            waits=[("s", pr.mark("s"))], incs=[("s", 1)])

                yield from h_blocks(h1b, BB_G + 0, BB_P, C_SH1, m_g, "v")
                m_p1 = pr.mark("s")
                yield from h_blocks(h2b, BB_P, BB_G, C_SH2, m_p1, "s")
                m_p2 = pr.mark("s")

                for blk in range(4):
                    ua = unit_rr[0] % 4
                    unit_rr[0] += 1
                    ub = unit_rr[0] % 4
                    unit_rr[0] += 1
                    mwa = wp_load(ua, w3b, blk * P * 32 * 256, 16, 32)
                    mwb = wp_load(ub, w3b, blk * P * 32 * 256 + 16 * 256,
                                  16, 32)
                    for mt in range(2):
                        fi = blk * 2 + mt
                        bank = 4 + (fi % 2)
                        for k in range(NH):
                            unit = ua if k < 16 else ub
                            kk = k % 16
                            w = []
                            if k == 0:
                                w = [("w%d" % ua, mwa), ("w%d" % ub, mwb),
                                     ("s", m_p2),
                                     bank_guard[bank]]
                            pr.emit("tensor",
                                    lambda e, k=k, kk=kk, mt=mt, b=bank,
                                    u=unit: e.matmul(
                                        ps[b][:, 0:256],
                                        bp(BB_WP + u * 4096 + kk * 256 +
                                           mt * P, P),
                                        bp(BB_G + k * 256, 256),
                                        start=(k == 0), stop=(k == NH - 1)),
                                    waits=w, incs=[("mm", 1)])
                        m_w3 = pr.mark("mm")
                        pr.emit("scalar",
                                lambda e, fi=fi, b=bank: e.activation(
                                    fp(FB_FF + fi * 256, 256),
                                    ps[b][:, 0:256], AF.Copy,
                                    scale=c_sb[:, C_S3 + fi:C_S3 + fi + 1]),
                                waits=[("mm", m_w3), ff_guard[fi]],
                                incs=[("s", 1)])
                        bank_guard[bank] = ("s", pr.mark("s"))
                        m_ev = pr.mark("s")
                        pr.emit("vector",
                                lambda e, fi=fi, hoff=hoff: e.tensor_tensor(
                                    fp(FB_FF + fi * 256, 256),
                                    fp(FB_FF + fi * 256, 256),
                                    fp(FB_X1 + fi * 512 + hoff, 256),
                                    ALU.add),
                                waits=[("s", m_ev)], incs=[("v", 1)])
                        m_out = pr.mark("v")
                        pr.dma("sync",
                               lambda g=g, fi=fi, hoff=hoff:
                               y_out[fi * P:(fi + 1) * P,
                                     t0 + hoff:t0 + hoff + 256],
                               lambda fi=fi: fp(FB_FF + fi * 256, 256),
                               "o%d" % fi, waits=[("v", m_out)])
                        ff_guard[fi] = ("o%d" % fi, pr.mark("o%d" % fi))
                    wp_guard[ua] = ("mm", pr.mark("mm"))
                    wp_guard[ub] = ("mm", pr.mark("mm"))
                    yield

        for j in range(NJ):
            u_guard[j] = ("spu", m_spu)
            dt_guard[j] = ("spdt", m_spdt)
        for j in range(2):
            sz_guard[j] = ("spsz", m_spsz)
        for u in range(4):
            wp_guard[u] = ("mm", 0)
        pr.emit("sync", lambda e: None, waits=list(m_a3_join))
        emit_loads(0)
        for s in range(NCH + 1):
            ffn = None
            if s >= 1:
                emit_back_head(s - 1)
                ffn = gen_back_ffn(s - 1)
            if s < NCH:
                emit_scan_head(s)
                for q in range(8):
                    emit_scan_q(s, q)
                    if ffn is not None:
                        for _ in range(17):
                            if next(ffn, "done") == "done":
                                ffn = None
                                break
                emit_scan_tail(s)
                if s + 1 < NCH:
                    emit_loads(s + 1)
                if debug and s == 0:
                    pr.dma("sync", lambda: dbg_y[:, :],
                           lambda: fp(FB_Y, NJ * 512),
                           "dbgs", waits=[("v", m_ymix[0])])
                    pr.dma("sync", lambda: dbg_ym[:, :],
                           lambda: bp(BB_YM, NJ * 512), "dbgs",
                           waits=[("v", m_ymix[0])])
            if ffn is not None:
                for _ in iter(lambda: next(ffn, "done"), "done"):
                    pass
            if debug and s == 1:
                pr.dma("sync", lambda: dbg_x1[:, :],
                       lambda: fp(FB_X1, NF * 512), "dbgs",
                       waits=[("v", m_lnx1_dbg[0])])
                pr.dma("sync", lambda: dbg_ln1[:, :],
                       lambda: bp(BB_LN1, NF * 512), "dbgs",
                       waits=[("v", m_lnx1_dbg[0])])

        final_waits = [("o%d" % fi, pr.mark("o%d" % fi))
                       for fi in range(NF)]
        if debug:
            final_waits.append(("dbgs", pr.mark("dbgs")))
        pr.emit("sync", lambda e: e.nop(), waits=final_waits)

        semh = {}
        with ExitStack() as semctx:
            for name in pr.semnames:
                semh[name] = semctx.enter_context(nc.semaphore(name))
            with nc.Block() as block:
                def runner(stream):
                    def run(eng_obj):
                        for waits, fn, incs in stream:
                            for s, val in waits:
                                eng_obj.wait_ge(semh[s], val)
                            instr = fn(eng_obj)
                            for s, d in incs:
                                instr.then_inc(semh[s], d)
                    return run

                block.sync(runner(pr.streams["sync"]))
                block.vector(runner(pr.streams["vector"]))
                block.scalar(runner(pr.streams["scalar"]))
                block.tensor(runner(pr.streams["tensor"]))
                block.gpsimd(runner(pr.streams["gpsimd"]))

    return nc



def _pack_cvec(conv_w, conv_b, dtb, A_log, D, nw, s1, s2, sh1, sh2, s3):
    c = np.zeros((P, 652), dtype=np.float32)
    c[:, 0:64] = conv_w.reshape(16, 128, 4).transpose(1, 0, 2).reshape(128, 64)
    c[:, 64:80] = conv_b.reshape(16, 128).T
    c[:, 80:96] = dtb.reshape(16, 128).T
    c[:, 96:352] = A_log.reshape(16, 128, 16).transpose(1, 0, 2).reshape(
        128, 256)
    c[:, 352:368] = D.reshape(16, 128).T
    c[:, 368:376] = nw.reshape(8, 128).T
    c[:, 376:408] = s1.reshape(32, 128).T
    c[:, 408:440] = s2.reshape(32, 128).T
    c[:, 440:472] = sh1.reshape(32, 128).T
    c[:, 472:504] = sh2.reshape(32, 128).T
    c[:, 504:512] = s3.reshape(8, 128).T
    c[:, 512] = 1.0
    c[:, 513] = EPS
    c[:, 514] = 1.0 + EPS
    c[:, 515] = EPS * EPS
    c[:, 516] = -0.5
    c[:, 517] = -1.0
    c[:, 518] = 1.0 / DM
    c[0, 520:648] = 1.0
    return c


def _blob_win(w):
    return np.ascontiguousarray(
        w.reshape(NF, P, 32, P).transpose(2, 1, 0, 3)).reshape(-1)


def _blob(wt, nblk):
    K, M = wt.shape
    assert M == nblk * 256
    nkt = K // P
    return np.ascontiguousarray(
        wt.reshape(nkt, P, nblk, 256).transpose(2, 1, 0, 3)).reshape(-1)


_NC_CACHE = {}


def kernel(**inp):
    f32 = np.float32
    x = np.ascontiguousarray(np.asarray(inp["L0_attn_x"], dtype=f32))
    B = x.shape[0]
    asnp = lambda k: np.ascontiguousarray(np.asarray(inp[k], dtype=f32))

    shared = dict(
        winp=_blob_win(asnp("L0_attn_in_proj_w")),
        wxp=asnp("L0_attn_x_proj_w"),
        wdt=asnp("L0_attn_dt_proj_w"),
        wout=_blob(asnp("L0_attn_out_proj_w"), 4),
        w1t=_blob(asnp("L0_ffn_w1_w").T, 16),
        w2t=_blob(asnp("L0_ffn_w2_w").T, 16),
        h1t=_blob(asnp("L0_ffn_h1_w").T, 16),
        h2t=_blob(asnp("L0_ffn_h2_w").T, 16),
        w3t=_blob(asnp("L0_ffn_w3_w").T, 4),
        cvec=_pack_cvec(
            asnp("L0_attn_conv_w"), asnp("L0_attn_conv_b"),
            asnp("L0_attn_dt_proj_b"), asnp("L0_attn_A_log"),
            asnp("L0_attn_D"), asnp("L0_attn_norm_w"),
            asnp("L0_ffn_w1_s"), asnp("L0_ffn_w2_s"),
            asnp("L0_ffn_h1_s"), asnp("L0_ffn_h2_s"),
            asnp("L0_ffn_w3_s")),
    )

    if "nc" not in _NC_CACHE:
        _NC_CACHE["nc"] = build_program()
    nc = _NC_CACHE["nc"]

    in_maps = []
    for b in range(B):
        m = dict(shared)
        m["xT"] = np.ascontiguousarray(x[b].T)
        in_maps.append(m)

    res = run_bass_kernel_spmd(nc, in_maps, core_ids=list(range(B)))
    out = np.stack([np.asarray(res.results[b]["y_out"]).T for b in range(B)])
    return out.astype(np.float32)
